# version 11
# baseline (speedup 1.0000x reference)
import numpy as np
import ml_dtypes
import concourse.bass as bass
import concourse.mybir as mybir
from concourse.bass_utils import run_bass_kernel_spmd

F32 = mybir.dt.float32
BF16 = mybir.dt.bfloat16
AF = mybir.ActivationFunctionType
ALU = mybir.AluOpType
AX = mybir.AxisListType

bf = ml_dtypes.bfloat16
D = 1024
T = 4096
C = 8
NT = 8
W = 512
NEG = -1.0e30

GROUPS = [[0, 1, 2, 3]]


class Ctr:
    __slots__ = ("sem", "n", "name")

    def __init__(self, sem, name):
        self.sem, self.n, self.name = sem, 0, name


class Prog:
    ENGINES = ("tensor", "vector", "scalar", "gpsimd", "sync")

    def __init__(self, nc):
        self.nc = nc
        self.items = {e: [] for e in self.ENGINES}

    def ctr(self, name):
        return Ctr(self.nc.alloc_semaphore(name), name)

    def do(self, eng, fn, waits=(), incs=()):
        wl = []
        for w in waits:
            if w is None:
                continue
            if isinstance(w, Ctr):
                wl.append((w.sem, w.n))
            else:
                c, v = w
                assert v <= c.n, f"future wait {c.name}: {v} > {c.n}"
                if v > 0:
                    wl.append((c.sem, v))
        il = []
        for inc in incs:
            c, amt = (inc, 1) if isinstance(inc, Ctr) else inc
            c.n += amt
            il.append((c.sem, amt))
        self.items[eng].append((wl, fn, il))

    def dma(self, queue, out, in_, waits=(), ctr=None, **kw):
        incs = [(ctr, 16)] if ctr is not None else []
        self.do(queue, lambda e: e.dma_start(out=out, in_=in_, **kw),
                waits=waits, incs=incs)

    def emit(self):
        with self.nc.Block() as block:
            for name in self.ENGINES:
                items = self.items[name]
                if not items:
                    continue

                def body(eng, items=items):
                    for wl, fn, il in items:
                        for sem, val in wl:
                            eng.wait_ge(sem, val)
                        inst = fn(eng)
                        for sem, amt in il:
                            inst = inst.then_inc(sem, amt)

                getattr(block, name)(body)


class Rot:

    def __init__(self, aps):
        self.aps = list(aps)
        self.cons = [None] * len(self.aps)

    def get(self, i):
        n = len(self.aps)
        return self.aps[i % n], self.cons[i % n]

    def consumed(self, i, ctr):
        self.cons[i % len(self.aps)] = (ctr, ctr.n)


def host_consts():
    cst = {}
    cst["c_ident"] = np.eye(128, dtype=bf)
    cst["c_ones_col_bf"] = np.ones((128, 1), dtype=bf)
    cst["c_ones_row_f"] = np.ones((1, 128), dtype=np.float32)
    cst["c_ones_row_bf"] = np.ones((1, 128), dtype=bf)
    e2 = np.zeros((128, 2), dtype=bf)
    e2[:64, 0] = 1
    e2[64:, 1] = 1
    cst["c_e2"] = e2
    sel = np.zeros((16, D), dtype=bf)
    for c in range(C):
        sel[2 * c, c * 128:c * 128 + 64] = 1
        sel[2 * c + 1, c * 128 + 64:(c + 1) * 128] = 1
    cst["c_sel"] = sel
    half = 64
    freqs = np.power(10000.0, -np.arange(half, dtype=np.float64) / half)
    rad = freqs[:, None] * np.arange(T, dtype=np.float64)[None, :]
    cos, sin = np.cos(rad), np.sin(rad)
    cst["c_cc"] = np.concatenate([cos, cos], axis=0).astype(bf)
    cst["c_ss"] = np.concatenate([-sin, sin], axis=0).astype(bf)
    swp = np.zeros((128, 128), np.float32)
    for i in range(128):
        swp[i, (i + 64) % 128] = 1
    cst["c_swp"] = swp.astype(bf)
    cst["c_identf"] = np.eye(128, dtype=np.float32)
    cst["c_onesf_all"] = np.ones((128, 128), dtype=np.float32)
    jj, qq = np.arange(128)[:, None], np.arange(128)[None, :]
    tri01 = np.where(jj <= qq, 1.0, 0.0)
    ctri = []
    for i in range(4):
        t = np.zeros((128, 512))
        for b in range(4):
            if b < i:
                t[:, b * 128:(b + 1) * 128] = 0.0
            elif b == i:
                t[:, b * 128:(b + 1) * 128] = tri01
            else:
                t[:, b * 128:(b + 1) * 128] = 1.0
        ctri.append(t)
    cst["c_ctri"] = np.concatenate(ctri, axis=1).astype(np.float32).astype(bf)
    offs = [-128, 0, 128, 256, 384, 512]
    j, q = np.arange(128)[:, None], np.arange(512)[None, :]
    bl = [(np.abs(off + j - q) <= 32).astype(bf) for off in offs]
    g = (np.abs(j - q) <= 32)
    g[:4, :] = True
    g[:, :4] = False
    bl.append(g.astype(bf))
    cst["c_band"] = np.ascontiguousarray(np.concatenate(bl, axis=1))
    cst["c_i4"] = np.eye(4, dtype=bf)
    return cst


_CONSTS = host_consts()
ARENA = {"c_band": (128, 3584), "c_ctri": (128, 2048), "c_sel": (16, 1024)}
CONST_NEED = {
    None: ["c_ones_col_bf", "c_ones_row_f", "c_ones_row_bf"],
    0: ["c_sel"],
    1: ["c_band", "c_i4", "c_e2", "c_onesf_all"],
    2: ["c_ident", "c_e2", "c_onesf_all"],
    3: ["c_cc", "c_ss", "c_swp", "c_ctri", "c_ident", "c_identf"],
}


def consts_for(layers):
    ks = list(CONST_NEED[None])
    for l in layers:
        ks += CONST_NEED[l]
    return {k: _CONSTS[k] for k in ks}


class Emitter:
    def __init__(self, layers):
        self.layers = layers
        nc = self.nc = bass.Bass("TRN2", target_bir_lowering=False)
        self.P = Prog(nc)
        EI, EO, IN = "ExternalInput", "ExternalOutput", "Internal"
        dram = lambda n, s, dt=F32, kind=IN: nc.dram_tensor(n, s, dt, kind=kind).ap()
        self.x_in = dram("x_in", [D, T], F32, EI)
        self.x_out = dram("x_out", [D, T], F32, EO)
        self.xbuf = [dram("xbufA", [D, T]), dram("xbufB", [D, T])]
        self.wd = {}
        for l in layers:
            for n, s in WSHAPES[l].items():
                self.wd[n] = dram(n, list(s), F32, EI)
        self.vecs = dram("vecs", [128, 128], F32, EI)
        self.consts = consts_for(layers)
        self.cd = {}
        for n, a in self.consts.items():
            self.cd[n] = dram(n, list(a.shape), BF16 if a.dtype == bf else F32, EI)
        self.xn_h = dram("xn_h", [NT, C, 128, W], BF16)
        self.sp1 = dram("sp1_h", [NT, C, 128, W], BF16)
        self.sp2 = dram("sp2_h", [NT, C, 128, W], BF16)
        self.sp3 = dram("sp3_h", [NT, C, 128, W], BF16)
        self.sp4 = dram("sp4_h", [NT, C, 128, W], BF16)
        self.q_hb = dram("q_hb", [D, T], BF16)
        self.k_hb = dram("k_hb", [D, T], BF16)
        self.kz_hb = dram("kz_hb", [2, D, T], BF16)
        self.v_hb = dram("v_hb", [16, 128, 32, 64], BF16)
        self.v_hb3 = dram("v_hb3", [8, 128, 32, 128], BF16)
        self.o_hb = dram("o_hb", [D, T], BF16)
        self.m_hb = dram("m_hb", [128, 128], F32)
        self.mT_hb = dram("mT_hb", [32, 512], F32)
        a = nc.alloc_sbuf_tensor
        self.F = [a(f"F{i}", [128, T], F32) for i in range(4)]
        nb_tiles = 6
        self.B = [a(f"B{i}", [128, T], BF16) for i in range(nb_tiles)]
        self.Wt = [a(f"W{i}", [128, 8192], BF16) for i in range(4)]
        self.VE = a("VE", [128, 128], F32)
        self.VD = a("VD", [128, 64], F32)
        self.PQ = a("PQ", [128, 128], F32) if 0 in layers else None
        self.XM = a("XM", [128, 256], F32) if 2 in layers else None
        self.SM = a("SM", [128, 512], BF16)
        self.ST = a("ST", [65, T], F32)
        self.CN = {}
        arena_need = [n for n in self.consts if n in ARENA]
        if arena_need:
            acols = max(ARENA[n][1] for n in arena_need)
            self.CA = a("const_arena", [128, acols], BF16)
            for n in arena_need:
                p, c0 = ARENA[n]
                self.CN[n] = self.CA[0:p, 0:c0]
        self.CCS = a("cs_ccss", [128, 2 * W], BF16) if "c_cc" in self.consts else None
        for n, arr in self.consts.items():
            if n in ARENA or n in ("c_cc", "c_ss"):
                continue
            self.CN[n] = a("s_" + n, list(arr.shape), BF16 if arr.dtype == bf else F32)
        self.PS = [nc.alloc_psum_tensor(f"ps{i}", [128, 512], F32) for i in range(6)]
        self.PR = [nc.alloc_psum_tensor(f"pr{i}", [16, 512], F32) for i in range(2)]
        self.psmark = {}
        P = self.P
        self.cW = P.ctr("cW")
        self.cL = P.ctr("cL")
        self.cSP = P.ctr("cSP")
        self.cPE = P.ctr("cPE")
        self.cV = P.ctr("cV")
        self.cS = P.ctr("cS")
        self.cG = P.ctr("cG")
        self.cXW = P.ctr("cXW")
        P.dma("sync", self.VE[:], self.vecs, ctr=self.cL)
        for n in self.CN:
            if n in ARENA:
                continue
            P.dma("sync", self.CN[n][:], self.cd[n], ctr=self.cL)
        self.consts_ld = (self.cL, self.cL.n)

    def load_layer_consts(self, l):
        P = self.P
        for n in CONST_NEED.get(l, []):
            if n in ARENA and n in self.CN:
                P.dma("sync", self.CN[n], self.cd[n], ctr=self.cL)
        self.consts_ld = (self.cL, self.cL.n)

    def wv8(self, wt):
        return wt[:].rearrange("p (c o) -> p c o", o=1024)

    def wvN(self, wt, o, ncols=None):
        v = wt[:] if ncols is None else wt[:, 0:ncols]
        return v.rearrange("p (c o) -> p c o", o=o)

    def wload(self, dst, name, waits=()):
        src = self.wd[name].rearrange("(c p) o -> p c o", p=128)
        self.P.dma("gpsimd", dst, src, waits=waits, ctr=self.cW)
        return (self.cW, self.cW.n)

    def ps_free(self, i):
        return self.psmark.get(i)

    def ps_read(self, i, ctr):
        self.psmark[i] = (ctr, ctr.n)

    def mm_proj(self, ps, wtile, oc, nsrc, rhs_w=W, kset=range(C), extra_waits=()):
        P = self.P
        wv = self.wv8(wtile)
        kl = list(kset)
        for ki, c in enumerate(kl):
            waits = []
            if ki == 0:
                waits = list(extra_waits) + [self.ps_free(ps)]
            P.do("tensor",
                 lambda e, c=c, first=(ki == 0), last=(ki == len(kl) - 1):
                 e.matmul(self.PS[ps][:, 0:rhs_w],
                          wv[:, c, oc * 128:(oc + 1) * 128],
                          nsrc(c), start=first, stop=last),
                 waits=waits,
                 incs=[self.cPE] if ki == len(kl) - 1 else [])
        return (self.cPE, self.cPE.n)

    def emit_ln(self, x_src, mixes=False, want_xm=False):
        P, nc = self.P, self.nc
        cL, cPE, cV, cS, cG = self.cL, self.cPE, self.cV, self.cS, self.cG
        F, B, PS, PR, ST = self.F, self.B, self.PS, self.PR, self.ST
        onesb = self.CN["c_ones_col_bf"]
        onesf = self.CN["c_ones_row_f"]
        xv = x_src.rearrange("(c p) t -> p c t", p=128)
        rX = Rot([F[2], F[3]])
        sq_slots = Rot([B[3][:, i * W:(i + 1) * W] for i in range(4)])
        xb_slots = Rot([B[2][:, i * W:(i + 1) * W] for i in range(4)])
        for tc in range(NT):
            xt, war = rX.get(tc)
            P.dma("sync", xt[:].rearrange("p (c w) -> p c w", w=W),
                  xv[:, :, tc * W:(tc + 1) * W],
                  waits=[war, self.consts_ld], ctr=cL)
            ld = (cL, cL.n)
            for c in range(C):
                src = xt[:, c * W:(c + 1) * W]
                xb, war1 = xb_slots.get(tc * C + c)
                sq, war2 = sq_slots.get(tc * C + c)
                P.do("vector", lambda e, s=src, d=xb: e.tensor_copy(d, s),
                     waits=[ld, war1], incs=[cV])
                P.do("scalar", lambda e, s=src, d=sq: e.activation(d, s, AF.Square),
                     waits=[ld, war2], incs=[cS])
                vv, ss = (cV, cV.n), (cS, cS.n)
                P.do("tensor",
                     lambda e, r=xb, f=(c == 0), l=(c == C - 1):
                     e.matmul(PR[0][0:1, :], onesb[:], r, start=f, stop=l),
                     waits=[vv] + ([self.ps_free(10)] if c == 0 else []),
                     incs=[cPE])
                xb_slots.consumed(tc * C + c, cPE)
                P.do("tensor",
                     lambda e, r=sq, f=(c == 0), l=(c == C - 1):
                     e.matmul(PR[1][0:1, :], onesb[:], r, start=f, stop=l),
                     waits=[ss] + ([self.ps_free(11)] if c == 0 else []),
                     incs=[cPE])
                sq_slots.consumed(tc * C + c, cPE)
            pe = (cPE, cPE.n)
            rX.consumed(tc, cPE)
            P.do("scalar",
                 lambda e, t=tc: e.activation(ST[0:1, t * W:(t + 1) * W], PR[0][0:1, :],
                                              AF.Copy),
                 waits=[pe], incs=[cS])
            self.ps_read(10, cS)
            P.do("scalar",
                 lambda e, t=tc: e.activation(ST[32:33, t * W:(t + 1) * W], PR[1][0:1, :],
                                              AF.Copy),
                 waits=[pe], incs=[cS])
            self.ps_read(11, cS)
        sA = (cS, cS.n)
        fa, fb = F[2][0:1, :], F[3][0:1, :]
        P.do("vector", lambda e, fa=fa: e.tensor_scalar_mul(fa, ST[0:1, :], 1.0 / D),
             waits=[sA], incs=[cV])
        mrow = (cV, cV.n)
        for tc in range(NT):
            P.do("tensor",
                 lambda e, t=tc, fa=fa: e.matmul(PS[0][:], onesf[:],
                                                 fa[:, t * W:(t + 1) * W],
                                                 start=True, stop=True),
                 waits=[mrow, self.ps_free(0)], incs=[cPE])
            pe = (cPE, cPE.n)
            P.do("scalar",
                 lambda e, t=tc: e.activation(F[0][:, t * W:(t + 1) * W], PS[0][:],
                                              AF.Copy), waits=[pe], incs=[cS])
            self.ps_read(0, cS)
        bcm = (cPE, cPE.n)
        P.do("scalar", lambda e, fa=fa, fb=fb: e.activation(fb, fa, AF.Square),
             waits=[mrow], incs=[cS])
        P.do("vector", lambda e, fa=fa: e.tensor_scalar_mul(fa, ST[32:33, :], 1.0 / D),
             waits=[(cS, cS.n), bcm], incs=[cV])
        P.do("vector", lambda e, fa=fa, fb=fb: e.tensor_sub(ST[0:1, :], fa, fb),
             incs=[cV])
        P.do("scalar", lambda e, fa=fa: e.activation(fa, ST[0:1, :], AF.Ln,
                                                     bias=self.VE[0:1, 62:63]),
             waits=[(cV, cV.n)], incs=[cS])
        P.do("scalar", lambda e, fa=fa: e.activation(ST[0:1, :], fa, AF.Exp,
                                                     scale=self.VE[0:1, 61:62]),
             incs=[cS])
        sST = (cS, cS.n)
        for tc in range(NT):
            P.do("tensor",
                 lambda e, t=tc: e.matmul(PS[1][:], onesf[:],
                                          ST[0:1, t * W:(t + 1) * W],
                                          start=True, stop=True),
                 waits=[sST, self.ps_free(1)], incs=[cPE])
            pe = (cPE, cPE.n)
            P.do("scalar",
                 lambda e, t=tc: e.activation(F[1][:, t * W:(t + 1) * W], PS[1][:],
                                              AF.Copy), waits=[pe], incs=[cS])
            self.ps_read(1, cS)
        sBC = (cS, cS.n)
        rX.cons = [(self.cS, self.cS.n), (self.cS, self.cS.n)]
        murs = self.VE[:, 24:32]
        muks = self.VE[:, 32:40]
        mur1 = self.VD[:, 8:16]
        muk1 = self.VD[:, 16:24]
        rXc = rX
        rXn = Rot([B[0], B[1]])
        for c in range(C):
            xc, war = rXc.get(c)
            P.dma("sync", xc[:], xv[:, c, :], waits=[war, sBC], ctr=cL)
            ld = (cL, cL.n)
            P.do("vector", lambda e, x=xc: e.tensor_sub(x[:], x[:], F[0][:]),
                 waits=[ld], incs=[cV])
            xnc, war2 = rXn.get(c)
            if want_xm:
                fo = rXc.aps[(c + 1) % 2]
                P.do("vector", lambda e, x=xc, f=fo: e.tensor_mul(f[:], x[:], F[1][:]),
                     waits=[(cV, cV.n), rXc.cons[(c + 1) % 2]], incs=[cV])
                P.do("vector",
                     lambda e, f=fo, c=c: e.tensor_reduce(
                         self.XM[:, c * 32:(c + 1) * 32],
                         f[:].rearrange("p (b w) -> p b w", w=128), AX.X, ALU.add),
                     incs=[cV])
                P.do("vector", lambda e, f=fo, d=xnc: e.tensor_copy(d[:], f[:]),
                     waits=[war2], incs=[cV])
                rXc.consumed(c + 1, cV)
            else:
                P.do("vector", lambda e, x=xc, d=xnc: e.tensor_mul(d[:], x[:], F[1][:]),
                     waits=[(cV, cV.n), war2], incs=[cV])
                rXc.consumed(c, cV)
            vxn = (cV, cV.n)
            P.dma("gpsimd", self.xn_h[:, c].rearrange("t p w -> p t w"),
                  xnc[:].rearrange("p (t w) -> p t w", w=W),
                  waits=[vxn], ctr=self.cSP)
            rXn.consumed(c, self.cSP)
            if mixes:
                for (mu, mu1, spdst, slot) in (
                        (murs[:, c:c + 1], mur1[:, c:c + 1], self.sp1, B[4]),
                        (muks[:, c:c + 1], muk1[:, c:c + 1], self.sp2, B[5])):
                    P.do("vector",
                         lambda e, x=xnc, m=mu, d=B[2]:
                         e.tensor_scalar_mul(d[:], x[:], m),
                         waits=[vxn, (self.cSP, self.cSP.n)], incs=[cV])
                    P.do("vector",
                         lambda e, x=xnc, m=mu1, d=slot:
                         e.scalar_tensor_tensor(d[:, 1:T], x[:, 0:T - 1], m,
                                                B[2][:, 1:T], ALU.mult, ALU.add),
                         waits=[(cV, cV.n)], incs=[cV])
                    P.do("vector",
                         lambda e, d=slot: e.tensor_copy(d[:, 0:1], B[2][:, 0:1]),
                         incs=[cV])
                    P.dma("gpsimd", spdst[:, c].rearrange("t p w -> p t w"),
                          slot[:].rearrange("p (t w) -> p t w", w=W),
                          waits=[(cV, cV.n)], ctr=self.cSP)
        self.lnmark = (self.cSP, self.cSP.n)

    def emit_l0(self, x_src, x_dst):
        P = self.P
        cL, cPE, cV, cS, cSP = self.cL, self.cPE, self.cV, self.cS, self.cSP
        F, B, PS, PR = self.F, self.B, self.PS, self.PR
        ST, VE, VD, PQ, SM = self.ST, self.VE, self.VD, self.PQ, self.SM
        ST2 = F[0]
        SMT = F[1]
        self._b2war = None
        self._f0war = 0
        self._ffwar = 0
        wq_ld = self.wload(self.wv8(self.Wt[0]), "L0Wq", waits=())
        smv = SM[:].rearrange("p (k o) -> p k o", o=16)
        P.dma("gpsimd", SM[:, 0:128].rearrange("p (c o) -> p c o", o=16),
              self.wd["L0Wqa"].rearrange("(c p) o -> p c o", p=128), ctr=self.cW)
        P.dma("gpsimd", SM[:, 128:256].rearrange("p (c o) -> p c o", o=16),
              self.wd["L0Wka"].rearrange("(c p) o -> p c o", p=128), ctr=self.cW)
        w_sm = (self.cW, self.cW.n)
        w1_ld = self.wload(self.wv8(self.Wt[1]), "L0WrT")
        w2_ld = self.wload(self.wv8(self.Wt[2]), "L0WkT")
        w3_ld = self.wload(self.wv8(self.Wt[3]), "L0WvT")
        P.do("vector", lambda e: e.tensor_scalar_mul(VD[0:16, 0:1], VE[0:16, 40:41],
                                                     0.125),
             waits=[self.consts_ld], incs=[cV])
        P.do("vector", lambda e: e.tensor_scalar_mul(VD[0:16, 1:2], VE[0:16, 41:42],
                                                     0.125), incs=[cV])
        P.do("vector", lambda e: e.tensor_scalar(VD[:, 8:16], VE[:, 24:32],
                                                 -1.0, 1.0, ALU.mult, ALU.add),
             incs=[cV])
        P.do("vector", lambda e: e.tensor_scalar(VD[:, 16:24], VE[:, 32:40],
                                                 -1.0, 1.0, ALU.mult, ALU.add),
             incs=[cV])
        self.emit_ln(x_src, mixes=True)
        ln = self.lnmark

        nst = lambda buf: (lambda c: buf[:, c * W:(c + 1) * W])
        rN = Rot([B[0], B[1]])

        for tc in range(NT):
            nb, war = rN.get(tc)
            P.dma("sync", nb[:].rearrange("p (c w) -> p c w", w=W),
                  self.xn_h[tc].rearrange("c p w -> p c w"), waits=[war, ln], ctr=cL)
            ld = (cL, cL.n)
            for oc in range(C):
                ps = oc % 2
                pe = self.mm_proj(ps, self.Wt[0], oc, nst(nb),
                                  extra_waits=[ld, wq_ld])
                P.do("scalar",
                     lambda e, p=ps, o=oc: e.activation(
                         B[2][:, o * W:(o + 1) * W], PS[p][:], AF.Identity,
                         bias=VE[:, o:o + 1]),
                     waits=[pe, (cV, cV.n) if (tc, oc) == (0, 0) else None,
                            self._b2war if oc == 0 else None],
                     incs=[cS])
                self.ps_read(ps, cS)
            sEv = (cS, cS.n)
            for oc in range(C):
                P.do("tensor",
                     lambda e, o=oc, f=(oc == 0), l=(oc == C - 1):
                     e.matmul(PR[0][0:16, 0:W], smv[:, o, :],
                              B[2][:, o * W:(o + 1) * W], start=f, stop=l),
                     waits=[sEv, w_sm, self.ps_free(10)] if oc == 0 else [],
                     incs=[cPE] if oc == C - 1 else [])
            pe = (cPE, cPE.n)
            P.do("scalar",
                 lambda e, t=tc: e.activation(ST2[0:16, t * W:(t + 1) * W],
                                              PR[0][0:16, 0:W], AF.Identity,
                                              scale=VE[0:16, 60:61],
                                              bias=VD[0:16, 0:1]),
                 waits=[pe], incs=[cS])
            self.ps_read(10, cS)
            P.dma("gpsimd", self.sp3[tc].rearrange("c p w -> p c w"),
                  B[2][:].rearrange("p (c w) -> p c w", w=W),
                  waits=[(cS, cS.n)], ctr=cSP)
            self._b2war = (cSP, cSP.n)
            rN.consumed(tc, cPE)
        self._softmax16(ST2, SMT, B[3], vd_col=2)
        pq = self._pool16(B[3], self.sp3, rN, PQ, 0)
        wk_ld = self.wload(self.wv8(self.Wt[0]), "L0Wk", waits=[(cPE, cPE.n)])
        self._b2war = (cSP, cSP.n)
        for tc in range(NT):
            nb, war = rN.get(NT + tc)
            P.dma("sync", nb[:].rearrange("p (c w) -> p c w", w=W),
                  self.xn_h[tc].rearrange("c p w -> p c w"), waits=[war], ctr=cL)
            ld = (cL, cL.n)
            for oc in range(C):
                ps = oc % 2
                pe = self.mm_proj(ps, self.Wt[0], oc, nst(nb),
                                  extra_waits=[ld, wk_ld, pq])
                P.do("vector",
                     lambda e, p=ps, o=oc: e.tensor_scalar(
                         B[2][:, o * W:(o + 1) * W], PS[p][:],
                         VE[:, 8 + o:9 + o], PQ[:, o:o + 1], ALU.add, ALU.mult),
                     waits=[pe, self._b2war if oc == 0 else None], incs=[cV])
                self.ps_read(ps, cV)
            vEv = (cV, cV.n)
            for oc in range(C):
                P.do("tensor",
                     lambda e, o=oc, f=(oc == 0), l=(oc == C - 1):
                     e.matmul(PR[0][0:16, 0:W], smv[:, 8 + o, :],
                              B[2][:, o * W:(o + 1) * W], start=f, stop=l),
                     waits=[vEv, self.ps_free(10)] if oc == 0 else [],
                     incs=[cPE] if oc == C - 1 else [])
            pe = (cPE, cPE.n)
            P.do("scalar",
                 lambda e, t=tc: e.activation(ST2[0:16, t * W:(t + 1) * W],
                                              PR[0][0:16, 0:W], AF.Identity,
                                              scale=VE[0:16, 60:61],
                                              bias=VD[0:16, 1:2]),
                 waits=[pe], incs=[cS])
            self.ps_read(10, cS)
            P.dma("gpsimd", self.sp4[tc].rearrange("c p w -> p c w"),
                  B[2][:].rearrange("p (c w) -> p c w", w=W),
                  waits=[(cS, cS.n)], ctr=cSP)
            self._b2war = (cSP, cSP.n)
            rN.consumed(NT + tc, cPE)
        self._softmax16(ST2, SMT, B[3], vd_col=4)
        pk = self._pool16(B[3], self.sp4, rN, PQ, 8)
        wt_ld = self.wload(self.wv8(self.Wt[0]), "L0Wt", waits=[(cPE, cPE.n)])
        xsv = x_src.rearrange("(c p) t -> p c t", p=128)
        xdv = x_dst.rearrange("(c p) t -> p c t", p=128)
        rX = Rot([F[2], F[3]])
        rX.cons = [(cV, cV.n), (cV, cV.n)]
        wvst = lambda c: B[2][:, c * W:(c + 1) * W]
        for tc in range(NT):
            nb, war = rN.get(tc)
            P.dma("sync", nb[:].rearrange("p (c w) -> p c w", w=W),
                  self.sp3[tc].rearrange("c p w -> p c w"), waits=[war], ctr=cL)
            xt, warx = rX.get(tc)
            P.dma("sync", xt[:].rearrange("p (c w) -> p c w", w=W),
                  xsv[:, :, tc * W:(tc + 1) * W], waits=[warx], ctr=cL)
            ld = (cL, cL.n)
            for c in range(C):
                P.do("vector",
                     lambda e, c=c, n=nb: e.tensor_scalar_mul(
                         wvst(c), n[:, c * W:(c + 1) * W], PQ[:, 8 + c:9 + c]),
                     waits=[ld, pk] + ([self._b2war] if c == 0 else []),
                     incs=[cV])
            vwv = (cV, cV.n)
            self._b2war = None
            for oc in range(C):
                ps = oc % 2
                pe = self.mm_proj(ps, self.Wt[0], oc, wvst, extra_waits=[vwv, wt_ld])
                P.do("vector",
                     lambda e, p=ps, o=oc, n=nb: e.scalar_tensor_tensor(
                         F[0][:, o * W:(o + 1) * W], PS[p][:], VE[:, 16 + o:17 + o],
                         n[:, o * W:(o + 1) * W], ALU.add, ALU.add),
                     waits=[pe, (self.cXW, self._f0war) if tc >= 1 else None],
                     incs=[cV])
                P.do("vector",
                     lambda e, o=oc, x=xt: e.tensor_add(
                         F[1][:, o * W:(o + 1) * W], F[0][:, o * W:(o + 1) * W],
                         x[:, o * W:(o + 1) * W]),
                     incs=[cV])
                self.ps_read(ps, cV)
            rN.consumed(tc, cV)
            rX.consumed(tc, cV)
            P.dma("sync", xdv[:, :, tc * W:(tc + 1) * W],
                  F[1][:].rearrange("p (c w) -> p c w", w=W),
                  waits=[(cV, cV.n)], ctr=self.cXW)
            self._f0war = self.cXW.n
        b6 = (self.cXW, self.cXW.n)
        rA = Rot([B[0], B[1]])
        rB = Rot([B[4], B[5]])
        self._rwar = None
        self._f1war = None
        self._kawar = None
        for tc in range(NT):
            ka = [B[2], B[3]][tc % 2]
            na, wara = rA.get(tc)
            P.dma("sync", na[:].rearrange("p (c w) -> p c w", w=W),
                  self.sp1[tc].rearrange("c p w -> p c w"), waits=[wara], ctr=cL)
            lda = (cL, cL.n)
            nb, warb = rB.get(tc)
            P.dma("sync", nb[:].rearrange("p (c w) -> p c w", w=W),
                  self.sp2[tc].rearrange("c p w -> p c w"), waits=[warb], ctr=cL)
            ldb = (cL, cL.n)
            for oc in range(C):
                ps = oc % 2
                pe = self.mm_proj(ps, self.Wt[1], oc, nst(na),
                                  extra_waits=[lda, w1_ld])
                P.do("scalar",
                     lambda e, p=ps, o=oc: e.activation(
                         F[0][:, o * W:(o + 1) * W], PS[p][:], AF.Sigmoid),
                     waits=[pe, self._rwar if oc == 0 else None], incs=[cS])
                self.ps_read(ps, cS)
            for oc in range(C):
                ps = 2 + oc % 2
                pe = self.mm_proj(ps, self.Wt[2], oc, nst(nb),
                                  extra_waits=[ldb, w2_ld])
                P.do("vector",
                     lambda e, p=ps, o=oc: e.tensor_scalar_max(
                         F[1][:, o * W:(o + 1) * W], PS[p][:], 0.0),
                     waits=[pe, self._f1war if oc == 0 else None], incs=[cV])
                P.do("scalar",
                     lambda e, o=oc, k=ka: e.activation(
                         k[:, o * W:(o + 1) * W], F[1][:, o * W:(o + 1) * W],
                         AF.Square),
                     waits=[(cV, cV.n), self._kawar if oc == 0 else None],
                     incs=[cS])
                self.ps_read(ps, cS)
            kEv = (cS, cS.n)
            self._f1war = (cS, cS.n)
            rA.consumed(tc, cS)
            rB.consumed(tc, cS)
            for oc in range(C):
                ps = 4 + oc % 2
                pe = self.mm_proj(ps, self.Wt[3], oc,
                                  lambda c, k=ka: k[:, c * W:(c + 1) * W],
                                  extra_waits=[kEv, w3_ld])
                P.do("vector",
                     lambda e, p=ps, o=oc: e.tensor_mul(
                         F[2][:, o * W:(o + 1) * W], PS[p][:],
                         F[0][:, o * W:(o + 1) * W]),
                     waits=[pe, (self.cSP, self._ffwar) if tc >= 1 else None],
                     incs=[cV])
                self.ps_read(ps, cV)
            self._rwar = (cV, cV.n)
            self._kawar = (cPE, cPE.n)
            P.dma("gpsimd", xdv[:, :, tc * W:(tc + 1) * W],
                  F[2][:].rearrange("p (c w) -> p c w", w=W),
                  waits=[(cV, cV.n), b6], ctr=cSP, accum_op=ALU.add)
            self._ffwar = cSP.n

    def emit_qkv64(self, wqt, wkt, wvt, wq, wk, wv_, bq0, bk0, ln, war0=None,
                   kz=False):
        P = self.P
        cL, cPE, cV, cS, cSP = self.cL, self.cPE, self.cV, self.cS, self.cSP
        B, PS = self.B, self.PS
        VE = self.VE
        qv = self.q_hb.rearrange("(c p) t -> p c t", p=128)
        kv = self.k_hb.rearrange("(c p) t -> p c t", p=128)
        nst = lambda buf: (lambda c: buf[:, c * W:(c + 1) * W])
        rN = Rot([B[0], B[1]])
        self._qwar = None
        self._kwar = None
        self._vwar = war0
        nst = lambda buf: (lambda c: buf[:, c * W:(c + 1) * W])
        for tc in range(NT):
            nb, war = rN.get(tc)
            P.dma("sync", nb[:].rearrange("p (c w) -> p c w", w=W),
                  self.xn_h[tc].rearrange("c p w -> p c w"),
                  waits=[war, ln], ctr=cL)
            ld = (cL, cL.n)
            for hp in range(C):
                ps = hp % 2
                pe = self.mm_proj(ps, wqt, hp, nst(nb),
                                  extra_waits=[ld, wq])
                P.do("scalar",
                     (lambda e, p=ps, o=hp: e.activation(
                         B[2][:, o * W:(o + 1) * W], PS[p][:], AF.Identity,
                         bias=VE[:, bq0 + o:bq0 + o + 1]))
                     if bq0 is not None else
                     (lambda e, p=ps, o=hp: e.activation(
                         B[2][:, o * W:(o + 1) * W], PS[p][:], AF.Copy)),
                     waits=[pe, self._qwar if hp == 0 else None], incs=[cS])
                self.ps_read(ps, cS)
            P.dma("gpsimd", qv[:, :, tc * W:(tc + 1) * W],
                  B[2][:].rearrange("p (c w) -> p c w", w=W),
                  waits=[(cS, cS.n)], ctr=cSP)
            self._qwar = (cSP, cSP.n)
            for hp in range(C):
                ps = hp % 2
                pe = self.mm_proj(ps, wkt, hp, nst(nb),
                                  extra_waits=[wk])
                P.do("scalar",
                     (lambda e, p=ps, o=hp: e.activation(
                         B[3][:, o * W:(o + 1) * W], PS[p][:], AF.Identity,
                         bias=VE[:, bk0 + o:bk0 + o + 1]))
                     if bk0 is not None else
                     (lambda e, p=ps, o=hp: e.activation(
                         B[3][:, o * W:(o + 1) * W], PS[p][:], AF.Copy)),
                     waits=[pe, self._kwar if hp == 0 else None], incs=[cS])
                self.ps_read(ps, cS)
            P.dma("gpsimd", kv[:, :, tc * W:(tc + 1) * W],
                  B[3][:].rearrange("p (c w) -> p c w", w=W),
                  waits=[(cS, cS.n)], ctr=cSP)
            if kz:
                kz0 = self.kz_hb[0].rearrange("(c p) t -> p c t", p=128)
                kz1 = self.kz_hb[1].rearrange("(c p) t -> p c t", p=128)
                P.dma("gpsimd", kz0[0:64, :, tc * W:(tc + 1) * W],
                      B[3][0:64, :].rearrange("p (c w) -> p c w", w=W),
                      ctr=cSP)
                P.dma("gpsimd", kz1[64:128, :, tc * W:(tc + 1) * W],
                      B[3][64:128, :].rearrange("p (c w) -> p c w", w=W),
                      ctr=cSP)
            self._kwar = (cSP, cSP.n)
            w2v = self.wv8(wvt)
            for hp in range(C):
                for tt in range(4):
                    ps = 2 + (hp * 4 + tt) % 2
                    for c in range(C):
                        P.do("tensor",
                             lambda e, c=c, o=hp, t=tt, p=ps, nb=nb,
                             f=(c == 0), l=(c == C - 1):
                             e.matmul(PS[p][:, 0:128],
                                      nb[:, c * W + t * 128:c * W + (t + 1) * 128],
                                      w2v[:, c, o * 128:(o + 1) * 128],
                                      start=f, stop=l),
                             waits=([wv_, self.ps_free(ps)] +
                                    ([self._vwar] if (hp, tt) == (0, 0) else []))
                             if c == 0 else [],
                             incs=[cPE] if c == C - 1 else [])
                    pe = (cPE, cPE.n)
                    P.do("scalar",
                         lambda e, p=ps, o=hp, t=tt: e.activation(
                             B[4][:, o * W + t * 128:o * W + (t + 1) * 128],
                             PS[p][:, 0:128], AF.Copy),
                         waits=[pe], incs=[cS])
                    self.ps_read(ps, cS)
            vv = (cS, cS.n)
            for h in range(16):
                hp, half = h // 2, h % 2
                src_v = B[4][:, hp * W:(hp + 1) * W].rearrange(
                    "p (tt half d) -> p tt half d", half=2, d=64)[:, :, half, :]
                P.dma("gpsimd", self.v_hb[h][:, tc * 4:(tc + 1) * 4, :], src_v,
                      waits=[vv], ctr=cSP)
            self._vwar = (cSP, cSP.n)
            rN.consumed(tc, cPE)

    def emit_l1(self, x_src, x_dst):
        P = self.P
        cL, cPE, cV, cS, cSP = self.cL, self.cPE, self.cV, self.cS, self.cSP
        F, B, PS, PR = self.F, self.B, self.PS, self.PR
        VE, VD = self.VE, self.VD
        onesb = self.CN["c_ones_col_bf"]
        onesf = self.CN["c_ones_row_f"]
        band = self.CN["c_band"]
        self._pjwar = {}
        self._pmwar = {}
        wq = self.wload(self.wv8(self.Wt[0]), "L1WqT")
        wk = self.wload(self.wv8(self.Wt[1]), "L1WkT")
        wv_ = self.wload(self.wv8(self.Wt[2]), "L1WvT")
        wo = self.wload(self.wv8(self.Wt[3]), "L1WoT")
        self.emit_ln(x_src)
        ln = self.lnmark
        P.do("vector", lambda e: e.tensor_copy(
            self.SM[:, 256:264].bitcast(BF16) if False else self.SM[:, 256:264],
            VE[:, 80:88]), waits=[self.consts_ld], incs=[cV])
        bvb = (cV, cV.n)
        w3v = self.wv8(self.Wt[3])
        for oc in range(C):
            for c in range(C):
                P.do("tensor",
                     lambda e, c=c, o=oc, f=(c == 0), l=(c == C - 1):
                     e.matmul(PR[1][0:1, 0:128].rearrange("o p -> p o") if False
                              else self.PS[5][:, 0:1],
                              w3v[:, c, o * 128:(o + 1) * 128],
                              self.SM[:, 256 + c:257 + c], start=f, stop=l),
                     waits=[bvb, wo, self.ps_free(5)] if c == 0 else [],
                     incs=[cPE] if c == C - 1 else [])
            pe = (cPE, cPE.n)
            P.do("vector",
                 lambda e, o=oc: e.tensor_scalar(VD[:, 32 + o:33 + o],
                                                 self.PS[5][:, 0:1],
                                                 VE[:, 88 + o:89 + o], None,
                                                 ALU.add),
                 waits=[pe], incs=[cV])
            self.ps_read(5, cV)
        nst = lambda buf: (lambda c: buf[:, c * W:(c + 1) * W])
        self.emit_qkv64(self.Wt[0], self.Wt[1], self.Wt[2], wq, wk, wv_,
                        64, 72, ln)
        pmark = (cPE, cPE.n)
        u1v = self.Wt[0][:, 0:2048].rearrange("p (c o) -> p c o", o=256)
        v2v = self.Wt[0][:, 2048:4096].rearrange("p (c o) -> p c o", o=1024)
        v1v = self.Wt[1][:].rearrange("p (c o) -> p c o", o=4096)
        u2v = self.Wt[2][:].rearrange("p (c o) -> p c o", o=256)
        wu1 = self.wload(u1v, "L1U1", waits=[pmark])
        wv2 = self.wload(v2v, "L1V2", waits=[pmark])
        wv1 = self.wload(v1v, "L1V1", waits=[pmark])
        wu2 = self.wload(u2v, "L1U2", waits=[pmark])
        ov = self.o_hb.rearrange("(h d) t -> h d t", d=64)
        self._owar = None
        self._corrwar = None
        P.do("gpsimd", lambda e: e.memset(B[1][:, 0:2080], 1.0),
             waits=[(cPE, cPE.n), (cV, cV.n), (cS, cS.n)], incs=[self.cG])
        P.do("gpsimd", lambda e: e.memset(B[5][:, 0:2080], 1.0),
             incs=[self.cG])
        vones = (self.cG, self.cG.n)
        for pr in range(8):
            pvm = (cPE, cPE.n)
            P.dma("sync", B[0][:], self.q_hb[pr * 128:(pr + 1) * 128, :],
                  waits=[(self._qwar if pr == 0 else None), (cPE, pvm[1])],
                  ctr=cL)
            P.dma("sync", B[3][:], self.k_hb[pr * 128:(pr + 1) * 128, :],
                  waits=[self._kwar if pr == 0 else None], ctr=cL)
            P.dma("sync", B[1][:, 0:2080].rearrange("p (t d) -> p t d", d=65)
                  [:, :, 0:64],
                  self.v_hb[2 * pr],
                  waits=[self._vwar if pr == 0 else None,
                         vones if pr == 0 else None], ctr=cL)
            P.dma("sync", B[5][:, 0:2080].rearrange("p (t d) -> p t d", d=65)
                  [:, :, 0:64],
                  self.v_hb[2 * pr + 1],
                  ctr=cL)
            ldp = (cL, cL.n)
            for (bsrc, col) in ((B[0], 52), (B[3], 53)):
                P.do("scalar", lambda e, s=bsrc: e.activation(B[2][:], s[:],
                                                              AF.Square),
                     waits=[ldp, (cV, cV.n)], incs=[cS])
                sq = (cS, cS.n)
                for i in range(NT):
                    pri = PR[i % 2]
                    P.do("tensor",
                         lambda e, i=i, pri=pri: e.matmul(
                             pri[0:1, 0:W], onesb[:],
                             B[2][:, i * W:(i + 1) * W], start=True, stop=True),
                         waits=[sq, self.ps_free(10 + i % 2)], incs=[cPE])
                    pe = (cPE, cPE.n)
                    P.do("vector",
                         lambda e, i=i, pri=pri, col=col: e.reduce_max(
                             VD[0:1, 44 + i:45 + i], pri[0:1, 0:W], AX.X),
                         waits=[pe], incs=[cV])
                    self.ps_read(10 + i % 2, cV)
                P.do("vector",
                     lambda e, col=col: e.reduce_max(VD[0:1, col:col + 1],
                                                     VD[0:1, 44:52], AX.X),
                     waits=[(cV, cV.n)], incs=[cV])
            P.do("vector", lambda e: e.tensor_mul(VD[0:1, 52:53], VD[0:1, 52:53],
                                                  VD[0:1, 53:54]),
                 waits=[(cV, cV.n)], incs=[cV])
            P.do("scalar", lambda e: e.activation(VD[0:1, 53:54], VD[0:1, 52:53],
                                                  AF.Sqrt),
                 waits=[(cV, cV.n)], incs=[cS])
            P.do("vector", lambda e: e.tensor_scalar_mul(VD[0:1, 52:53],
                                                         VD[0:1, 53:54], -0.125),
                 waits=[(cS, cS.n)], incs=[cV])
            P.do("tensor", lambda e: e.matmul(PS[5][:, 0:1], onesf[:],
                                              VD[0:1, 52:53], start=True,
                                              stop=True),
                 waits=[(cV, cV.n), self.ps_free(5)], incs=[cPE])
            P.do("scalar", lambda e: e.activation(VD[:, 54:55], PS[5][:, 0:1],
                                                  AF.Copy),
                 waits=[(cPE, cPE.n)], incs=[cS])
            self.ps_read(5, cS)
            negM = VD[:, 54:55]
            mdone = (cS, cS.n)
            for hh in range(2):
                h = 2 * pr + hh
                qrows = B[0][hh * 64:(hh + 1) * 64, :]
                krows = B[3][hh * 64:(hh + 1) * 64, :]
                vtile = [B[1], B[5]][hh]
                for qg in range(NT):
                    if qg == 0:
                        jts = [(0, 6), (1, 2), (2, 3), (3, 4), (4, 5)]
                    else:
                        jts = [(jt, [-128, 0, 128, 256, 384, 512].index(
                            jt * 128 - qg * 512))
                            for jt in range(max(0, 4 * qg - 1),
                                            min(32, 4 * qg + 5))]
                    ce = None
                    if qg == 0:
                        for jt in range(32):
                            P.do("tensor",
                                 lambda e, jt=jt, kr=krows, qr=qrows: e.matmul(
                                     PS[3][:, jt * 4:(jt + 1) * 4],
                                     kr[:, jt * 128:(jt + 1) * 128],
                                     qr[:, 0:4], start=True, stop=True),
                                 waits=[mdone, ldp, self.ps_free(3)]
                                 if jt == 0 else [],
                                 incs=[cPE] if jt == 31 else [])
                        pe = (cPE, cPE.n)
                        P.do("scalar",
                             lambda e: e.activation(B[2][:, 2048:2176],
                                                    PS[3][:, 0:128], AF.Exp,
                                                    scale=VE[:, 60:61],
                                                    bias=negM),
                             waits=[pe, self._corrwar], incs=[cS])
                        self.ps_read(3, cS)
                        ce = (cS, cS.n)
                    SCB = (0, 1, 3, 4)
                    pend = []

                    def emit_D(mop, ji, jt, mm, first):
                        P.do("tensor",
                             lambda e, m=mop, jt=jt, vt=vtile, f=first:
                             e.matmul(
                                 PS[2][0:65, :],
                                 vt[:, jt * 65:(jt + 1) * 65], m,
                                 start=f, stop=False),
                             waits=[mm] + ([self.ps_free(2)] if first else []),
                             incs=[cPE])
                        self._pmwar[ji % 4] = (cPE, cPE.n)

                    for ji, (jt, bidx) in enumerate(jts):
                        ps = SCB[ji % 4]
                        P.do("tensor",
                             lambda e, jt=jt, qg=qg, p=ps, kr=krows,
                             qr=qrows: e.matmul(
                                 PS[p][:],
                                 kr[:, jt * 128:(jt + 1) * 128],
                                 qr[:, qg * W:(qg + 1) * W],
                                 start=True, stop=True),
                             waits=[mdone if (hh, qg, ji) == (0, 0, 0) else None,
                                    self.ps_free(ps)],
                             incs=[cPE])
                        pe = (cPE, cPE.n)
                        if len(pend) == 4:
                            a = pend.pop(0)
                            emit_D(*a, first=(a[1] == 0))
                        slot = B[2][:, (ji % 4) * W:(ji % 4 + 1) * W]
                        mslot = B[4][:, (ji % 4) * W:(ji % 4 + 1) * W]
                        P.do("scalar",
                             lambda e, p=ps, s=slot: e.activation(
                                 s, PS[p][:], AF.Exp, scale=VE[:, 60:61],
                                 bias=negM),
                             waits=[pe, self._pjwar.get(ji % 4)], incs=[cS])
                        self.ps_read(ps, cS)
                        P.do("vector",
                             lambda e, s=slot, m=mslot, bidx=bidx:
                             e.tensor_mul(m, s, band[:, bidx * W:(bidx + 1) * W]),
                             waits=[(cS, cS.n), self._pmwar.get(ji % 4)],
                             incs=[cV])
                        self._pjwar[ji % 4] = (cV, cV.n)
                        pend.append((mslot, ji, jt, (cV, cV.n)))
                    for a in pend:
                        emit_D(*a, first=(a[1] == 0))
                    if qg == 0:
                        for jt in range(32):
                            P.do("tensor",
                                 lambda e, jt=jt, vt=vtile:
                                 e.matmul(PS[2][0:65, 0:4],
                                          vt[:, jt * 65:jt * 65 + 65],
                                          B[2][:, 2048 + jt * 4:2052 + jt * 4],
                                          start=False, stop=False),
                                 waits=[ce] if jt == 0 else [],
                                 incs=[cPE] if jt == 31 else [])
                        self._corrwar = (cPE, cPE.n)
                    if qg >= 1:
                        gi = len(jts) % 4
                        ps = SCB[gi]
                        P.do("tensor",
                             lambda e, qg=qg, p=ps, kr=krows, qr=qrows:
                             e.matmul(
                                 PS[p][0:4, :], kr[:, 0:4],
                                 qr[:, qg * W:(qg + 1) * W],
                                 start=True, stop=True),
                             waits=[self.ps_free(ps)], incs=[cPE])
                        pe = (cPE, cPE.n)
                        slot = B[2][:, gi * W:(gi + 1) * W]
                        P.do("scalar",
                             lambda e, p=ps, s=slot: e.activation(
                                 s[0:4, :], PS[p][0:4, :], AF.Exp,
                                 scale=VE[0:4, 60:61], bias=negM[0:4, :]),
                             waits=[pe, self._pjwar.get(gi)],
                             incs=[cS])
                        self.ps_read(ps, cS)
                        ge = (cS, cS.n)
                        P.do("tensor",
                             lambda e, s=slot, vt=vtile: e.matmul(
                                 PS[2][0:65, :], vt[0:4, 0:65], s[0:4, :],
                                 start=False, stop=False),
                             waits=[ge], incs=[cPE])
                        self._pjwar[gi] = (cPE, cPE.n)
                    self.emit_att_finalize(ov, h, qg, den64=True)
        attm = (cSP, cSP.n)
        xsv = x_src.rearrange("(c p) t -> p c t", p=128)
        xdv = x_dst.rearrange("(c p) t -> p c t", p=128)
        ohv = self.o_hb.rearrange("(c p) t -> p c t", p=128)
        rN = Rot([B[0], B[1]])
        rX = Rot([F[2], F[3]])
        rX.cons = [(cV, cV.n), (cV, cV.n)]
        self._f1war = 0
        for tc in range(NT):
            nb, war = rN.get(tc)
            P.dma("sync", nb[:].rearrange("p (c w) -> p c w", w=W),
                  ohv[:, :, tc * W:(tc + 1) * W],
                  waits=[war if war else None, attm if tc == 0 else None],
                  ctr=cL)
            xt, warx = rX.get(tc)
            P.dma("sync", xt[:].rearrange("p (c w) -> p c w", w=W),
                  xsv[:, :, tc * W:(tc + 1) * W], waits=[warx], ctr=cL)
            ld = (cL, cL.n)
            for oc in range(C):
                ps = oc % 2
                pe = self.mm_proj(ps, self.Wt[3], oc, nst(nb),
                                  extra_waits=[ld, wo])
                P.do("vector",
                     lambda e, p=ps, o=oc, x=xt: e.scalar_tensor_tensor(
                         F[1][:, o * W:(o + 1) * W], PS[p][:],
                         VD[:, 32 + o:33 + o], x[:, o * W:(o + 1) * W],
                         ALU.add, ALU.add),
                     waits=[pe, (self.cXW, self._f1war) if tc >= 1 else None],
                     incs=[cV])
                self.ps_read(ps, cV)
            rN.consumed(tc, cV)
            rX.consumed(tc, cV)
            P.dma("sync", xdv[:, :, tc * W:(tc + 1) * W],
                  F[1][:].rearrange("p (c w) -> p c w", w=W),
                  waits=[(cV, cV.n)], ctr=self.cXW)
            self._f1war = self.cXW.n
        prj = (self.cXW, self.cXW.n)
        rN = Rot([B[0], B[1]])
        self._ffwar = 0
        self._hwar = {}
        self._qwar2 = None
        for tc in range(NT):
            nb, war = rN.get(tc)
            P.dma("sync", nb[:].rearrange("p (c w) -> p c w", w=W),
                  self.xn_h[tc].rearrange("c p w -> p c w"),
                  waits=[war], ctr=cL)
            ld = (cL, cL.n)
            for rc in range(2):
                for c in range(C):
                    P.do("tensor",
                         lambda e, c=c, rc=rc, nb=nb, f=(c == 0), l=(c == C - 1):
                         e.matmul(PS[rc][:], u1v[:, c, rc * 128:(rc + 1) * 128],
                                  nb[:, c * W:(c + 1) * W], start=f, stop=l),
                         waits=[ld, wu1, self.ps_free(rc)] if c == 0 else [],
                         incs=[cPE] if c == C - 1 else [])
                pe = (cPE, cPE.n)
                P.do("scalar",
                     lambda e, rc=rc: e.activation(B[2][:, rc * W:(rc + 1) * W],
                                                   PS[rc][:], AF.Copy),
                     waits=[pe, self._qwar2 if rc == 0 else None], incs=[cS])
                self.ps_read(rc, cS)
            pev = (cS, cS.n)
            for fc in range(32):
                psh = 2 + fc % 2
                for rc in range(2):
                    P.do("tensor",
                         lambda e, rc=rc, fc=fc, p=psh, f=(rc == 0), l=(rc == 1):
                         e.matmul(PS[p][:], v1v[:, rc, fc * 128:(fc + 1) * 128],
                                  B[2][:, rc * W:(rc + 1) * W],
                                  start=f, stop=l),
                         waits=[pev, wv1, self.ps_free(psh)] if rc == 0 else [],
                         incs=[cPE] if rc == 1 else [])
                pe = (cPE, cPE.n)
                hslot = B[3][:, (fc % 2) * W:(fc % 2 + 1) * W]
                P.do("scalar",
                     lambda e, p=psh, s=hslot, fc=fc: e.activation(
                         s, PS[p][:], AF.Gelu_apprx_tanh,
                         bias=VE[:, 96 + fc:97 + fc]),
                     waits=[pe, self._hwar.get(fc % 2)], incs=[cS])
                self.ps_read(psh, cS)
                he = (cS, cS.n)
                for rc in range(2):
                    P.do("tensor",
                         lambda e, rc=rc, fc=fc, s=hslot, f=(fc == 0):
                         e.matmul(PS[4 + rc][:],
                                  u2v[:, fc, rc * 128:(rc + 1) * 128], s,
                                  start=f, stop=(fc == 31)),
                         waits=[he, wu2] + ([self.ps_free(4 + rc)]
                                            if fc == 0 else []),
                         incs=[cPE])
                self._hwar[fc % 2] = (cPE, cPE.n)
            qpe = (cPE, cPE.n)
            for rc in range(2):
                P.do("scalar",
                     lambda e, rc=rc: e.activation(
                         B[4][:, rc * W:(rc + 1) * W], PS[4 + rc][:], AF.Copy),
                     waits=[qpe], incs=[cS])
                self.ps_read(4 + rc, cS)
            qev = (cS, cS.n)
            self._qwar2 = qev
            for oc in range(C):
                ps = oc % 2
                for rc in range(2):
                    P.do("tensor",
                         lambda e, rc=rc, oc=oc, p=ps, f=(rc == 0), l=(rc == 1):
                         e.matmul(PS[p][:], v2v[:, rc, oc * 128:(oc + 1) * 128],
                                  B[4][:, rc * W:(rc + 1) * W],
                                  start=f, stop=l),
                         waits=[qev, wv2, self.ps_free(ps)] if rc == 0 else [],
                         incs=[cPE] if rc == 1 else [])
                pe = (cPE, cPE.n)
                P.do("vector",
                     lambda e, p=ps, o=oc: e.tensor_scalar(
                         F[1][:, o * W:(o + 1) * W], PS[p][:],
                         VE[:, 42 + o:43 + o], None, ALU.add),
                     waits=[pe, (cSP, self._ffwar) if tc >= 1 else None,
                            prj if tc == 0 else None],
                     incs=[cV])
                self.ps_read(ps, cV)
            rN.consumed(tc, cV)
            P.dma("gpsimd", xdv[:, :, tc * W:(tc + 1) * W],
                  F[1][:].rearrange("p (c w) -> p c w", w=W),
                  waits=[(cV, cV.n)], ctr=cSP, accum_op=ALU.add)
            self._ffwar = cSP.n

    def emit_l2(self, x_src, x_dst):
        P = self.P
        cL, cPE, cV, cS, cSP = self.cL, self.cPE, self.cV, self.cS, self.cSP
        F, B, PS, PR = self.F, self.B, self.PS, self.PR
        VE, VD, XM, ST = self.VE, self.VD, self.XM, self.ST
        onesb = self.CN["c_ones_col_bf"]
        onesf = self.CN["c_ones_row_f"]
        self._pjwar = {}
        self._pmwar = {}
        qkvT = self.wd["L2qkvT"].rearrange("(c p) o -> p c o", p=128)
        P.do("gpsimd", lambda e: e.memset(B[4][:, 0:T], 0.0),
             incs=[self.cG])
        kzz = (self.cG, self.cG.n)
        kzv0 = self.kz_hb[0].rearrange("(c p) t -> p c t", p=128)
        kzv1 = self.kz_hb[1].rearrange("(c p) t -> p c t", p=128)
        for c in range(C):
            P.dma("gpsimd", kzv0[64:128, c, :], B[4][0:64, 0:T],
                  waits=[kzz] if c == 0 else [], ctr=cSP)
            P.dma("gpsimd", kzv1[0:64, c, :], B[4][0:64, 0:T], ctr=cSP)
        kzfill = (cSP, cSP.n)
        wq = self.cW, self.cW.n
        P.dma("gpsimd", self.wv8(self.Wt[0]), qkvT[:, :, 0:1024], ctr=self.cW)
        wq = (self.cW, self.cW.n)
        P.dma("gpsimd", self.wv8(self.Wt[1]), qkvT[:, :, 1024:2048], ctr=self.cW)
        wk = (self.cW, self.cW.n)
        P.dma("gpsimd", self.wv8(self.Wt[2]), qkvT[:, :, 2048:3072], ctr=self.cW)
        wv_ = (self.cW, self.cW.n)
        wo = self.wload(self.wv8(self.Wt[3]), "L2projT")
        self.emit_ln(x_src, want_xm=True)
        ln = self.lnmark
        xmdone = (cV, cV.n)
        self._fw = ln
        for half, colbase in ((0, 0), (1, 512)):
            for oc in range(C):
                P.dma("sync", F[3][:, 0:1024].rearrange("p (c o) -> p c o", o=128),
                      qkvT[:, :, half * 1024 + oc * 128:half * 1024 + (oc + 1) * 128],
                      waits=[self._fw], ctr=cL)
                ld = (cL, cL.n)
                for c in range(C):
                    P.do("tensor",
                         lambda e, c=c, f=(c == 0), l=(c == C - 1): e.matmul(
                             PS[0][:, 0:32],
                             F[3][:, c * 128:(c + 1) * 128],
                             XM[:, c * 32:(c + 1) * 32], start=f, stop=l),
                         waits=[ld, xmdone, self.ps_free(0)] if c == 0 else [],
                         incs=[cPE] if c == C - 1 else [])
                pe = (cPE, cPE.n)
                self._fw = pe
                P.do("scalar",
                     lambda e, o=oc, cb=colbase: e.activation(
                         F[2][:, cb + o * 32:cb + (o + 1) * 32], PS[0][:, 0:32],
                         AF.Copy),
                     waits=[pe], incs=[cS])
                self.ps_read(0, cS)
        qkm = (cS, cS.n)
        for h in range(16):
            r0 = 64 * (h % 2)
            qm_h = F[2][r0:r0 + 64, (h // 2) * 32:(h // 2) * 32 + 32]
            km_h = F[2][r0:r0 + 64, 512 + (h // 2) * 32:512 + (h // 2) * 32 + 32]
            P.do("tensor",
                 lambda e, q=qm_h, k=km_h: e.matmul(PS[1][0:32, 0:32], q, k,
                                                    start=True, stop=True),
                 waits=[qkm if h == 0 else None, self.ps_free(1)], incs=[cPE])
            pe = (cPE, cPE.n)
            P.do("scalar",
                 lambda e, h=h: e.activation(
                     F[3][32 * (h % 2):32 * (h % 2) + 32,
                          (h // 2) * 32:(h // 2) * 32 + 32],
                     PS[1][0:32, 0:32], AF.Copy),
                 waits=[pe], incs=[cS])
            self.ps_read(1, cS)
        simd = (cS, cS.n)
        sim3 = F[3][0:64, 0:256].rearrange("p (hc k) -> p hc k", k=32)
        cnt3 = F[3][0:64, 256:512].rearrange("p (hc k) -> p hc k", k=32)
        for i in range(32):
            P.do("vector",
                 lambda e, i=i: e.tensor_tensor(
                     F[3][0:64, 512:768].rearrange("p (hc k) -> p hc k", k=32),
                     sim3, sim3[:, :, i:i + 1].broadcast_to([64, 8, 32]),
                     ALU.is_gt),
                 waits=[simd] if i == 0 else [], incs=[cV])
            P.do("vector",
                 lambda e, i=i: e.tensor_reduce(
                     cnt3[:, :, i:i + 1].rearrange("p hc one -> p (hc one)")
                     if False else cnt3[:, :, i],
                     F[3][0:64, 512:768].rearrange("p (hc k) -> p hc k", k=32),
                     AX.X, ALU.add),
                 incs=[cV])
        P.do("vector",
             lambda e: e.tensor_single_scalar(F[3][0:64, 256:512],
                                              F[3][0:64, 256:512], 15.0,
                                              ALU.is_le),
             incs=[cV])
        mdone0 = (cV, cV.n)
        P.do("vector", lambda e: e.tensor_copy(B[2][0:64, 0:256],
                                               F[3][0:64, 256:512]),
             waits=[mdone0], incs=[cV])
        mb = (cV, cV.n)
        ident = self.CN["c_ident"]
        for hc in range(C):
            P.do("tensor",
                 lambda e, hc=hc: e.transpose(
                     self.PS[2].bitcast(BF16)[0:32, 0:64] if False else
                     self._psbf(2)[0:32, 0:64],
                     B[2][0:64, 256 + 0:256][:, 0:0] if False else
                     B[2][0:64, hc * 32:(hc + 1) * 32],
                     ident[0:64, 0:64]),
                 waits=[mb, self.ps_free(2)], incs=[cPE])
            pe = (cPE, cPE.n)
            P.do("scalar",
                 lambda e, hc=hc: e.activation(
                     F[3][0:32, 768 + hc * 64:768 + (hc + 1) * 64],
                     self._psbf(2)[0:32, 0:64], AF.Copy),
                 waits=[pe], incs=[cS])
            self.ps_read(2, cS)
        mtd = (cS, cS.n)
        P.dma("gpsimd", self.mT_hb[0:32, :], F[3][0:32, 768:1280],
              waits=[mtd], ctr=cSP)
        mhbm = (cSP, cSP.n)
        nst = lambda buf: (lambda c: buf[:, c * W:(c + 1) * W])
        self.emit_qkv64(self.Wt[0], self.Wt[1], self.Wt[2], wq, wk, wv_,
                        None, None, ln, war0=kzfill, kz=True)
        ov = self.o_hb.rearrange("(h d) t -> h d t", d=64)
        self._owar = None
        self._mrowwar = None
        P.do("gpsimd", lambda e: e.memset(B[1][:, 0:2080], 1.0),
             waits=[(cPE, cPE.n), (cV, cV.n), (cS, cS.n)], incs=[self.cG])
        P.do("gpsimd", lambda e: e.memset(B[5][:, 0:2080], 1.0),
             incs=[self.cG])
        vones = (self.cG, self.cG.n)
        self._kzwar = None
        for pr in range(8):
            pvm = (cPE, cPE.n)
            P.dma("sync", B[0][:], self.q_hb[pr * 128:(pr + 1) * 128, :],
                  waits=[(self._qwar if pr == 0 else None), (cPE, pvm[1])],
                  ctr=cL)
            P.dma("sync", B[1][:, 0:2080].rearrange("p (t d) -> p t d", d=65)
                  [:, :, 0:64],
                  self.v_hb[2 * pr],
                  waits=[self._vwar if pr == 0 else None,
                         vones if pr == 0 else None], ctr=cL)
            P.dma("sync", B[5][:, 0:2080].rearrange("p (t d) -> p t d", d=65)
                  [:, :, 0:64],
                  self.v_hb[2 * pr + 1],
                  ctr=cL)
            for hh in range(2):
                h = 2 * pr + hh
                P.dma("sync", B[3][:],
                      self.kz_hb[hh][pr * 128:(pr + 1) * 128, :],
                      waits=[self._kwar if (pr, hh) == (0, 0) else None] +
                      (self._kzwar or []), ctr=cL)
                ldp = (cL, cL.n)
                self.emit_pair_bound(ldp, 0.125)
                negM = VD[:, 54:55]
                mdone = (cS, cS.n)
                qrows = B[0]
                krows = B[3]
                vtile = [B[1], B[5]][hh]
                P.dma("sync", ST[0:1, 0:1024].rearrange("o (k q) -> o k q", q=32),
                      self.mT_hb[:, pr * 64 + hh * 32:pr * 64 + hh * 32 + 32]
                      .rearrange("k q -> k q")[None] if False else
                      self.mT_hb[:, pr * 64 + hh * 32:pr * 64 + hh * 32 + 32],
                      waits=[mhbm if pr == 0 and hh == 0 else None,
                             self._mrowwar], ctr=cL)
                mrl = (cL, cL.n)
                for g2 in range(2):
                    P.do("tensor",
                         lambda e, g2=g2: e.matmul(
                             PS[5][:], onesf[:],
                             ST[0:1, g2 * 512:(g2 + 1) * 512],
                             start=True, stop=True),
                         waits=[mrl, self.ps_free(5)], incs=[cPE])
                    pe = (cPE, cPE.n)
                    P.do("scalar",
                         lambda e, g2=g2: e.activation(
                             B[2][:, 2048 + g2 * 512:2048 + (g2 + 1) * 512],
                             PS[5][:], AF.Copy),
                         waits=[pe], incs=[cS])
                    self.ps_read(5, cS)
                self._mrowwar = (cS, cS.n)
                mex = (cS, cS.n)
                mexT = B[2][:, 2048:3072]
                SCB = (0, 1, 3, 4)
                for qg in range(NT):
                    pend = []

                    def emit_D(mslot, kt, mm, first):
                        P.do("tensor",
                             lambda e, m=mslot, kt=kt, vt=vtile, f=first:
                             e.matmul(PS[2][0:65, :],
                                      vt[:, kt * 65:(kt + 1) * 65], m,
                                      start=f, stop=False),
                             waits=[mm] + ([self.ps_free(2)] if first else []),
                             incs=[cPE])
                        self._pmwar[kt % 4] = (cPE, cPE.n)

                    for kt in range(32):
                        ps = SCB[kt % 4]
                        P.do("tensor",
                             lambda e, kt=kt, qg=qg, p=ps, kr=krows, qr=qrows:
                             e.matmul(PS[p][:],
                                      kr[:, kt * 128:(kt + 1) * 128],
                                      qr[:, qg * W:(qg + 1) * W],
                                      start=True, stop=True),
                             waits=[mdone if (hh, qg, kt) == (0, 0, 0) else None,
                                    mex if (qg, kt) == (0, 0) else None,
                                    self.ps_free(ps)],
                             incs=[cPE])
                        pe = (cPE, cPE.n)
                        if len(pend) == 4:
                            a = pend.pop(0)
                            emit_D(*a, first=(a[1] == 0))
                        slot = B[2][:, (kt % 4) * W:(kt % 4 + 1) * W]
                        mslot = B[4][:, (kt % 4) * W:(kt % 4 + 1) * W]
                        P.do("scalar",
                             lambda e, p=ps, s=slot: e.activation(
                                 s, PS[p][:], AF.Exp, scale=VE[:, 60:61],
                                 bias=negM),
                             waits=[pe, self._pjwar.get(kt % 4)], incs=[cS])
                        self.ps_read(ps, cS)
                        P.do("vector",
                             lambda e, s=slot, m=mslot, kt=kt, qg=qg, mx=mexT:
                             e.tensor_tensor(
                                 m.rearrange("p (b r) -> p b r", r=128),
                                 s.rearrange("p (b r) -> p b r", r=128),
                                 mx[:, kt * 32 + qg * 4:kt * 32 + qg * 4 + 4]
                                 .rearrange("p (b one) -> p b one", one=1)
                                 .broadcast_to([128, 4, 128]),
                                 ALU.mult),
                             waits=[(cS, cS.n), self._pmwar.get(kt % 4)],
                             incs=[cV])
                        self._pjwar[kt % 4] = (cV, cV.n)
                        pend.append((mslot, kt, (cV, cV.n)))
                    for args in pend:
                        emit_D(*args, first=(args[1] == 0))
                    self.emit_att_finalize(ov, h, qg, den64=True)
                self._kzwar = [(cPE, cPE.n), (cS, cS.n)]
        attm = (cSP, cSP.n)
        xsv = x_src.rearrange("(c p) t -> p c t", p=128)
        xdv = x_dst.rearrange("(c p) t -> p c t", p=128)
        ohv = self.o_hb.rearrange("(c p) t -> p c t", p=128)
        rN = Rot([B[0], B[1]])
        rX = Rot([F[2], F[3]])
        rX.cons = [(cV, cV.n), (cV, cV.n)]
        self._f1war = 0
        for tc in range(NT):
            nb, war = rN.get(tc)
            P.dma("sync", nb[:].rearrange("p (c w) -> p c w", w=W),
                  ohv[:, :, tc * W:(tc + 1) * W],
                  waits=[war if war else None, attm if tc == 0 else None],
                  ctr=cL)
            xt, warx = rX.get(tc)
            P.dma("sync", xt[:].rearrange("p (c w) -> p c w", w=W),
                  xsv[:, :, tc * W:(tc + 1) * W], waits=[warx], ctr=cL)
            ld = (cL, cL.n)
            for oc in range(C):
                ps = oc % 2
                pe = self.mm_proj(ps, self.Wt[3], oc, nst(nb),
                                  extra_waits=[ld, wo])
                P.do("vector",
                     lambda e, p=ps, o=oc, x=xt: e.scalar_tensor_tensor(
                         F[1][:, o * W:(o + 1) * W], PS[p][:],
                         VE[:, 50 + o:51 + o], x[:, o * W:(o + 1) * W],
                         ALU.add, ALU.add),
                     waits=[pe, (self.cXW, self._f1war) if tc >= 1 else None],
                     incs=[cV])
                self.ps_read(ps, cV)
            rN.consumed(tc, cV)
            rX.consumed(tc, cV)
            P.dma("sync", xdv[:, :, tc * W:(tc + 1) * W],
                  F[1][:].rearrange("p (c w) -> p c w", w=W),
                  waits=[(cV, cV.n)], ctr=self.cXW)
            self._f1war = self.cXW.n
        prj = (self.cXW, self.cXW.n)
        sgT = self.wd["L2sgnT"].rearrange("(c p) o -> p c o", p=128)
        w3v = self.wv8(self.Wt[3])
        self._sgw = (cPE, cPE.n)
        for c in range(C):
            P.dma("sync", F[3][:, 0:1024].rearrange("p (c o) -> p c o", o=1024)
                  [:, 0, :] if False else F[3][:, 0:1024],
                  sgT[:, c, :], waits=[self._sgw, prj if c == 0 else None],
                  ctr=cL)
            ld = (cL, cL.n)
            P.do("scalar",
                 lambda e, c=c: e.activation(w3v[:, c, :], F[3][:, 0:1024],
                                             AF.Sign),
                 waits=[ld], incs=[cS])
            self._sgw = (cS, cS.n)
        sgd = (cS, cS.n)
        rN = Rot([B[0], B[1]])
        self._ffwar = 0
        for tc in range(NT):
            nb, war = rN.get(tc)
            P.dma("sync", nb[:].rearrange("p (c w) -> p c w", w=W),
                  self.xn_h[tc].rearrange("c p w -> p c w"),
                  waits=[war], ctr=cL)
            ld = (cL, cL.n)
            for oc in range(C):
                ps = oc % 2
                pe = self.mm_proj(ps, self.Wt[3], oc, nst(nb),
                                  extra_waits=[ld, sgd])
                P.do("vector",
                     lambda e, p=ps, o=oc: e.tensor_copy(
                         F[1][:, o * W:(o + 1) * W], PS[p][:]),
                     waits=[pe, (cSP, self._ffwar) if tc >= 1 else None],
                     incs=[cV])
                self.ps_read(ps, cV)
            rN.consumed(tc, cV)
            P.dma("gpsimd", xdv[:, :, tc * W:(tc + 1) * W],
                  F[1][:].rearrange("p (c w) -> p c w", w=W),
                  waits=[(cV, cV.n)], ctr=cSP, accum_op=ALU.add)
            self._ffwar = cSP.n

    def emit_l3(self, x_src, x_dst):
        P = self.P
        cL, cPE, cV, cS, cSP = self.cL, self.cPE, self.cV, self.cS, self.cSP
        F, B, PS, PR = self.F, self.B, self.PS, self.PR
        VE, VD = self.VE, self.VD
        onesb = self.CN["c_ones_col_bf"]
        onesf = self.CN["c_ones_row_f"]
        cswp = self.CN["c_swp"]
        CCS = self.CCS
        self._ccwar = None
        ctri = self.CN["c_ctri"]
        self._pjwar = {}
        self._pmwar = {}
        waq = self.wload(self.wv8(self.Wt[0]), "L3waq")
        wak = self.wload(self.wv8(self.Wt[1]), "L3wak")
        wav = self.wload(self.wv8(self.Wt[2]), "L3wav")
        wao = self.wload(self.wv8(self.Wt[3]), "L3wao")
        P.dma("gpsimd", self.SM[:, 0:128].rearrange("p (c o) -> p c o", o=16),
              self.wd["L3sel"].rearrange("(c p) o -> p c o", p=128), ctr=self.cW)
        wsel = (self.cW, self.cW.n)
        self.emit_ln(x_src)
        ln = self.lnmark
        P.do("tensor", lambda e: e.matmul(PS[5][:, 0:1], onesf[:],
                                          VE[0:1, 58:59], start=True, stop=True),
             waits=[self.consts_ld, self.ps_free(5)], incs=[cPE])
        P.do("scalar", lambda e: e.activation(VD[:, 56:57], PS[5][:, 0:1],
                                              AF.Copy),
             waits=[(cPE, cPE.n)], incs=[cS])
        self.ps_read(5, cS)
        qv3 = self.q_hb.rearrange("(h d) t -> h d t", d=128)
        kv3 = self.k_hb.rearrange("(h d) t -> h d t", d=128)
        nst = lambda buf: (lambda c: buf[:, c * W:(c + 1) * W])
        rN = Rot([B[0], B[1]])
        self._qsp = None
        self._vsp = None
        for tc in range(NT):
            nb, war = rN.get(tc)
            P.dma("sync", nb[:].rearrange("p (c w) -> p c w", w=W),
                  self.xn_h[tc].rearrange("c p w -> p c w"),
                  waits=[war, ln], ctr=cL)
            P.dma("sync", CCS[:, 0:W], self.cd["c_cc"][:, tc * W:(tc + 1) * W],
                  waits=[self._ccwar], ctr=cL)
            P.dma("sync", CCS[:, W:2 * W],
                  self.cd["c_ss"][:, tc * W:(tc + 1) * W], ctr=cL)
            ccld = (cL, cL.n)
            ld = (cL, cL.n)
            for h in range(C):
                for (wt, wld, dst, dorope) in ((self.Wt[0], waq, qv3, True),
                                               (self.Wt[1], wak, kv3, True),
                                               (self.Wt[2], wav, None, False)):
                    if dorope:
                        ps = h % 2
                        pe = self.mm_proj(ps, wt, h, nst(nb),
                                          extra_waits=[ld, wld])
                        P.do("scalar",
                             lambda e, p=ps: e.activation(B[2][:, 0:W],
                                                          PS[p][:], AF.Copy),
                             waits=[pe, self._qsp], incs=[cS])
                        self.ps_read(ps, cS)
                        ev1 = (cS, cS.n)
                        P.do("tensor",
                             lambda e: e.matmul(PS[4][:], cswp[:], B[2][:, 0:W],
                                                start=True, stop=True),
                             waits=[ev1, self.ps_free(4)], incs=[cPE])
                        pe2 = (cPE, cPE.n)
                        P.do("scalar",
                             lambda e: e.activation(B[2][:, W:2 * W], PS[4][:],
                                                    AF.Copy),
                             waits=[pe2], incs=[cS])
                        self.ps_read(4, cS)
                        ev2 = (cS, cS.n)
                        P.do("vector",
                             lambda e: e.tensor_mul(
                                 F[3][:, 0:W], B[2][:, 0:W], CCS[:, 0:W]),
                             waits=[ev1, ccld], incs=[cV])
                        P.do("vector",
                             lambda e: e.tensor_mul(
                                 F[3][:, W:2 * W], B[2][:, W:2 * W],
                                 CCS[:, W:2 * W]),
                             waits=[ev2], incs=[cV])
                        P.do("gpsimd",
                             lambda e: e.tensor_add(B[2][:, 2 * W:3 * W],
                                                    F[3][:, 0:W],
                                                    F[3][:, W:2 * W]),
                             waits=[(cV, cV.n), (self.cG, self.cG.n)],
                             incs=[self.cG])
                        self._qsp = (self.cG, self.cG.n)
                        P.dma("sync", dst[h][:, tc * W:(tc + 1) * W],
                              B[2][:, 2 * W:3 * W],
                              waits=[(self.cG, self.cG.n)], ctr=cSP)
                    else:
                        w2v = self.wv8(wt)
                        for tt in range(4):
                            ps = 2 + tt % 2
                            for c in range(C):
                                P.do("tensor",
                                     lambda e, c=c, h=h, t=tt, p=ps, nb=nb,
                                     f=(c == 0), l=(c == C - 1):
                                     e.matmul(PS[p][:, 0:128],
                                              nb[:, c * W + t * 128:
                                                 c * W + (t + 1) * 128],
                                              w2v[:, c, h * 128:(h + 1) * 128],
                                              start=f, stop=l),
                                     waits=[wld, self.ps_free(ps)]
                                     if c == 0 else [],
                                     incs=[cPE] if c == C - 1 else [])
                            pe = (cPE, cPE.n)
                            P.do("scalar",
                                 lambda e, p=ps, t=tt: e.activation(
                                     B[3][:, t * 128:(t + 1) * 128],
                                     PS[p][:, 0:128], AF.Copy),
                                 waits=[pe, self._vsp if tt == 0 else None],
                                 incs=[cS])
                            self.ps_read(ps, cS)
                        P.dma("gpsimd",
                              self.v_hb3[h][:, tc * 4:(tc + 1) * 4, :],
                              B[3][:, 0:512].rearrange("p (t d) -> p t d",
                                                       d=128),
                              waits=[(cS, cS.n)], ctr=cSP)
                        self._vsp = (cSP, cSP.n)
            self._ccwar = (cV, cV.n)
            rN.consumed(tc, cPE)
        qkvm = (cSP, cSP.n)
        ov3 = self.o_hb.rearrange("(h d) t -> h d t", d=128)
        self._owar = None
        for h in range(C):
            pvm = (cPE, cPE.n)
            P.dma("sync", B[0][:], self.q_hb[h * 128:(h + 1) * 128, :],
                  waits=[qkvm if h == 0 else None, (cPE, pvm[1])], ctr=cL)
            P.dma("sync", B[3][:], self.k_hb[h * 128:(h + 1) * 128, :],
                  ctr=cL)
            P.dma("sync", B[1][:],
                  self.v_hb3[h].rearrange("p t d -> p (t d)"),
                  ctr=cL)
            ldp = (cL, cL.n)
            self.emit_pair_bound(ldp, 1.0 / 128.0)
            negM = VD[:, 54:55]
            mdone = (cS, cS.n)
            SCB = (0, 1, 3, 4)
            for qg in range(NT):
                nj = 4 * qg + 4
                pend = []

                def emit_D(mop, jt, mm, first):
                    P.do("tensor",
                         lambda e, m=mop, f=first: e.matmul(
                             PR[0][0:1, 0:W], onesb[:], m, start=f, stop=False),
                         waits=[mm] + ([self.ps_free(10)] if first else []),
                         incs=[cPE])
                    P.do("tensor",
                         lambda e, m=mop, jt=jt, f=first: e.matmul(
                             PS[2][:], B[1][:, jt * 128:(jt + 1) * 128], m,
                             start=f, stop=False),
                         waits=[self.ps_free(2)] if first else [],
                         incs=[cPE])
                    self._pmwar[jt % 4] = (cPE, cPE.n)
                    self._pjwar[jt % 4] = (cPE, cPE.n)

                for jt in range(nj):
                    ps = SCB[jt % 4]
                    diag_i = jt - 4 * qg
                    P.do("tensor",
                         lambda e, jt=jt, qg=qg, p=ps: e.matmul(
                             PS[p][:], B[3][:, jt * 128:(jt + 1) * 128],
                             B[0][:, qg * W:(qg + 1) * W],
                             start=True, stop=True),
                         waits=[mdone if (qg, jt) == (0, 0) else None,
                                self.ps_free(ps)],
                         incs=[cPE])
                    pe = (cPE, cPE.n)
                    if len(pend) == 4:
                        a = pend.pop(0)
                        emit_D(*a, first=(a[1] == 0))
                    slot = B[2][:, (jt % 4) * W:(jt % 4 + 1) * W]
                    P.do("scalar",
                         lambda e, p=ps, s=slot: e.activation(
                             s, PS[p][:], AF.Exp, scale=VE[:, 63:64],
                             bias=negM),
                         waits=[pe, self._pjwar.get(jt % 4)], incs=[cS])
                    self.ps_read(ps, cS)
                    if diag_i >= 0:
                        mslot = B[4][:, (jt % 4) * W:(jt % 4 + 1) * W]
                        P.do("vector",
                             lambda e, s=slot, m=mslot, i=diag_i: e.tensor_mul(
                                 m, s, ctri[:, i * W:(i + 1) * W]),
                             waits=[(cS, cS.n), self._pmwar.get(jt % 4)],
                             incs=[cV])
                        self._pjwar[jt % 4] = (cV, cV.n)
                        pend.append((mslot, jt, (cV, cV.n)))
                    else:
                        self._pjwar[jt % 4] = (cS, cS.n)
                        pend.append((slot, jt, (cS, cS.n)))
                for a in pend:
                    emit_D(*a, first=(a[1] == 0))
                self.emit_att_finalize(ov3, h, qg, drows=128)
        attm = (cSP, cSP.n)
        xsv = x_src.rearrange("(c p) t -> p c t", p=128)
        xdv = x_dst.rearrange("(c p) t -> p c t", p=128)
        ohv = self.o_hb.rearrange("(c p) t -> p c t", p=128)
        rN = Rot([B[0], B[1]])
        rX = Rot([F[2], F[3]])
        rX.cons = [(cV, cV.n), (cV, cV.n)]
        self._f1war = 0
        for tc in range(NT):
            nb, war = rN.get(tc)
            P.dma("sync", nb[:].rearrange("p (c w) -> p c w", w=W),
                  ohv[:, :, tc * W:(tc + 1) * W],
                  waits=[war if war else None, attm if tc == 0 else None],
                  ctr=cL)
            xt, warx = rX.get(tc)
            P.dma("sync", xt[:].rearrange("p (c w) -> p c w", w=W),
                  xsv[:, :, tc * W:(tc + 1) * W], waits=[warx], ctr=cL)
            ld = (cL, cL.n)
            for oc in range(C):
                ps = oc % 2
                pe = self.mm_proj(ps, self.Wt[3], oc, nst(nb),
                                  extra_waits=[ld, wao])
                P.do("vector",
                     lambda e, p=ps, o=oc, x=xt: e.tensor_add(
                         F[1][:, o * W:(o + 1) * W], PS[p][:],
                         x[:, o * W:(o + 1) * W]),
                     waits=[pe, (self.cXW, self._f1war) if tc >= 1 else None],
                     incs=[cV])
                self.ps_read(ps, cV)
            rN.consumed(tc, cV)
            rX.consumed(tc, cV)
            P.dma("sync", xdv[:, :, tc * W:(tc + 1) * W],
                  F[1][:].rearrange("p (c w) -> p c w", w=W),
                  waits=[(cV, cV.n)], ctr=self.cXW)
            self._f1war = self.cXW.n
        prj = (self.cXW, self.cXW.n)
        rN = Rot([B[0], B[1]])
        for tc in range(NT):
            nb, war = rN.get(tc)
            P.dma("sync", nb[:].rearrange("p (c w) -> p c w", w=W),
                  self.xn_h[tc].rearrange("c p w -> p c w"),
                  waits=[war], ctr=cL)
            ld = (cL, cL.n)
            for tt in range(4):
                ps = tt % 2
                for c in range(C):
                    P.do("tensor",
                         lambda e, c=c, t=tt, p=ps, nb=nb, f=(c == 0),
                         l=(c == C - 1): e.matmul(
                             PS[p][:, 0:16],
                             nb[:, c * W + t * 128:c * W + (t + 1) * 128],
                             self.SM[:, 0:128].rearrange(
                                 "p (c o) -> p c o", o=16)[:, c, :],
                             start=f, stop=l),
                         waits=[ld, wsel, self.ps_free(ps)] if c == 0 else [],
                         incs=[cPE] if c == C - 1 else [])
                pe = (cPE, cPE.n)
                P.do("scalar",
                     lambda e, p=ps, t=tt, tc=tc: e.activation(
                         F[2][:, (tc * 4 + t) * 16:(tc * 4 + t + 1) * 16],
                         PS[p][:, 0:16], AF.Sigmoid),
                     waits=[pe], incs=[cS])
                self.ps_read(ps, cS)
            rN.consumed(tc, cPE)
        seld = (cS, cS.n)
        sel3 = F[2][:, 0:512].rearrange("p (t e) -> p t e", e=16)
        cnt3 = F[2][:, 512:1024].rearrange("p (t e) -> p t e", e=16)
        for i in range(16):
            P.do("vector",
                 lambda e, i=i: e.tensor_tensor(
                     F[3][:, 0:512].rearrange("p (t e) -> p t e", e=16),
                     sel3, sel3[:, :, i:i + 1].broadcast_to([128, 32, 16]),
                     ALU.is_gt),
                 waits=[seld if i == 0 else None, (cV, cV.n)], incs=[cV])
            P.do("vector",
                 lambda e, i=i: e.tensor_reduce(
                     cnt3[:, :, i],
                     F[3][:, 0:512].rearrange("p (t e) -> p t e", e=16),
                     AX.X, ALU.add),
                 waits=[(cV, cV.n)], incs=[cV])
        P.do("vector", lambda e: e.tensor_single_scalar(
            F[2][:, 512:1024], F[2][:, 512:1024], 3.0, ALU.is_le),
             waits=[(cV, cV.n)], incs=[cV])
        P.do("vector", lambda e: e.tensor_mul(F[3][:, 0:512], F[2][:, 0:512],
                                              F[2][:, 512:1024]),
             waits=[(cV, cV.n)], incs=[cV])
        P.do("vector", lambda e: e.tensor_reduce(
            F[3][:, 512:544], F[3][:, 0:512].rearrange("p (t e) -> p t e",
                                                       e=16),
            AX.X, ALU.add), waits=[(cV, cV.n)], incs=[cV])
        P.do("vector", lambda e: e.tensor_single_scalar(
            F[3][:, 512:544], F[3][:, 512:544], 1e-9, ALU.max),
             waits=[(cV, cV.n)], incs=[cV])
        P.do("vector", lambda e: e.reciprocal(F[3][:, 544:576],
                                              F[3][:, 512:544]),
             waits=[(cV, cV.n)], incs=[cV])
        P.do("vector", lambda e: e.tensor_tensor(
            F[3][:, 0:512].rearrange("p (t e) -> p t e", e=16),
            F[3][:, 0:512].rearrange("p (t e) -> p t e", e=16),
            F[3][:, 544:576].rearrange("p (t o) -> p t o", o=1)
            .broadcast_to([128, 32, 16]), ALU.mult),
             waits=[(cV, cV.n)], incs=[cV])
        wnd = (cV, cV.n)
        WN = F[3]
        P.do("vector", lambda e: e.tensor_copy(B[5][:, 0:512], F[3][:, 0:512]),
             waits=[(cV, cV.n)], incs=[cV])
        WNB = B[5]
        wnd = (cV, cV.n)
        if getattr(self, "dbg_l3", None) is not None:
            P.dma("sync", self.dbg_l3["sel"], F[2][:, 0:512],
                  waits=[wnd], ctr=cSP)
            P.dma("sync", self.dbg_l3["wn"], F[3][:, 0:576],
                  waits=[wnd], ctr=cSP)
            wnd = (cSP, cSP.n)
        kv_of = lambda wt, slot: wt[:, slot * 4096:slot * 4096 + 2048].rearrange(
            "p (c o) -> p c o", o=256)
        vv_of = lambda wt, slot: wt[:, slot * 4096 + 2048:(slot + 1) * 4096
                                    ].rearrange("p (c o) -> p c o", o=1024)
        self._pairwar = {}
        self._ffwar = 0
        self._hswar = {}
        self._wbwar = {}
        self._wrowwar = None
        rNf = Rot([B[0], B[1]])
        for tc in range(NT):
            nb, war = rNf.get(tc)
            P.dma("sync", nb[:].rearrange("p (c w) -> p c w", w=W),
                  self.xn_h[tc].rearrange("c p w -> p c w"),
                  waits=[war, wnd if tc == 0 else None], ctr=cL)
            ld = (cL, cL.n)
            for pss in range(4):
                pair = (self.Wt[0], self.Wt[1]) if pss % 2 == 0 else                     (self.Wt[2], self.Wt[3])
                wwait = [self._pairwar.get(pss % 2),
                         prj if (tc, pss) == (0, 0) else None]
                for ei in range(4):
                    ex = pss * 4 + ei
                    wt, slot = pair[ei // 2], ei % 2
                    P.dma("gpsimd", kv_of(wt, slot),
                          self.wd["L3keys"][ex].rearrange("(c p) o -> p c o",
                                                          p=128),
                          waits=wwait if ei == 0 else [], ctr=self.cW)
                    P.dma("gpsimd", vv_of(wt, slot),
                          self.wd["L3vals"][ex].rearrange("(c p) o -> p c o",
                                                          p=128),
                          ctr=self.cW)
                wmark = (self.cW, self.cW.n)
                identf = self.CN["c_identf"]
                for ei in range(4):
                    ex = pss * 4 + ei
                    wt, slot = pair[ei // 2], ei % 2
                    for tt in range(4):
                        P.do("tensor",
                             lambda e, tt=tt, ex=ex, tc=tc: e.matmul(
                                 PR[1][0:1, tt * 128:(tt + 1) * 128],
                                 WNB[:, (tc * 4 + tt) * 16 + ex:
                                     (tc * 4 + tt) * 16 + ex + 1],
                                 self.CN["c_ident"][:], start=True, stop=True),
                             waits=[wnd if (tc, pss, ei, tt) == (0, 0, 0, 0)
                                    else None,
                                    self.ps_free(11) if tt == 0 else None],
                             incs=[cPE] if tt == 3 else [])
                    pewb = (cPE, cPE.n)
                    P.do("scalar",
                         lambda e: e.activation(F[2][0:1, 512:1024],
                                                PR[1][0:1, 0:512], AF.Copy),
                         waits=[pewb, self._wrowwar], incs=[cS])
                    self.ps_read(11, cS)
                    wre = (cS, cS.n)
                    P.do("tensor",
                         lambda e: e.matmul(PS[5][:], onesf[:],
                                            F[2][0:1, 512:1024],
                                            start=True, stop=True),
                         waits=[wre, self.ps_free(5)], incs=[cPE])
                    self._wrowwar = (cPE, cPE.n)
                    pewb2 = (cPE, cPE.n)
                    P.do("scalar",
                         lambda e, ei=ei: e.activation(
                             B[4][:, ei * 512:(ei + 1) * 512], PS[5][:],
                             AF.Copy),
                         waits=[pewb2, self._wbwar.get(ei)], incs=[cS])
                    self.ps_read(5, cS)
                    for half in range(2):
                        ps = half
                        for c in range(C):
                            P.do("tensor",
                                 lambda e, c=c, h=half, nb=nb, wt=wt,
                                 slot=slot, p=ps, f=(c == 0), l=(c == C - 1):
                                 e.matmul(
                                     PS[p][:],
                                     kv_of(wt, slot)[:, c,
                                                     h * 128:(h + 1) * 128],
                                     nb[:, c * W:(c + 1) * W],
                                     start=f, stop=l),
                                 waits=[ld, wmark, self.ps_free(ps)]
                                 if c == 0 else [],
                                 incs=[cPE] if c == C - 1 else [])
                        pek = (cPE, cPE.n)
                        P.do("scalar",
                             lambda e, ei=ei, h=half, p=ps: e.activation(
                                 B[2][:, (ei * 2 + h) * 512:
                                      (ei * 2 + h + 1) * 512],
                                 PS[p][:], AF.Relu),
                             waits=[pek, self._hswar.get((ei, h))], incs=[cS])
                        self.ps_read(ps, cS)
                        P.do("vector",
                             lambda e, ei=ei, h=half: e.tensor_mul(
                                 B[2][:, (ei * 2 + h) * 512:
                                      (ei * 2 + h + 1) * 512],
                                 B[2][:, (ei * 2 + h) * 512:
                                      (ei * 2 + h + 1) * 512],
                                 B[4][:, ei * 512:(ei + 1) * 512]),
                             waits=[(cS, cS.n)], incs=[cV])
                    self._wbwar[ei] = (cV, cV.n)
                hall = (cV, cV.n)
                for oc in range(C):
                    ps = 2 + oc % 2
                    kk = 0
                    for ei in range(4):
                        wt, slot = pair[ei // 2], ei % 2
                        for hc in range(2):
                            P.do("tensor",
                                 lambda e, ei=ei, hc=hc, oc=oc, wt=wt,
                                 slot=slot, p=ps, f=(kk == 0), l=(kk == 7):
                                 e.matmul(
                                     PS[p][:],
                                     vv_of(wt, slot)[:, hc,
                                                     oc * 128:(oc + 1) * 128],
                                     B[2][:, (ei * 2 + hc) * 512:
                                          (ei * 2 + hc + 1) * 512],
                                     start=f, stop=l),
                                 waits=[hall, self.ps_free(ps)]
                                 if kk == 0 else [],
                                 incs=[cPE] if kk == 7 else [])
                            kk += 1
                    pev = (cPE, cPE.n)
                    col = oc * W
                    if pss == 0:
                        P.do("vector",
                             lambda e, col=col, p=ps: e.tensor_copy(
                                 F[1][:, col:col + W], PS[p][:]),
                             waits=[pev] + ([(cSP, self._ffwar)
                                             if self._ffwar else None]
                                            if oc == 0 else []),
                             incs=[cV])
                    else:
                        src_acc = F[1] if pss in (1, 3) else F[0]
                        dst_acc = F[0] if pss in (1, 3) else F[1]
                        P.do("vector",
                             lambda e, col=col, p=ps, s=src_acc, d=dst_acc:
                             e.tensor_add(d[:, col:col + W], PS[p][:],
                                          s[:, col:col + W]),
                             waits=[pev], incs=[cV])
                    self.ps_read(ps, cV)
                vend = (cPE, cPE.n)
                for ei in range(4):
                    self._hswar[(ei, 0)] = vend
                    self._hswar[(ei, 1)] = vend
                self._pairwar[pss % 2] = vend
            rNf.consumed(tc, cPE)
            P.do("vector",
                 lambda e: e.tensor_scalar_mul(F[1][:], F[0][:], VD[:, 56:57]),
                 waits=[(cV, cV.n)], incs=[cV])
            if getattr(self, "dbg_l3", None) is not None:
                mdv = self.dbg_l3["moe"].rearrange("(c p) t -> p c t", p=128)
                P.dma("gpsimd", mdv[:, :, tc * W:(tc + 1) * W],
                      F[1][:].rearrange("p (c w) -> p c w", w=W),
                      waits=[(cV, cV.n)], ctr=cSP)
            else:
                P.dma("gpsimd", xdv[:, :, tc * W:(tc + 1) * W],
                      F[1][:].rearrange("p (c w) -> p c w", w=W),
                      waits=[(cV, cV.n)], ctr=cSP, accum_op=ALU.add)
            self._ffwar = cSP.n

    def _psbf(self, i):
        return self.PS[i][:].bitcast(BF16)[:, 0:1024]

    def emit_pair_bound(self, ldp, scale):
        P = self.P
        cPE, cV, cS = self.cPE, self.cV, self.cS
        B, PR, PS, VD = self.B, self.PR, self.PS, self.VD
        onesb = self.CN["c_ones_col_bf"]
        onesf = self.CN["c_ones_row_f"]
        for (bsrc, col) in ((B[0], 52), (B[3], 53)):
            P.do("scalar", lambda e, s=bsrc: e.activation(B[2][:], s[:],
                                                          AF.Square),
                 waits=[ldp, (cV, cV.n)], incs=[cS])
            sq = (cS, cS.n)
            for i in range(NT):
                pri = PR[i % 2]
                P.do("tensor",
                     lambda e, i=i, pri=pri: e.matmul(
                         pri[0:1, 0:W], onesb[:],
                         B[2][:, i * W:(i + 1) * W], start=True, stop=True),
                     waits=[sq, self.ps_free(10 + i % 2)], incs=[cPE])
                pe = (cPE, cPE.n)
                P.do("vector",
                     lambda e, i=i, pri=pri: e.reduce_max(
                         VD[0:1, 44 + i:45 + i], pri[0:1, 0:W], AX.X),
                     waits=[pe], incs=[cV])
                self.ps_read(10 + i % 2, cV)
            P.do("vector",
                 lambda e, col=col: e.reduce_max(VD[0:1, col:col + 1],
                                                 VD[0:1, 44:52], AX.X),
                 waits=[(cV, cV.n)], incs=[cV])
        P.do("vector", lambda e: e.tensor_mul(VD[0:1, 52:53], VD[0:1, 52:53],
                                              VD[0:1, 53:54]),
             waits=[(cV, cV.n)], incs=[cV])
        P.do("scalar", lambda e: e.activation(VD[0:1, 53:54], VD[0:1, 52:53],
                                              AF.Sqrt),
             waits=[(cV, cV.n)], incs=[cS])
        P.do("vector", lambda e, scale=scale: e.tensor_scalar_mul(
            VD[0:1, 52:53], VD[0:1, 53:54], -scale),
             waits=[(cS, cS.n)], incs=[cV])
        P.do("tensor", lambda e: e.matmul(PS[5][:, 0:1], onesf[:],
                                          VD[0:1, 52:53], start=True,
                                          stop=True),
             waits=[(cV, cV.n), self.ps_free(5)], incs=[cPE])
        P.do("scalar", lambda e: e.activation(VD[:, 54:55], PS[5][:, 0:1],
                                              AF.Copy),
             waits=[(cPE, cPE.n)], incs=[cS])
        self.ps_read(5, cS)

    def emit_att_finalize(self, ov, h, qg, corr=None, drows=64, den64=False):
        P = self.P
        cPE, cV, cS, cSP = self.cPE, self.cV, self.cS, self.cSP
        F, B, PS, PR = self.F, self.B, self.PS, self.PR
        onesf = self.CN["c_ones_row_f"]
        denpe = (cPE, cPE.n)
        if den64:
            P.do("vector",
                 lambda e: e.reciprocal(F[2][64:65, 1024:1024 + W],
                                        PS[2][64:65, 0:W]),
                 waits=[denpe, (cV, cV.n)], incs=[cV])
            P.do("tensor",
                 lambda e: e.matmul(PS[5][:],
                                    self.CN["c_onesf_all"][64:65, 0:128],
                                    F[2][64:65, 1024:1024 + W],
                                    start=True, stop=True),
                 waits=[(cV, cV.n), self.ps_free(5)], incs=[cPE])
            P.do("scalar",
                 lambda e: e.activation(F[3][:, 0:W], PS[5][:], AF.Copy),
                 waits=[(cPE, cPE.n)], incs=[cS])
            self.ps_read(5, cS)
            dn = (cS, cS.n)
            P.do("vector",
                 lambda e, dr=drows: e.tensor_mul(B[4][0:dr, 3584:3584 + W],
                                                  PS[2][0:dr, :],
                                                  F[3][0:dr, 0:W]),
                 waits=[dn, self._owar], incs=[cV])
            self.ps_read(2, cV)
            P.dma("gpsimd", ov[h][:, qg * W:(qg + 1) * W],
                  B[4][0:drows, 3584:3584 + W],
                  waits=[(cV, cV.n)], ctr=cSP)
            self._owar = (cSP, cSP.n)
            return
        P.do("scalar",
             lambda e: e.activation(F[2][0:1, 0:W], PR[0][0:1, 0:W], AF.Copy),
             waits=[denpe], incs=[cS])
        self.ps_read(10, cS)
        if corr is not None:
            corr()
        P.do("vector",
             lambda e: e.reciprocal(F[2][0:1, 1024:1024 + W], F[2][0:1, 0:W]),
             waits=[(cS, cS.n), (cV, cV.n)], incs=[cV])
        P.do("tensor",
             lambda e: e.matmul(PS[5][:], onesf[:], F[2][0:1, 1024:1024 + W],
                                start=True, stop=True),
             waits=[(cV, cV.n), self.ps_free(5)], incs=[cPE])
        P.do("scalar",
             lambda e: e.activation(F[3][:, 0:W], PS[5][:], AF.Copy),
             waits=[(cPE, cPE.n)], incs=[cS])
        self.ps_read(5, cS)
        dn = (cS, cS.n)
        P.do("vector",
             lambda e, dr=drows: e.tensor_mul(B[4][0:dr, 3584:3584 + W],
                                              PS[2][0:dr, :], F[3][0:dr, 0:W]),
             waits=[dn, self._owar], incs=[cV])
        self.ps_read(2, cV)
        self._fin_extra(dn)
        P.dma("gpsimd", ov[h][:, qg * W:(qg + 1) * W],
              B[4][0:drows, 3584:3584 + W],
              waits=[(cV, cV.n)], ctr=cSP)
        self._owar = (cSP, cSP.n)

    def _fin_extra(self, dn):
        pass

    def _softmax16(self, raw, tmp, bdst, vd_col):
        P, cV, cS = self.P, self.cV, self.cS
        VD = self.VD
        P.do("vector", lambda e: e.reduce_max(VD[0:16, vd_col:vd_col + 1],
                                              raw[0:16, :], AX.X, negate=True),
             waits=[(cS, cS.n)], incs=[cV])
        P.do("scalar", lambda e: e.activation(tmp[0:16, :], raw[0:16, :], AF.Exp,
                                              bias=VD[0:16, vd_col:vd_col + 1],
                                              accum_out=VD[0:16, vd_col + 1:vd_col + 2]),
             waits=[(cV, cV.n)], incs=[cS])
        P.do("vector", lambda e: e.reciprocal(VD[0:16, vd_col + 1:vd_col + 2],
                                              VD[0:16, vd_col + 1:vd_col + 2]),
             waits=[(cS, cS.n)], incs=[cV])
        P.do("vector", lambda e: e.tensor_scalar_mul(
            bdst[0:16, :], tmp[0:16, :], VD[0:16, vd_col + 1:vd_col + 2]),
             waits=[(cV, cV.n)], incs=[cV])
        self._smx = (cV, cV.n)

    def _pool16(self, wbf, sp, rN, PQ, col):
        P, cL, cPE, cV, cS = self.P, self.cL, self.cPE, self.cV, self.cS
        selv = self.CN["c_sel"]
        PS = self.PS
        parts = self.F[2]
        for tc in range(NT):
            nb, war = rN.get(tc)
            P.dma("sync", nb[:].rearrange("p (c w) -> p c w", w=W),
                  sp[tc].rearrange("c p w -> p c w"),
                  waits=[war], ctr=cL)
            ld = (cL, cL.n)
            for c in range(C):
                ps = 2 + (tc * C + c) % 2
                P.do("tensor",
                     lambda e, c=c, t=tc, p=ps: e.matmul(
                         PS[p][:, 0:W], selv[:, c * 128:(c + 1) * 128],
                         wbf[0:16, t * W:(t + 1) * W], start=True, stop=True),
                     waits=[self._smx, self.ps_free(ps)], incs=[cPE])
                pe = (cPE, cPE.n)
                P.do("vector",
                     lambda e, p=ps, c=c, t=tc, n=nb: e.scalar_tensor_tensor(
                         self.B[2][:, 0:W], PS[p][:, 0:W], 1.0,
                         n[:, c * W:(c + 1) * W], ALU.mult, ALU.mult,
                         accum_out=parts[:, (c * NT + t):(c * NT + t + 1)]),
                     waits=[pe, ld,
                            self._b2war if (tc, c) == (0, 0) else None],
                     incs=[cV])
                self.ps_read(ps, cV)
            rN.consumed(tc, cV)
        vv = (cV, cV.n)
        for c in range(C):
            P.do("vector",
                 lambda e, c=c: e.tensor_reduce(
                     PQ[:, col + c:col + c + 1],
                     parts[:, c * NT:(c + 1) * NT], AX.X, ALU.add),
                 waits=[vv] if c == 0 else [], incs=[cV])
        return (self.cV, self.cV.n)

    def finalize(self, dbg=()):
        P = self.P
        allw = [(c, c.n) for c in (self.cW, self.cL, self.cSP, self.cPE, self.cV,
                                   self.cS, self.cG, self.cXW) if c.n > 0]
        cD = P.ctr("cD")
        for name, src_ap, shape, dt in dbg:
            out = self.nc.dram_tensor("dbg_" + name, list(shape), dt,
                                      kind="ExternalOutput").ap()
            P.dma("sync", out, src_ap, waits=allw, ctr=cD)
        if cD.n:
            allw = allw + [(cD, cD.n)]
        P.do("sync", lambda e: e.nop(), waits=allw)
        P.emit()
        return self.nc


WSHAPES = {
    0: {"L0Wq": (D, D), "L0Wk": (D, D), "L0Wt": (D, D), "L0Wqa": (D, 16),
        "L0Wka": (D, 16), "L0WrT": (D, D), "L0WkT": (D, D), "L0WvT": (D, D)},
    1: {"L1WqT": (D, D), "L1WkT": (D, D), "L1WvT": (D, D), "L1WoT": (D, D),
        "L1U1": (D, 256), "L1V1": (256, 4096), "L1U2": (4096, 256),
        "L1V2": (256, D)},
    2: {"L2qkvT": (D, 3 * D), "L2projT": (D, D), "L2sgnT": (D, D)},
    3: {"L3waq": (D, D), "L3wak": (D, D), "L3wav": (D, D), "L3wao": (D, D),
        "L3sel": (D, 16), "L3keys": (16, D, 256), "L3vals": (16, 256, D)},
}


def emit_barrier(em):
    P = em.P
    allw = [(c, c.n) for c in (em.cW, em.cL, em.cSP, em.cPE, em.cV,
                               em.cS, em.cG, em.cXW) if c.n > 0]
    for eng in ("tensor", "vector", "scalar", "gpsimd", "sync"):
        P.do(eng, lambda e: e.nop(), waits=allw)


def build_program(layers, dbg=False):
    em = Emitter(layers)
    src = em.x_in
    for i, l in enumerate(layers):
        dst = em.x_out if i == len(layers) - 1 else em.xbuf[i % 2]
        if i > 0:
            emit_barrier(em)
        em.load_layer_consts(l)
        if l == 0:
            em.emit_l0(src, dst)
        elif l == 1:
            em.emit_l1(src, dst)
        elif l == 2:
            em.emit_l2(src, dst)
        elif l == 3:
            em.emit_l3(src, dst)
        src = dst
    taps = []
    if dbg and 0 in layers:
        taps = [("xn", em.xn_h, [NT, C, 128, W], BF16),
                ("sp1", em.sp1, [NT, C, 128, W], BF16),
                ("sp2", em.sp2, [NT, C, 128, W], BF16),
                ("sp3", em.sp3, [NT, C, 128, W], BF16),
                ("sp4", em.sp4, [NT, C, 128, W], BF16),
                ("pq", em.PQ[:], [128, 128], F32)]
    elif dbg:
        taps = [("xn", em.xn_h, [NT, C, 128, W], BF16),
                ("q", em.q_hb, [D, T], BF16),
                ("k", em.k_hb, [D, T], BF16),
                ("v", em.v_hb, [16, 128, 32, 64], BF16),
                ("o", em.o_hb, [D, T], BF16)]
    return em.finalize(taps)


def _np_fm(a2d):
    return np.ascontiguousarray(np.asarray(a2d, np.float32).T)


def _pack_vecs(g):
    v = np.zeros((128, 128), np.float32)

    def put(col, arr):
        a = np.asarray(arr, np.float32).reshape(-1, 128)
        v[:, col:col + a.shape[0]] = a.T

    put(0, g["L0_attn_bq"])
    put(8, g["L0_attn_bk"])
    put(16, g["L0_attn_bt"])
    put(24, g["L0_ffn_mu_r"])
    put(32, g["L0_ffn_mu_k"])
    v[0:16, 40] = np.asarray(g["L0_attn_bqa"], np.float32)
    v[0:16, 41] = np.asarray(g["L0_attn_bka"], np.float32)
    v[:, 60] = 0.125
    v[:, 61] = -0.5
    v[:, 62] = 1e-5
    v[:, 63] = 1.0 / 128.0
    put(64, g["L1_attn_bq"]); put(72, g["L1_attn_bk"])
    put(80, g["L1_attn_bv"]); put(88, g["L1_attn_bo"])
    put(96, g["L1_ffn_b1"]); put(42, g["L1_ffn_b2"])
    put(50, g["L2_attn_proj_b"])
    v[0, 58] = np.asarray(g["L3_ffn_route_scale"], np.float32).reshape(-1)[0]
    return v


def host_prep(inputs):
    g = {k: np.asarray(v) for k, v in inputs.items()}
    base = {"vecs": _pack_vecs(g)}
    t = lambda a: np.ascontiguousarray(np.asarray(a, np.float32))
    tt = lambda a: np.ascontiguousarray(np.asarray(a, np.float32).T)
    base.update({
        "L0Wq": t(g["L0_attn_Wq"]), "L0Wk": t(g["L0_attn_Wk"]),
        "L0Wt": t(g["L0_attn_Wt"]), "L0Wqa": t(g["L0_attn_Wqa"]),
        "L0Wka": t(g["L0_attn_Wka"]), "L0WrT": tt(g["L0_ffn_W_r"]),
        "L0WkT": tt(g["L0_ffn_W_k"]), "L0WvT": tt(g["L0_ffn_W_v"]),
        "L1WqT": tt(g["L1_attn_Wq"]), "L1WkT": tt(g["L1_attn_Wk"]),
        "L1WvT": tt(g["L1_attn_Wv"]), "L1WoT": tt(g["L1_attn_Wo"]),
        "L1U1": t(g["L1_ffn_U1"]), "L1V1": t(g["L1_ffn_V1"]),
        "L1U2": t(g["L1_ffn_U2"]), "L1V2": t(g["L1_ffn_V2"]),
        "L2qkvT": tt(g["L2_attn_qkv_w"]), "L2projT": tt(g["L2_attn_proj_w"]),
        "L2sgnT": tt(g["L2_ffn_weight"]),
        "L3waq": t(g["L3_attn_w_aq"].reshape(D, D)),
        "L3wak": t(g["L3_attn_w_ak"].reshape(D, D)),
        "L3wav": t(g["L3_attn_w_av"].reshape(D, D)),
        "L3wao": t(g["L3_attn_w_ao"].reshape(D, D)),
        "L3sel": tt(g["L3_ffn_expert_sel"]),
        "L3keys": t(g["L3_ffn_keys"]), "L3vals": t(g["L3_ffn_values"]),
    })
    for n, a in _CONSTS.items():
        base[n] = np.ascontiguousarray(a)
    x = np.asarray(g["L0_attn_hidden_states"], np.float32)
    return [dict(base, x_in=_np_fm(x[b])) for b in range(8)]


def _filter_inputs(ncp, m):
    names = set()
    for mls_name in list(m.keys()):
        names.add(mls_name)
    return m


def kernel(**inputs):
    in_maps = host_prep(inputs)
    x_cur = None
    for layers in GROUPS:
        ncp = build_program(layers)
        need = {"x_in", "vecs"} | set(WSHAPES_KEYS(layers)) | set(consts_for(layers))
        maps = []
        for b in range(8):
            m = {k: v for k, v in in_maps[b].items() if k in need}
            if x_cur is not None:
                m["x_in"] = x_cur[b]
            maps.append(m)
        res = run_bass_kernel_spmd(ncp, maps, core_ids=list(range(8)))
        x_cur = [np.ascontiguousarray(np.asarray(res.results[b]["x_out"],
                                                 np.float32)) for b in range(8)]
    out = np.stack([x.T for x in x_cur], axis=0)
    return np.ascontiguousarray(out, np.float32)


def WSHAPES_KEYS(layers):
    ks = []
    for l in layers:
        ks += list(WSHAPES[l].keys())
    return ks


# version 13 (final)
# speedup vs baseline: 1.1789x; 1.1789x over previous
import numpy as np
import ml_dtypes
import concourse.bass as bass
import concourse.mybir as mybir
from concourse.bass_utils import run_bass_kernel_spmd

F32 = mybir.dt.float32
BF16 = mybir.dt.bfloat16
AF = mybir.ActivationFunctionType
ALU = mybir.AluOpType
AX = mybir.AxisListType

bf = ml_dtypes.bfloat16
D = 1024
T = 4096
C = 8
NT = 8
W = 512
NEG = -1.0e30

GROUPS = [[0, 1, 2, 3]]


class Ctr:
    __slots__ = ("sem", "n", "name")

    def __init__(self, sem, name):
        self.sem, self.n, self.name = sem, 0, name


class Prog:
    ENGINES = ("tensor", "vector", "scalar", "gpsimd", "sync")

    def __init__(self, nc):
        self.nc = nc
        self.items = {e: [] for e in self.ENGINES}

    def ctr(self, name):
        return Ctr(self.nc.alloc_semaphore(name), name)

    def do(self, eng, fn, waits=(), incs=()):
        wl = []
        for w in waits:
            if w is None:
                continue
            if isinstance(w, Ctr):
                wl.append((w.sem, w.n))
            else:
                c, v = w
                assert v <= c.n, f"future wait {c.name}: {v} > {c.n}"
                if v > 0:
                    wl.append((c.sem, v))
        il = []
        for inc in incs:
            c, amt = (inc, 1) if isinstance(inc, Ctr) else inc
            c.n += amt
            il.append((c.sem, amt))
        self.items[eng].append((wl, fn, il))

    def dma(self, queue, out, in_, waits=(), ctr=None, **kw):
        incs = [(ctr, 16)] if ctr is not None else []
        self.do(queue, lambda e: e.dma_start(out=out, in_=in_, **kw),
                waits=waits, incs=incs)

    def emit(self):
        with self.nc.Block() as block:
            for name in self.ENGINES:
                items = self.items[name]
                if not items:
                    continue

                def body(eng, items=items):
                    for wl, fn, il in items:
                        for sem, val in wl:
                            eng.wait_ge(sem, val)
                        inst = fn(eng)
                        for sem, amt in il:
                            inst = inst.then_inc(sem, amt)

                getattr(block, name)(body)


class Rot:

    def __init__(self, aps):
        self.aps = list(aps)
        self.cons = [None] * len(self.aps)

    def get(self, i):
        n = len(self.aps)
        return self.aps[i % n], self.cons[i % n]

    def consumed(self, i, ctr):
        self.cons[i % len(self.aps)] = (ctr, ctr.n)


def host_consts():
    cst = {}
    cst["c_ident"] = np.eye(128, dtype=bf)
    cst["c_ones_col_bf"] = np.ones((128, 1), dtype=bf)
    cst["c_ones_row_f"] = np.ones((1, 128), dtype=np.float32)
    cst["c_ones_row_bf"] = np.ones((1, 128), dtype=bf)
    e2 = np.zeros((128, 2), dtype=bf)
    e2[:64, 0] = 1
    e2[64:, 1] = 1
    cst["c_e2"] = e2
    sel = np.zeros((16, D), dtype=bf)
    for c in range(C):
        sel[2 * c, c * 128:c * 128 + 64] = 1
        sel[2 * c + 1, c * 128 + 64:(c + 1) * 128] = 1
    cst["c_sel"] = sel
    half = 64
    freqs = np.power(10000.0, -np.arange(half, dtype=np.float64) / half)
    rad = freqs[:, None] * np.arange(T, dtype=np.float64)[None, :]
    cos, sin = np.cos(rad), np.sin(rad)
    cst["c_cc"] = np.concatenate([cos, cos], axis=0).astype(bf)
    cst["c_ss"] = np.concatenate([-sin, sin], axis=0).astype(bf)
    swp = np.zeros((128, 128), np.float32)
    for i in range(128):
        swp[i, (i + 64) % 128] = 1
    cst["c_swp"] = swp.astype(bf)
    cst["c_identf"] = np.eye(128, dtype=np.float32)
    cst["c_onesf_all"] = np.ones((128, 128), dtype=np.float32)
    jj, qq = np.arange(128)[:, None], np.arange(128)[None, :]
    tri01 = np.where(jj <= qq, 1.0, 0.0)
    ctri = []
    for i in range(4):
        t = np.zeros((128, 512))
        for b in range(4):
            if b < i:
                t[:, b * 128:(b + 1) * 128] = 0.0
            elif b == i:
                t[:, b * 128:(b + 1) * 128] = tri01
            else:
                t[:, b * 128:(b + 1) * 128] = 1.0
        ctri.append(t)
    cst["c_ctri"] = np.concatenate(ctri, axis=1).astype(np.float32).astype(bf)
    offs = [-128, 0, 128, 256, 384, 512]
    j, q = np.arange(128)[:, None], np.arange(512)[None, :]
    bl = [(np.abs(off + j - q) <= 32).astype(bf) for off in offs]
    g = (np.abs(j - q) <= 32)
    g[:4, :] = True
    g[:, :4] = False
    bl.append(g.astype(bf))
    cst["c_band"] = np.ascontiguousarray(np.concatenate(bl, axis=1))
    cst["c_i4"] = np.eye(4, dtype=bf)
    return cst


_CONSTS = host_consts()
ARENA = {"c_band": (128, 3584), "c_ctri": (128, 2048), "c_sel": (16, 1024)}
CONST_NEED = {
    None: ["c_ones_col_bf", "c_ones_row_f", "c_ones_row_bf"],
    0: ["c_sel"],
    1: ["c_band", "c_i4", "c_e2", "c_onesf_all"],
    2: ["c_ident", "c_e2", "c_onesf_all"],
    3: ["c_cc", "c_ss", "c_swp", "c_ctri", "c_ident", "c_identf"],
}


def consts_for(layers):
    ks = list(CONST_NEED[None])
    for l in layers:
        ks += CONST_NEED[l]
    return {k: _CONSTS[k] for k in ks}


class Emitter:
    def __init__(self, layers):
        self.layers = layers
        nc = self.nc = bass.Bass("TRN2", target_bir_lowering=False)
        self.P = Prog(nc)
        EI, EO, IN = "ExternalInput", "ExternalOutput", "Internal"
        dram = lambda n, s, dt=F32, kind=IN: nc.dram_tensor(n, s, dt, kind=kind).ap()
        self.x_in = dram("x_in", [D, T], F32, EI)
        self.x_out = dram("x_out", [D, T], F32, EO)
        self.xbuf = [dram("xbufA", [D, T]), dram("xbufB", [D, T])]
        self.wd = {}
        for l in layers:
            for n, s in WSHAPES[l].items():
                self.wd[n] = dram(n, list(s), F32, EI)
        self.vecs = dram("vecs", [128, 128], F32, EI)
        self.consts = consts_for(layers)
        self.cd = {}
        for n, a in self.consts.items():
            self.cd[n] = dram(n, list(a.shape), BF16 if a.dtype == bf else F32, EI)
        self.xn_h = dram("xn_h", [NT, C, 128, W], BF16)
        self.sp1 = dram("sp1_h", [NT, C, 128, W], BF16)
        self.sp2 = dram("sp2_h", [NT, C, 128, W], BF16)
        self.sp3 = dram("sp3_h", [NT, C, 128, W], BF16)
        self.sp4 = dram("sp4_h", [NT, C, 128, W], BF16)
        self.q_hb = dram("q_hb", [D, T], BF16)
        self.k_hb = dram("k_hb", [D, T], BF16)
        self.kz_hb = dram("kz_hb", [2, D, T], BF16)
        self.v_hb = dram("v_hb", [16, 128, 32, 64], BF16)
        self.v_hb3 = dram("v_hb3", [8, 128, 32, 128], BF16)
        self.o_hb = dram("o_hb", [D, T], BF16)
        self.m_hb = dram("m_hb", [128, 128], F32)
        self.mT_hb = dram("mT_hb", [32, 512], F32)
        a = nc.alloc_sbuf_tensor
        self.F = [a(f"F{i}", [128, T], F32) for i in range(4)]
        nb_tiles = 6
        self.B = [a(f"B{i}", [128, T], BF16) for i in range(nb_tiles)]
        self.Wt = [a(f"W{i}", [128, 8192], BF16) for i in range(4)]
        self.VE = a("VE", [128, 128], F32)
        self.VD = a("VD", [128, 64], F32)
        self.PQ = a("PQ", [128, 128], F32) if 0 in layers else None
        self.XM = a("XM", [128, 256], F32) if 2 in layers else None
        self.SM = a("SM", [128, 512], BF16)
        self.ST = a("ST", [65, T], F32)
        self.CN = {}
        arena_need = [n for n in self.consts if n in ARENA]
        if arena_need:
            acols = max(ARENA[n][1] for n in arena_need)
            self.CA = a("const_arena", [128, acols], BF16)
            for n in arena_need:
                p, c0 = ARENA[n]
                self.CN[n] = self.CA[0:p, 0:c0]
        self.CCS = a("cs_ccss", [128, 2 * W], BF16) if "c_cc" in self.consts else None
        for n, arr in self.consts.items():
            if n in ARENA or n in ("c_cc", "c_ss"):
                continue
            self.CN[n] = a("s_" + n, list(arr.shape), BF16 if arr.dtype == bf else F32)
        self.PS = [nc.alloc_psum_tensor(f"ps{i}", [128, 512], F32) for i in range(6)]
        self.PR = [nc.alloc_psum_tensor(f"pr{i}", [16, 512], F32) for i in range(2)]
        self.psmark = {}
        P = self.P
        self.cW = P.ctr("cW")
        self.cL = P.ctr("cL")
        self.cSP = P.ctr("cSP")
        self.cPE = P.ctr("cPE")
        self.cV = P.ctr("cV")
        self.cS = P.ctr("cS")
        self.cG = P.ctr("cG")
        self.cXW = P.ctr("cXW")
        P.dma("sync", self.VE[:], self.vecs, ctr=self.cL)
        for n in self.CN:
            if n in ARENA:
                continue
            P.dma("sync", self.CN[n][:], self.cd[n], ctr=self.cL)
        self.consts_ld = (self.cL, self.cL.n)

    def load_layer_consts(self, l):
        P = self.P
        for n in CONST_NEED.get(l, []):
            if n in ARENA and n in self.CN:
                P.dma("sync", self.CN[n], self.cd[n], ctr=self.cL)
        self.consts_ld = (self.cL, self.cL.n)

    def wv8(self, wt):
        return wt[:].rearrange("p (c o) -> p c o", o=1024)

    def wvN(self, wt, o, ncols=None):
        v = wt[:] if ncols is None else wt[:, 0:ncols]
        return v.rearrange("p (c o) -> p c o", o=o)

    def wload(self, dst, name, waits=()):
        src = self.wd[name].rearrange("(c p) o -> p c o", p=128)
        self.P.dma("gpsimd", dst, src, waits=waits, ctr=self.cW)
        return (self.cW, self.cW.n)

    def ps_free(self, i):
        return self.psmark.get(i)

    def ps_read(self, i, ctr):
        self.psmark[i] = (ctr, ctr.n)

    def mm_proj(self, ps, wtile, oc, nsrc, rhs_w=W, kset=range(C), extra_waits=()):
        P = self.P
        wv = self.wv8(wtile)
        kl = list(kset)
        for ki, c in enumerate(kl):
            waits = []
            if ki == 0:
                waits = list(extra_waits) + [self.ps_free(ps)]
            P.do("tensor",
                 lambda e, c=c, first=(ki == 0), last=(ki == len(kl) - 1):
                 e.matmul(self.PS[ps][:, 0:rhs_w],
                          wv[:, c, oc * 128:(oc + 1) * 128],
                          nsrc(c), start=first, stop=last),
                 waits=waits,
                 incs=[self.cPE] if ki == len(kl) - 1 else [])
        return (self.cPE, self.cPE.n)

    def emit_ln(self, x_src, mixes=False, want_xm=False):
        P, nc = self.P, self.nc
        cL, cPE, cV, cS, cG = self.cL, self.cPE, self.cV, self.cS, self.cG
        F, B, PS, PR, ST = self.F, self.B, self.PS, self.PR, self.ST
        onesb = self.CN["c_ones_col_bf"]
        onesf = self.CN["c_ones_row_f"]
        xv = x_src.rearrange("(c p) t -> p c t", p=128)
        rX = Rot([F[2], F[3]])
        sq_slots = Rot([B[3][:, i * W:(i + 1) * W] for i in range(4)])
        xb_slots = Rot([B[2][:, i * W:(i + 1) * W] for i in range(4)])
        for tc in range(NT):
            xt, war = rX.get(tc)
            P.dma("sync", xt[:].rearrange("p (c w) -> p c w", w=W),
                  xv[:, :, tc * W:(tc + 1) * W],
                  waits=[war, self.consts_ld], ctr=cL)
            ld = (cL, cL.n)
            for c in range(C):
                src = xt[:, c * W:(c + 1) * W]
                xb, war1 = xb_slots.get(tc * C + c)
                sq, war2 = sq_slots.get(tc * C + c)
                P.do("vector", lambda e, s=src, d=xb: e.tensor_copy(d, s),
                     waits=[ld, war1], incs=[cV])
                P.do("scalar", lambda e, s=src, d=sq: e.activation(d, s, AF.Square),
                     waits=[ld, war2], incs=[cS])
                vv, ss = (cV, cV.n), (cS, cS.n)
                P.do("tensor",
                     lambda e, r=xb, f=(c == 0), l=(c == C - 1):
                     e.matmul(PR[0][0:1, :], onesb[:], r, start=f, stop=l),
                     waits=[vv] + ([self.ps_free(10)] if c == 0 else []),
                     incs=[cPE])
                xb_slots.consumed(tc * C + c, cPE)
                P.do("tensor",
                     lambda e, r=sq, f=(c == 0), l=(c == C - 1):
                     e.matmul(PR[1][0:1, :], onesb[:], r, start=f, stop=l),
                     waits=[ss] + ([self.ps_free(11)] if c == 0 else []),
                     incs=[cPE])
                sq_slots.consumed(tc * C + c, cPE)
            pe = (cPE, cPE.n)
            rX.consumed(tc, cPE)
            P.do("scalar",
                 lambda e, t=tc: e.activation(ST[0:1, t * W:(t + 1) * W], PR[0][0:1, :],
                                              AF.Copy),
                 waits=[pe], incs=[cS])
            self.ps_read(10, cS)
            P.do("scalar",
                 lambda e, t=tc: e.activation(ST[32:33, t * W:(t + 1) * W], PR[1][0:1, :],
                                              AF.Copy),
                 waits=[pe], incs=[cS])
            self.ps_read(11, cS)
        sA = (cS, cS.n)
        fa, fb = F[2][0:1, :], F[3][0:1, :]
        P.do("vector", lambda e, fa=fa: e.tensor_scalar_mul(fa, ST[0:1, :], 1.0 / D),
             waits=[sA], incs=[cV])
        mrow = (cV, cV.n)
        for tc in range(NT):
            P.do("tensor",
                 lambda e, t=tc, fa=fa: e.matmul(PS[0][:], onesf[:],
                                                 fa[:, t * W:(t + 1) * W],
                                                 start=True, stop=True),
                 waits=[mrow, self.ps_free(0)], incs=[cPE])
            pe = (cPE, cPE.n)
            P.do("scalar",
                 lambda e, t=tc: e.activation(F[0][:, t * W:(t + 1) * W], PS[0][:],
                                              AF.Copy), waits=[pe], incs=[cS])
            self.ps_read(0, cS)
        bcm = (cPE, cPE.n)
        P.do("scalar", lambda e, fa=fa, fb=fb: e.activation(fb, fa, AF.Square),
             waits=[mrow], incs=[cS])
        P.do("vector", lambda e, fa=fa: e.tensor_scalar_mul(fa, ST[32:33, :], 1.0 / D),
             waits=[(cS, cS.n), bcm], incs=[cV])
        P.do("vector", lambda e, fa=fa, fb=fb: e.tensor_sub(ST[0:1, :], fa, fb),
             incs=[cV])
        P.do("scalar", lambda e, fa=fa: e.activation(fa, ST[0:1, :], AF.Ln,
                                                     bias=self.VE[0:1, 62:63]),
             waits=[(cV, cV.n)], incs=[cS])
        P.do("scalar", lambda e, fa=fa: e.activation(ST[0:1, :], fa, AF.Exp,
                                                     scale=self.VE[0:1, 61:62]),
             incs=[cS])
        sST = (cS, cS.n)
        for tc in range(NT):
            P.do("tensor",
                 lambda e, t=tc: e.matmul(PS[1][:], onesf[:],
                                          ST[0:1, t * W:(t + 1) * W],
                                          start=True, stop=True),
                 waits=[sST, self.ps_free(1)], incs=[cPE])
            pe = (cPE, cPE.n)
            P.do("scalar",
                 lambda e, t=tc: e.activation(F[1][:, t * W:(t + 1) * W], PS[1][:],
                                              AF.Copy), waits=[pe], incs=[cS])
            self.ps_read(1, cS)
        sBC = (cS, cS.n)
        rX.cons = [(self.cS, self.cS.n), (self.cS, self.cS.n)]
        murs = self.VE[:, 24:32]
        muks = self.VE[:, 32:40]
        mur1 = self.VD[:, 8:16]
        muk1 = self.VD[:, 16:24]
        rXc = rX
        rXn = Rot([B[0], B[1]])
        for c in range(C):
            xc, war = rXc.get(c)
            P.dma("sync", xc[:], xv[:, c, :], waits=[war, sBC], ctr=cL)
            ld = (cL, cL.n)
            P.do("vector", lambda e, x=xc: e.tensor_sub(x[:], x[:], F[0][:]),
                 waits=[ld], incs=[cV])
            xnc, war2 = rXn.get(c)
            if want_xm:
                fo = rXc.aps[(c + 1) % 2]
                P.do("vector", lambda e, x=xc, f=fo: e.tensor_mul(f[:], x[:], F[1][:]),
                     waits=[(cV, cV.n), rXc.cons[(c + 1) % 2]], incs=[cV])
                P.do("vector",
                     lambda e, f=fo, c=c: e.tensor_reduce(
                         self.XM[:, c * 32:(c + 1) * 32],
                         f[:].rearrange("p (b w) -> p b w", w=128), AX.X, ALU.add),
                     incs=[cV])
                P.do("vector", lambda e, f=fo, d=xnc: e.tensor_copy(d[:], f[:]),
                     waits=[war2], incs=[cV])
                rXc.consumed(c + 1, cV)
            else:
                P.do("vector", lambda e, x=xc, d=xnc: e.tensor_mul(d[:], x[:], F[1][:]),
                     waits=[(cV, cV.n), war2], incs=[cV])
                rXc.consumed(c, cV)
            vxn = (cV, cV.n)
            P.dma("gpsimd", self.xn_h[:, c].rearrange("t p w -> p t w"),
                  xnc[:].rearrange("p (t w) -> p t w", w=W),
                  waits=[vxn], ctr=self.cSP)
            rXn.consumed(c, self.cSP)
            if mixes:
                for (mu, mu1, spdst, slot) in (
                        (murs[:, c:c + 1], mur1[:, c:c + 1], self.sp1, B[4]),
                        (muks[:, c:c + 1], muk1[:, c:c + 1], self.sp2, B[5])):
                    P.do("vector",
                         lambda e, x=xnc, m=mu, d=B[2]:
                         e.tensor_scalar_mul(d[:], x[:], m),
                         waits=[vxn, (self.cSP, self.cSP.n)], incs=[cV])
                    P.do("vector",
                         lambda e, x=xnc, m=mu1, d=slot:
                         e.scalar_tensor_tensor(d[:, 1:T], x[:, 0:T - 1], m,
                                                B[2][:, 1:T], ALU.mult, ALU.add),
                         waits=[(cV, cV.n)], incs=[cV])
                    P.do("vector",
                         lambda e, d=slot: e.tensor_copy(d[:, 0:1], B[2][:, 0:1]),
                         incs=[cV])
                    P.dma("gpsimd", spdst[:, c].rearrange("t p w -> p t w"),
                          slot[:].rearrange("p (t w) -> p t w", w=W),
                          waits=[(cV, cV.n)], ctr=self.cSP)
        self.lnmark = (self.cSP, self.cSP.n)

    def emit_l0(self, x_src, x_dst):
        P = self.P
        cL, cPE, cV, cS, cSP = self.cL, self.cPE, self.cV, self.cS, self.cSP
        F, B, PS, PR = self.F, self.B, self.PS, self.PR
        ST, VE, VD, PQ, SM = self.ST, self.VE, self.VD, self.PQ, self.SM
        ST2 = F[0]
        SMT = F[1]
        self._b2war = None
        self._f0war = 0
        self._ffwar = 0
        wq_ld = self.wload(self.wv8(self.Wt[0]), "L0Wq", waits=())
        smv = SM[:].rearrange("p (k o) -> p k o", o=16)
        P.dma("gpsimd", SM[:, 0:128].rearrange("p (c o) -> p c o", o=16),
              self.wd["L0Wqa"].rearrange("(c p) o -> p c o", p=128), ctr=self.cW)
        P.dma("gpsimd", SM[:, 128:256].rearrange("p (c o) -> p c o", o=16),
              self.wd["L0Wka"].rearrange("(c p) o -> p c o", p=128), ctr=self.cW)
        w_sm = (self.cW, self.cW.n)
        w1_ld = self.wload(self.wv8(self.Wt[1]), "L0WrT")
        w2_ld = self.wload(self.wv8(self.Wt[2]), "L0WkT")
        w3_ld = self.wload(self.wv8(self.Wt[3]), "L0WvT")
        P.do("vector", lambda e: e.tensor_scalar_mul(VD[0:16, 0:1], VE[0:16, 40:41],
                                                     0.125),
             waits=[self.consts_ld], incs=[cV])
        P.do("vector", lambda e: e.tensor_scalar_mul(VD[0:16, 1:2], VE[0:16, 41:42],
                                                     0.125), incs=[cV])
        P.do("vector", lambda e: e.tensor_scalar(VD[:, 8:16], VE[:, 24:32],
                                                 -1.0, 1.0, ALU.mult, ALU.add),
             incs=[cV])
        P.do("vector", lambda e: e.tensor_scalar(VD[:, 16:24], VE[:, 32:40],
                                                 -1.0, 1.0, ALU.mult, ALU.add),
             incs=[cV])
        self.emit_ln(x_src, mixes=True)
        ln = self.lnmark

        nst = lambda buf: (lambda c: buf[:, c * W:(c + 1) * W])
        rN = Rot([B[0], B[1]])

        for tc in range(NT):
            nb, war = rN.get(tc)
            P.dma("sync", nb[:].rearrange("p (c w) -> p c w", w=W),
                  self.xn_h[tc].rearrange("c p w -> p c w"), waits=[war, ln], ctr=cL)
            ld = (cL, cL.n)
            for oc in range(C):
                ps = oc % 2
                pe = self.mm_proj(ps, self.Wt[0], oc, nst(nb),
                                  extra_waits=[ld, wq_ld])
                P.do("scalar",
                     lambda e, p=ps, o=oc: e.activation(
                         B[2][:, o * W:(o + 1) * W], PS[p][:], AF.Identity,
                         bias=VE[:, o:o + 1]),
                     waits=[pe, (cV, cV.n) if (tc, oc) == (0, 0) else None,
                            self._b2war if oc == 0 else None],
                     incs=[cS])
                self.ps_read(ps, cS)
            sEv = (cS, cS.n)
            for oc in range(C):
                P.do("tensor",
                     lambda e, o=oc, f=(oc == 0), l=(oc == C - 1):
                     e.matmul(PR[0][0:16, 0:W], smv[:, o, :],
                              B[2][:, o * W:(o + 1) * W], start=f, stop=l),
                     waits=[sEv, w_sm, self.ps_free(10)] if oc == 0 else [],
                     incs=[cPE] if oc == C - 1 else [])
            pe = (cPE, cPE.n)
            P.do("scalar",
                 lambda e, t=tc: e.activation(ST2[0:16, t * W:(t + 1) * W],
                                              PR[0][0:16, 0:W], AF.Identity,
                                              scale=VE[0:16, 60:61],
                                              bias=VD[0:16, 0:1]),
                 waits=[pe], incs=[cS])
            self.ps_read(10, cS)
            P.dma("gpsimd", self.sp3[tc].rearrange("c p w -> p c w"),
                  B[2][:].rearrange("p (c w) -> p c w", w=W),
                  waits=[(cS, cS.n)], ctr=cSP)
            self._b2war = (cSP, cSP.n)
            rN.consumed(tc, cPE)
        self._softmax16(ST2, SMT, B[3], vd_col=2)
        pq = self._pool16(B[3], self.sp3, rN, PQ, 0)
        wk_ld = self.wload(self.wv8(self.Wt[0]), "L0Wk", waits=[(cPE, cPE.n)])
        self._b2war = (cSP, cSP.n)
        for tc in range(NT):
            nb, war = rN.get(NT + tc)
            P.dma("sync", nb[:].rearrange("p (c w) -> p c w", w=W),
                  self.xn_h[tc].rearrange("c p w -> p c w"), waits=[war], ctr=cL)
            ld = (cL, cL.n)
            for oc in range(C):
                ps = oc % 2
                pe = self.mm_proj(ps, self.Wt[0], oc, nst(nb),
                                  extra_waits=[ld, wk_ld, pq])
                P.do("vector",
                     lambda e, p=ps, o=oc: e.tensor_scalar(
                         B[2][:, o * W:(o + 1) * W], PS[p][:],
                         VE[:, 8 + o:9 + o], PQ[:, o:o + 1], ALU.add, ALU.mult),
                     waits=[pe, self._b2war if oc == 0 else None], incs=[cV])
                self.ps_read(ps, cV)
            vEv = (cV, cV.n)
            for oc in range(C):
                P.do("tensor",
                     lambda e, o=oc, f=(oc == 0), l=(oc == C - 1):
                     e.matmul(PR[0][0:16, 0:W], smv[:, 8 + o, :],
                              B[2][:, o * W:(o + 1) * W], start=f, stop=l),
                     waits=[vEv, self.ps_free(10)] if oc == 0 else [],
                     incs=[cPE] if oc == C - 1 else [])
            pe = (cPE, cPE.n)
            P.do("scalar",
                 lambda e, t=tc: e.activation(ST2[0:16, t * W:(t + 1) * W],
                                              PR[0][0:16, 0:W], AF.Identity,
                                              scale=VE[0:16, 60:61],
                                              bias=VD[0:16, 1:2]),
                 waits=[pe], incs=[cS])
            self.ps_read(10, cS)
            P.dma("gpsimd", self.sp4[tc].rearrange("c p w -> p c w"),
                  B[2][:].rearrange("p (c w) -> p c w", w=W),
                  waits=[(cS, cS.n)], ctr=cSP)
            self._b2war = (cSP, cSP.n)
            rN.consumed(NT + tc, cPE)
        self._softmax16(ST2, SMT, B[3], vd_col=4)
        pk = self._pool16(B[3], self.sp4, rN, PQ, 8)
        wt_ld = self.wload(self.wv8(self.Wt[0]), "L0Wt", waits=[(cPE, cPE.n)])
        xsv = x_src.rearrange("(c p) t -> p c t", p=128)
        xdv = x_dst.rearrange("(c p) t -> p c t", p=128)
        rX = Rot([F[2], F[3]])
        rX.cons = [(cV, cV.n), (cV, cV.n)]
        wvst = lambda c: B[2][:, c * W:(c + 1) * W]
        for tc in range(NT):
            nb, war = rN.get(tc)
            P.dma("sync", nb[:].rearrange("p (c w) -> p c w", w=W),
                  self.sp3[tc].rearrange("c p w -> p c w"), waits=[war], ctr=cL)
            xt, warx = rX.get(tc)
            P.dma("sync", xt[:].rearrange("p (c w) -> p c w", w=W),
                  xsv[:, :, tc * W:(tc + 1) * W], waits=[warx], ctr=cL)
            ld = (cL, cL.n)
            for c in range(C):
                P.do("vector",
                     lambda e, c=c, n=nb: e.tensor_scalar_mul(
                         wvst(c), n[:, c * W:(c + 1) * W], PQ[:, 8 + c:9 + c]),
                     waits=[ld, pk] + ([self._b2war] if c == 0 else []),
                     incs=[cV])
            vwv = (cV, cV.n)
            self._b2war = None
            for oc in range(C):
                ps = oc % 2
                pe = self.mm_proj(ps, self.Wt[0], oc, wvst, extra_waits=[vwv, wt_ld])
                P.do("vector",
                     lambda e, p=ps, o=oc, n=nb: e.scalar_tensor_tensor(
                         F[0][:, o * W:(o + 1) * W], PS[p][:], VE[:, 16 + o:17 + o],
                         n[:, o * W:(o + 1) * W], ALU.add, ALU.add),
                     waits=[pe, (self.cXW, self._f0war) if tc >= 1 else None],
                     incs=[cV])
                P.do("vector",
                     lambda e, o=oc, x=xt: e.tensor_add(
                         F[1][:, o * W:(o + 1) * W], F[0][:, o * W:(o + 1) * W],
                         x[:, o * W:(o + 1) * W]),
                     incs=[cV])
                self.ps_read(ps, cV)
            rN.consumed(tc, cV)
            rX.consumed(tc, cV)
            P.dma("sync", xdv[:, :, tc * W:(tc + 1) * W],
                  F[1][:].rearrange("p (c w) -> p c w", w=W),
                  waits=[(cV, cV.n)], ctr=self.cXW)
            self._f0war = self.cXW.n
        b6 = (self.cXW, self.cXW.n)
        rA = Rot([B[0], B[1]])
        rB = Rot([B[4], B[5]])
        self._rwar = None
        self._f1war = None
        self._kawar = None
        for tc in range(NT):
            ka = [B[2], B[3]][tc % 2]
            na, wara = rA.get(tc)
            P.dma("sync", na[:].rearrange("p (c w) -> p c w", w=W),
                  self.sp1[tc].rearrange("c p w -> p c w"), waits=[wara], ctr=cL)
            lda = (cL, cL.n)
            nb, warb = rB.get(tc)
            P.dma("sync", nb[:].rearrange("p (c w) -> p c w", w=W),
                  self.sp2[tc].rearrange("c p w -> p c w"), waits=[warb], ctr=cL)
            ldb = (cL, cL.n)
            for oc in range(C):
                ps = oc % 2
                pe = self.mm_proj(ps, self.Wt[1], oc, nst(na),
                                  extra_waits=[lda, w1_ld])
                P.do("scalar",
                     lambda e, p=ps, o=oc: e.activation(
                         F[0][:, o * W:(o + 1) * W], PS[p][:], AF.Sigmoid),
                     waits=[pe, self._rwar if oc == 0 else None], incs=[cS])
                self.ps_read(ps, cS)
            for oc in range(C):
                ps = 2 + oc % 2
                pe = self.mm_proj(ps, self.Wt[2], oc, nst(nb),
                                  extra_waits=[ldb, w2_ld])
                P.do("vector",
                     lambda e, p=ps, o=oc: e.tensor_scalar_max(
                         F[1][:, o * W:(o + 1) * W], PS[p][:], 0.0),
                     waits=[pe, self._f1war if oc == 0 else None], incs=[cV])
                P.do("scalar",
                     lambda e, o=oc, k=ka: e.activation(
                         k[:, o * W:(o + 1) * W], F[1][:, o * W:(o + 1) * W],
                         AF.Square),
                     waits=[(cV, cV.n), self._kawar if oc == 0 else None],
                     incs=[cS])
                self.ps_read(ps, cS)
            kEv = (cS, cS.n)
            self._f1war = (cS, cS.n)
            rA.consumed(tc, cS)
            rB.consumed(tc, cS)
            for oc in range(C):
                ps = 4 + oc % 2
                pe = self.mm_proj(ps, self.Wt[3], oc,
                                  lambda c, k=ka: k[:, c * W:(c + 1) * W],
                                  extra_waits=[kEv, w3_ld])
                P.do("vector",
                     lambda e, p=ps, o=oc: e.tensor_mul(
                         F[2][:, o * W:(o + 1) * W], PS[p][:],
                         F[0][:, o * W:(o + 1) * W]),
                     waits=[pe, (self.cSP, self._ffwar) if tc >= 1 else None],
                     incs=[cV])
                self.ps_read(ps, cV)
            self._rwar = (cV, cV.n)
            self._kawar = (cPE, cPE.n)
            P.dma("gpsimd", xdv[:, :, tc * W:(tc + 1) * W],
                  F[2][:].rearrange("p (c w) -> p c w", w=W),
                  waits=[(cV, cV.n), b6], ctr=cSP, accum_op=ALU.add)
            self._ffwar = cSP.n

    def emit_qkv64(self, wqt, wkt, wvt, wq, wk, wv_, bq0, bk0, ln, war0=None,
                   kz=False):
        P = self.P
        cL, cPE, cV, cS, cSP = self.cL, self.cPE, self.cV, self.cS, self.cSP
        B, PS = self.B, self.PS
        VE = self.VE
        qv = self.q_hb.rearrange("(c p) t -> p c t", p=128)
        kv = self.k_hb.rearrange("(c p) t -> p c t", p=128)
        nst = lambda buf: (lambda c: buf[:, c * W:(c + 1) * W])
        rN = Rot([B[0], B[1]])
        self._qwar = None
        self._kwar = None
        self._vwar = war0
        nst = lambda buf: (lambda c: buf[:, c * W:(c + 1) * W])
        for tc in range(NT):
            nb, war = rN.get(tc)
            P.dma("sync", nb[:].rearrange("p (c w) -> p c w", w=W),
                  self.xn_h[tc].rearrange("c p w -> p c w"),
                  waits=[war, ln], ctr=cL)
            ld = (cL, cL.n)
            for hp in range(C):
                ps = hp % 2
                pe = self.mm_proj(ps, wqt, hp, nst(nb),
                                  extra_waits=[ld, wq])
                P.do("scalar",
                     (lambda e, p=ps, o=hp: e.activation(
                         B[2][:, o * W:(o + 1) * W], PS[p][:], AF.Identity,
                         bias=VE[:, bq0 + o:bq0 + o + 1]))
                     if bq0 is not None else
                     (lambda e, p=ps, o=hp: e.activation(
                         B[2][:, o * W:(o + 1) * W], PS[p][:], AF.Copy)),
                     waits=[pe, self._qwar if hp == 0 else None], incs=[cS])
                self.ps_read(ps, cS)
            P.dma("gpsimd", qv[:, :, tc * W:(tc + 1) * W],
                  B[2][:].rearrange("p (c w) -> p c w", w=W),
                  waits=[(cS, cS.n)], ctr=cSP)
            self._qwar = (cSP, cSP.n)
            for hp in range(C):
                ps = hp % 2
                pe = self.mm_proj(ps, wkt, hp, nst(nb),
                                  extra_waits=[wk])
                P.do("scalar",
                     (lambda e, p=ps, o=hp: e.activation(
                         B[3][:, o * W:(o + 1) * W], PS[p][:], AF.Identity,
                         bias=VE[:, bk0 + o:bk0 + o + 1]))
                     if bk0 is not None else
                     (lambda e, p=ps, o=hp: e.activation(
                         B[3][:, o * W:(o + 1) * W], PS[p][:], AF.Copy)),
                     waits=[pe, self._kwar if hp == 0 else None], incs=[cS])
                self.ps_read(ps, cS)
            P.dma("gpsimd", kv[:, :, tc * W:(tc + 1) * W],
                  B[3][:].rearrange("p (c w) -> p c w", w=W),
                  waits=[(cS, cS.n)], ctr=cSP)
            if kz:
                kz0 = self.kz_hb[0].rearrange("(c p) t -> p c t", p=128)
                kz1 = self.kz_hb[1].rearrange("(c p) t -> p c t", p=128)
                P.dma("gpsimd", kz0[0:64, :, tc * W:(tc + 1) * W],
                      B[3][0:64, :].rearrange("p (c w) -> p c w", w=W),
                      ctr=cSP)
                P.dma("gpsimd", kz1[64:128, :, tc * W:(tc + 1) * W],
                      B[3][64:128, :].rearrange("p (c w) -> p c w", w=W),
                      ctr=cSP)
            self._kwar = (cSP, cSP.n)
            w2v = self.wv8(wvt)
            for hp in range(C):
                for tt in range(4):
                    ps = 2 + (hp * 4 + tt) % 2
                    for c in range(C):
                        P.do("tensor",
                             lambda e, c=c, o=hp, t=tt, p=ps, nb=nb,
                             f=(c == 0), l=(c == C - 1):
                             e.matmul(PS[p][:, 0:128],
                                      nb[:, c * W + t * 128:c * W + (t + 1) * 128],
                                      w2v[:, c, o * 128:(o + 1) * 128],
                                      start=f, stop=l),
                             waits=([wv_, self.ps_free(ps)] +
                                    ([self._vwar] if (hp, tt) == (0, 0) else []))
                             if c == 0 else [],
                             incs=[cPE] if c == C - 1 else [])
                    pe = (cPE, cPE.n)
                    P.do("scalar",
                         lambda e, p=ps, o=hp, t=tt: e.activation(
                             B[4][:, o * W + t * 128:o * W + (t + 1) * 128],
                             PS[p][:, 0:128], AF.Copy),
                         waits=[pe], incs=[cS])
                    self.ps_read(ps, cS)
            vv = (cS, cS.n)
            for h in range(16):
                hp, half = h // 2, h % 2
                src_v = B[4][:, hp * W:(hp + 1) * W].rearrange(
                    "p (tt half d) -> p tt half d", half=2, d=64)[:, :, half, :]
                P.dma("gpsimd", self.v_hb[h][:, tc * 4:(tc + 1) * 4, :], src_v,
                      waits=[vv], ctr=cSP)
            self._vwar = (cSP, cSP.n)
            rN.consumed(tc, cPE)

    def emit_l1(self, x_src, x_dst):
        P = self.P
        cL, cPE, cV, cS, cSP = self.cL, self.cPE, self.cV, self.cS, self.cSP
        F, B, PS, PR = self.F, self.B, self.PS, self.PR
        VE, VD = self.VE, self.VD
        onesb = self.CN["c_ones_col_bf"]
        onesf = self.CN["c_ones_row_f"]
        band = self.CN["c_band"]
        self._pjwar = {}
        self._pmwar = {}
        wq = self.wload(self.wv8(self.Wt[0]), "L1WqT")
        wk = self.wload(self.wv8(self.Wt[1]), "L1WkT")
        wv_ = self.wload(self.wv8(self.Wt[2]), "L1WvT")
        wo = self.wload(self.wv8(self.Wt[3]), "L1WoT")
        self.emit_ln(x_src)
        ln = self.lnmark
        P.do("vector", lambda e: e.tensor_copy(
            self.SM[:, 256:264].bitcast(BF16) if False else self.SM[:, 256:264],
            VE[:, 80:88]), waits=[self.consts_ld], incs=[cV])
        bvb = (cV, cV.n)
        w3v = self.wv8(self.Wt[3])
        for oc in range(C):
            for c in range(C):
                P.do("tensor",
                     lambda e, c=c, o=oc, f=(c == 0), l=(c == C - 1):
                     e.matmul(PR[1][0:1, 0:128].rearrange("o p -> p o") if False
                              else self.PS[5][:, 0:1],
                              w3v[:, c, o * 128:(o + 1) * 128],
                              self.SM[:, 256 + c:257 + c], start=f, stop=l),
                     waits=[bvb, wo, self.ps_free(5)] if c == 0 else [],
                     incs=[cPE] if c == C - 1 else [])
            pe = (cPE, cPE.n)
            P.do("vector",
                 lambda e, o=oc: e.tensor_scalar(VD[:, 32 + o:33 + o],
                                                 self.PS[5][:, 0:1],
                                                 VE[:, 88 + o:89 + o], None,
                                                 ALU.add),
                 waits=[pe], incs=[cV])
            self.ps_read(5, cV)
        nst = lambda buf: (lambda c: buf[:, c * W:(c + 1) * W])
        self.emit_qkv64(self.Wt[0], self.Wt[1], self.Wt[2], wq, wk, wv_,
                        64, 72, ln)
        pmark = (cPE, cPE.n)
        u1v = self.Wt[0][:, 0:2048].rearrange("p (c o) -> p c o", o=256)
        v2v = self.Wt[0][:, 2048:4096].rearrange("p (c o) -> p c o", o=1024)
        v1v = self.Wt[1][:].rearrange("p (c o) -> p c o", o=4096)
        u2v = self.Wt[2][:].rearrange("p (c o) -> p c o", o=256)
        wu1 = self.wload(u1v, "L1U1", waits=[pmark])
        wv2 = self.wload(v2v, "L1V2", waits=[pmark])
        wv1 = self.wload(v1v, "L1V1", waits=[pmark])
        wu2 = self.wload(u2v, "L1U2", waits=[pmark])
        ov = self.o_hb.rearrange("(h d) t -> h d t", d=64)
        self._owar = None
        self._corrwar = None
        P.do("gpsimd", lambda e: e.memset(B[1][:, 0:2080], 1.0),
             waits=[(cPE, cPE.n), (cV, cV.n), (cS, cS.n)], incs=[self.cG])
        P.do("gpsimd", lambda e: e.memset(B[5][:, 0:2080], 1.0),
             incs=[self.cG])
        vones = (self.cG, self.cG.n)
        for pr in range(8):
            pvm = (cPE, cPE.n)
            P.dma("sync", B[0][:], self.q_hb[pr * 128:(pr + 1) * 128, :],
                  waits=[(self._qwar if pr == 0 else None), (cPE, pvm[1])],
                  ctr=cL)
            P.dma("sync", B[3][:], self.k_hb[pr * 128:(pr + 1) * 128, :],
                  waits=[self._kwar if pr == 0 else None], ctr=cL)
            P.dma("sync", B[1][:, 0:2080].rearrange("p (t d) -> p t d", d=65)
                  [:, :, 0:64],
                  self.v_hb[2 * pr],
                  waits=[self._vwar if pr == 0 else None,
                         vones if pr == 0 else None], ctr=cL)
            P.dma("sync", B[5][:, 0:2080].rearrange("p (t d) -> p t d", d=65)
                  [:, :, 0:64],
                  self.v_hb[2 * pr + 1],
                  ctr=cL)
            ldp = (cL, cL.n)
            for (bsrc, col) in ((B[0], 52), (B[3], 53)):
                P.do("scalar", lambda e, s=bsrc: e.activation(B[2][:], s[:],
                                                              AF.Square),
                     waits=[ldp, (cV, cV.n)], incs=[cS])
                sq = (cS, cS.n)
                for i in range(NT):
                    pri = PR[i % 2]
                    P.do("tensor",
                         lambda e, i=i, pri=pri: e.matmul(
                             pri[0:1, 0:W], onesb[:],
                             B[2][:, i * W:(i + 1) * W], start=True, stop=True),
                         waits=[sq, self.ps_free(10 + i % 2)], incs=[cPE])
                    pe = (cPE, cPE.n)
                    P.do("vector",
                         lambda e, i=i, pri=pri, col=col: e.reduce_max(
                             VD[0:1, 44 + i:45 + i], pri[0:1, 0:W], AX.X),
                         waits=[pe], incs=[cV])
                    self.ps_read(10 + i % 2, cV)
                P.do("vector",
                     lambda e, col=col: e.reduce_max(VD[0:1, col:col + 1],
                                                     VD[0:1, 44:52], AX.X),
                     waits=[(cV, cV.n)], incs=[cV])
            P.do("vector", lambda e: e.tensor_mul(VD[0:1, 52:53], VD[0:1, 52:53],
                                                  VD[0:1, 53:54]),
                 waits=[(cV, cV.n)], incs=[cV])
            P.do("scalar", lambda e: e.activation(VD[0:1, 53:54], VD[0:1, 52:53],
                                                  AF.Sqrt),
                 waits=[(cV, cV.n)], incs=[cS])
            P.do("vector", lambda e: e.tensor_scalar_mul(VD[0:1, 52:53],
                                                         VD[0:1, 53:54], -0.125),
                 waits=[(cS, cS.n)], incs=[cV])
            P.do("tensor", lambda e: e.matmul(PS[5][:, 0:1], onesf[:],
                                              VD[0:1, 52:53], start=True,
                                              stop=True),
                 waits=[(cV, cV.n), self.ps_free(5)], incs=[cPE])
            P.do("scalar", lambda e: e.activation(VD[:, 54:55], PS[5][:, 0:1],
                                                  AF.Copy),
                 waits=[(cPE, cPE.n)], incs=[cS])
            self.ps_read(5, cS)
            negM = VD[:, 54:55]
            mdone = (cS, cS.n)
            for hh in range(2):
                h = 2 * pr + hh
                qrows = B[0][hh * 64:(hh + 1) * 64, :]
                krows = B[3][hh * 64:(hh + 1) * 64, :]
                vtile = [B[1], B[5]][hh]
                for qg in range(NT):
                    if qg == 0:
                        jts = [(0, 6), (1, 2), (2, 3), (3, 4), (4, 5)]
                    else:
                        jts = [(jt, [-128, 0, 128, 256, 384, 512].index(
                            jt * 128 - qg * 512))
                            for jt in range(max(0, 4 * qg - 1),
                                            min(32, 4 * qg + 5))]
                    ce = None
                    if qg == 0:
                        for jt in range(32):
                            P.do("tensor",
                                 lambda e, jt=jt, kr=krows, qr=qrows: e.matmul(
                                     PS[3][:, jt * 4:(jt + 1) * 4],
                                     kr[:, jt * 128:(jt + 1) * 128],
                                     qr[:, 0:4], start=True, stop=True),
                                 waits=[mdone, ldp, self.ps_free(3)]
                                 if jt == 0 else [],
                                 incs=[cPE] if jt == 31 else [])
                        pe = (cPE, cPE.n)
                        P.do("scalar",
                             lambda e: e.activation(B[2][:, 2048:2176],
                                                    PS[3][:, 0:128], AF.Exp,
                                                    scale=VE[:, 60:61],
                                                    bias=negM),
                             waits=[pe, self._corrwar], incs=[cS])
                        self.ps_read(3, cS)
                        ce = (cS, cS.n)
                    SCB = (0, 1, 3, 4)
                    pend = []

                    def emit_D(mop, ji, jt, mm, first):
                        P.do("tensor",
                             lambda e, m=mop, jt=jt, vt=vtile, f=first:
                             e.matmul(
                                 PS[2][0:65, :],
                                 vt[:, jt * 65:(jt + 1) * 65], m,
                                 start=f, stop=False),
                             waits=[mm] + ([self.ps_free(2)] if first else []),
                             incs=[cPE])
                        self._pmwar[ji % 4] = (cPE, cPE.n)

                    for ji, (jt, bidx) in enumerate(jts):
                        ps = SCB[ji % 4]
                        P.do("tensor",
                             lambda e, jt=jt, qg=qg, p=ps, kr=krows,
                             qr=qrows: e.matmul(
                                 PS[p][:],
                                 kr[:, jt * 128:(jt + 1) * 128],
                                 qr[:, qg * W:(qg + 1) * W],
                                 start=True, stop=True),
                             waits=[mdone if (hh, qg, ji) == (0, 0, 0) else None,
                                    self.ps_free(ps)],
                             incs=[cPE])
                        pe = (cPE, cPE.n)
                        if len(pend) == 4:
                            a = pend.pop(0)
                            emit_D(*a, first=(a[1] == 0))
                        slot = B[2][:, (ji % 4) * W:(ji % 4 + 1) * W]
                        mslot = B[4][:, (ji % 4) * W:(ji % 4 + 1) * W]
                        P.do("scalar",
                             lambda e, p=ps, s=slot: e.activation(
                                 s, PS[p][:], AF.Exp, scale=VE[:, 60:61],
                                 bias=negM),
                             waits=[pe, self._pjwar.get(ji % 4)], incs=[cS])
                        self.ps_read(ps, cS)
                        P.do("vector",
                             lambda e, s=slot, m=mslot, bidx=bidx:
                             e.tensor_mul(m, s, band[:, bidx * W:(bidx + 1) * W]),
                             waits=[(cS, cS.n), self._pmwar.get(ji % 4)],
                             incs=[cV])
                        self._pjwar[ji % 4] = (cV, cV.n)
                        pend.append((mslot, ji, jt, (cV, cV.n)))
                    for a in pend:
                        emit_D(*a, first=(a[1] == 0))
                    if qg == 0:
                        for jt in range(32):
                            P.do("tensor",
                                 lambda e, jt=jt, vt=vtile:
                                 e.matmul(PS[2][0:65, 0:4],
                                          vt[:, jt * 65:jt * 65 + 65],
                                          B[2][:, 2048 + jt * 4:2052 + jt * 4],
                                          start=False, stop=False),
                                 waits=[ce] if jt == 0 else [],
                                 incs=[cPE] if jt == 31 else [])
                        self._corrwar = (cPE, cPE.n)
                    if qg >= 1:
                        gi = len(jts) % 4
                        ps = SCB[gi]
                        P.do("tensor",
                             lambda e, qg=qg, p=ps, kr=krows, qr=qrows:
                             e.matmul(
                                 PS[p][0:4, :], kr[:, 0:4],
                                 qr[:, qg * W:(qg + 1) * W],
                                 start=True, stop=True),
                             waits=[self.ps_free(ps)], incs=[cPE])
                        pe = (cPE, cPE.n)
                        slot = B[2][:, gi * W:(gi + 1) * W]
                        P.do("scalar",
                             lambda e, p=ps, s=slot: e.activation(
                                 s[0:4, :], PS[p][0:4, :], AF.Exp,
                                 scale=VE[0:4, 60:61], bias=negM[0:4, :]),
                             waits=[pe, self._pjwar.get(gi)],
                             incs=[cS])
                        self.ps_read(ps, cS)
                        ge = (cS, cS.n)
                        P.do("tensor",
                             lambda e, s=slot, vt=vtile: e.matmul(
                                 PS[2][0:65, :], vt[0:4, 0:65], s[0:4, :],
                                 start=False, stop=False),
                             waits=[ge], incs=[cPE])
                        self._pjwar[gi] = (cPE, cPE.n)
                    self.emit_att_finalize(ov, h, qg, den64=True)
        attm = (cSP, cSP.n)
        xsv = x_src.rearrange("(c p) t -> p c t", p=128)
        xdv = x_dst.rearrange("(c p) t -> p c t", p=128)
        ohv = self.o_hb.rearrange("(c p) t -> p c t", p=128)
        rN = Rot([B[0], B[1]])
        rX = Rot([F[2], F[3]])
        rX.cons = [(cV, cV.n), (cV, cV.n)]
        self._f1war = 0
        for tc in range(NT):
            nb, war = rN.get(tc)
            P.dma("sync", nb[:].rearrange("p (c w) -> p c w", w=W),
                  ohv[:, :, tc * W:(tc + 1) * W],
                  waits=[war if war else None, attm if tc == 0 else None],
                  ctr=cL)
            xt, warx = rX.get(tc)
            P.dma("sync", xt[:].rearrange("p (c w) -> p c w", w=W),
                  xsv[:, :, tc * W:(tc + 1) * W], waits=[warx], ctr=cL)
            ld = (cL, cL.n)
            for oc in range(C):
                ps = oc % 2
                pe = self.mm_proj(ps, self.Wt[3], oc, nst(nb),
                                  extra_waits=[ld, wo])
                P.do("vector",
                     lambda e, p=ps, o=oc, x=xt: e.scalar_tensor_tensor(
                         F[1][:, o * W:(o + 1) * W], PS[p][:],
                         VD[:, 32 + o:33 + o], x[:, o * W:(o + 1) * W],
                         ALU.add, ALU.add),
                     waits=[pe, (self.cXW, self._f1war) if tc >= 1 else None],
                     incs=[cV])
                self.ps_read(ps, cV)
            rN.consumed(tc, cV)
            rX.consumed(tc, cV)
            P.dma("sync", xdv[:, :, tc * W:(tc + 1) * W],
                  F[1][:].rearrange("p (c w) -> p c w", w=W),
                  waits=[(cV, cV.n)], ctr=self.cXW)
            self._f1war = self.cXW.n
        prj = (self.cXW, self.cXW.n)
        rN = Rot([B[0], B[1]])
        self._ffwar = 0
        self._hwar = {}
        self._qwar2 = None
        for tc in range(NT):
            nb, war = rN.get(tc)
            P.dma("sync", nb[:].rearrange("p (c w) -> p c w", w=W),
                  self.xn_h[tc].rearrange("c p w -> p c w"),
                  waits=[war], ctr=cL)
            ld = (cL, cL.n)
            for rc in range(2):
                for c in range(C):
                    P.do("tensor",
                         lambda e, c=c, rc=rc, nb=nb, f=(c == 0), l=(c == C - 1):
                         e.matmul(PS[rc][:], u1v[:, c, rc * 128:(rc + 1) * 128],
                                  nb[:, c * W:(c + 1) * W], start=f, stop=l),
                         waits=[ld, wu1, self.ps_free(rc)] if c == 0 else [],
                         incs=[cPE] if c == C - 1 else [])
                pe = (cPE, cPE.n)
                P.do("scalar",
                     lambda e, rc=rc: e.activation(B[2][:, rc * W:(rc + 1) * W],
                                                   PS[rc][:], AF.Copy),
                     waits=[pe, self._qwar2 if rc == 0 else None], incs=[cS])
                self.ps_read(rc, cS)
            pev = (cS, cS.n)
            for fc in range(32):
                psh = 2 + fc % 2
                for rc in range(2):
                    P.do("tensor",
                         lambda e, rc=rc, fc=fc, p=psh, f=(rc == 0), l=(rc == 1):
                         e.matmul(PS[p][:], v1v[:, rc, fc * 128:(fc + 1) * 128],
                                  B[2][:, rc * W:(rc + 1) * W],
                                  start=f, stop=l),
                         waits=[pev, wv1, self.ps_free(psh)] if rc == 0 else [],
                         incs=[cPE] if rc == 1 else [])
                pe = (cPE, cPE.n)
                hslot = B[3][:, (fc % 2) * W:(fc % 2 + 1) * W]
                P.do("scalar",
                     lambda e, p=psh, s=hslot, fc=fc: e.activation(
                         s, PS[p][:], AF.Gelu_apprx_tanh,
                         bias=VE[:, 96 + fc:97 + fc]),
                     waits=[pe, self._hwar.get(fc % 2)], incs=[cS])
                self.ps_read(psh, cS)
                he = (cS, cS.n)
                for rc in range(2):
                    P.do("tensor",
                         lambda e, rc=rc, fc=fc, s=hslot, f=(fc == 0):
                         e.matmul(PS[4 + rc][:],
                                  u2v[:, fc, rc * 128:(rc + 1) * 128], s,
                                  start=f, stop=(fc == 31)),
                         waits=[he, wu2] + ([self.ps_free(4 + rc)]
                                            if fc == 0 else []),
                         incs=[cPE])
                self._hwar[fc % 2] = (cPE, cPE.n)
            qpe = (cPE, cPE.n)
            for rc in range(2):
                P.do("scalar",
                     lambda e, rc=rc: e.activation(
                         B[4][:, rc * W:(rc + 1) * W], PS[4 + rc][:], AF.Copy),
                     waits=[qpe], incs=[cS])
                self.ps_read(4 + rc, cS)
            qev = (cS, cS.n)
            self._qwar2 = qev
            for oc in range(C):
                ps = oc % 2
                for rc in range(2):
                    P.do("tensor",
                         lambda e, rc=rc, oc=oc, p=ps, f=(rc == 0), l=(rc == 1):
                         e.matmul(PS[p][:], v2v[:, rc, oc * 128:(oc + 1) * 128],
                                  B[4][:, rc * W:(rc + 1) * W],
                                  start=f, stop=l),
                         waits=[qev, wv2, self.ps_free(ps)] if rc == 0 else [],
                         incs=[cPE] if rc == 1 else [])
                pe = (cPE, cPE.n)
                P.do("vector",
                     lambda e, p=ps, o=oc: e.tensor_scalar(
                         F[1][:, o * W:(o + 1) * W], PS[p][:],
                         VE[:, 42 + o:43 + o], None, ALU.add),
                     waits=[pe, (cSP, self._ffwar) if tc >= 1 else None,
                            prj if tc == 0 else None],
                     incs=[cV])
                self.ps_read(ps, cV)
            rN.consumed(tc, cV)
            P.dma("gpsimd", xdv[:, :, tc * W:(tc + 1) * W],
                  F[1][:].rearrange("p (c w) -> p c w", w=W),
                  waits=[(cV, cV.n)], ctr=cSP, accum_op=ALU.add)
            self._ffwar = cSP.n

    def emit_l2(self, x_src, x_dst):
        P = self.P
        cL, cPE, cV, cS, cSP = self.cL, self.cPE, self.cV, self.cS, self.cSP
        F, B, PS, PR = self.F, self.B, self.PS, self.PR
        VE, VD, XM, ST = self.VE, self.VD, self.XM, self.ST
        onesb = self.CN["c_ones_col_bf"]
        onesf = self.CN["c_ones_row_f"]
        self._pjwar = {}
        self._pmwar = {}
        qkvT = self.wd["L2qkvT"].rearrange("(c p) o -> p c o", p=128)
        P.do("gpsimd", lambda e: e.memset(B[4][:, 0:T], 0.0),
             incs=[self.cG])
        kzz = (self.cG, self.cG.n)
        kzv0 = self.kz_hb[0].rearrange("(c p) t -> p c t", p=128)
        kzv1 = self.kz_hb[1].rearrange("(c p) t -> p c t", p=128)
        for c in range(C):
            P.dma("gpsimd", kzv0[64:128, c, :], B[4][0:64, 0:T],
                  waits=[kzz] if c == 0 else [], ctr=cSP)
            P.dma("gpsimd", kzv1[0:64, c, :], B[4][0:64, 0:T], ctr=cSP)
        kzfill = (cSP, cSP.n)
        wq = self.cW, self.cW.n
        P.dma("gpsimd", self.wv8(self.Wt[0]), qkvT[:, :, 0:1024], ctr=self.cW)
        wq = (self.cW, self.cW.n)
        P.dma("gpsimd", self.wv8(self.Wt[1]), qkvT[:, :, 1024:2048], ctr=self.cW)
        wk = (self.cW, self.cW.n)
        P.dma("gpsimd", self.wv8(self.Wt[2]), qkvT[:, :, 2048:3072], ctr=self.cW)
        wv_ = (self.cW, self.cW.n)
        wo = self.wload(self.wv8(self.Wt[3]), "L2projT")
        self.emit_ln(x_src, want_xm=True)
        ln = self.lnmark
        xmdone = (cV, cV.n)
        self._fw = [ln, ln]
        for i, (half, colbase) in enumerate(
                [(h, cb) for (h, cb) in ((0, 0), (1, 512)) for _ in range(C)]):
            oc = i % C
            b = i % 2
            P.dma("sync",
                  F[3][:, b * 1024:(b + 1) * 1024]
                  .rearrange("p (c o) -> p c o", o=128),
                  qkvT[:, :, half * 1024 + oc * 128:half * 1024 + (oc + 1) * 128],
                  waits=[self._fw[b]], ctr=cL)
            ld = (cL, cL.n)
            for c in range(C):
                P.do("tensor",
                     lambda e, c=c, b=b, f=(c == 0), l=(c == C - 1): e.matmul(
                         PS[b][:, 0:32],
                         F[3][:, b * 1024 + c * 128:b * 1024 + (c + 1) * 128],
                         XM[:, c * 32:(c + 1) * 32], start=f, stop=l),
                     waits=[ld, xmdone, self.ps_free(b)] if c == 0 else [],
                     incs=[cPE] if c == C - 1 else [])
            pe = (cPE, cPE.n)
            self._fw[b] = pe
            P.do("scalar",
                 lambda e, o=oc, b=b, cb=colbase: e.activation(
                     F[2][:, cb + o * 32:cb + (o + 1) * 32], PS[b][:, 0:32],
                     AF.Copy),
                 waits=[pe], incs=[cS])
            self.ps_read(b, cS)
        qkm = (cS, cS.n)
        for h in range(16):
            r0 = 64 * (h % 2)
            qm_h = F[2][r0:r0 + 64, (h // 2) * 32:(h // 2) * 32 + 32]
            km_h = F[2][r0:r0 + 64, 512 + (h // 2) * 32:512 + (h // 2) * 32 + 32]
            P.do("tensor",
                 lambda e, q=qm_h, k=km_h: e.matmul(PS[1][0:32, 0:32], q, k,
                                                    start=True, stop=True),
                 waits=[qkm if h == 0 else None, self.ps_free(1)], incs=[cPE])
            pe = (cPE, cPE.n)
            P.do("scalar",
                 lambda e, h=h: e.activation(
                     F[3][32 * (h % 2):32 * (h % 2) + 32,
                          (h // 2) * 32:(h // 2) * 32 + 32],
                     PS[1][0:32, 0:32], AF.Copy),
                 waits=[pe], incs=[cS])
            self.ps_read(1, cS)
        simd = (cS, cS.n)
        sim3 = F[3][0:64, 0:256].rearrange("p (hc k) -> p hc k", k=32)
        cnt3 = F[3][0:64, 256:512].rearrange("p (hc k) -> p hc k", k=32)
        for i in range(32):
            P.do("vector",
                 lambda e, i=i: e.tensor_tensor(
                     F[3][0:64, 512:768].rearrange("p (hc k) -> p hc k", k=32),
                     sim3, sim3[:, :, i:i + 1].broadcast_to([64, 8, 32]),
                     ALU.is_gt),
                 waits=[simd] if i == 0 else [], incs=[cV])
            P.do("vector",
                 lambda e, i=i: e.tensor_reduce(
                     cnt3[:, :, i:i + 1].rearrange("p hc one -> p (hc one)")
                     if False else cnt3[:, :, i],
                     F[3][0:64, 512:768].rearrange("p (hc k) -> p hc k", k=32),
                     AX.X, ALU.add),
                 incs=[cV])
        P.do("vector",
             lambda e: e.tensor_single_scalar(F[3][0:64, 256:512],
                                              F[3][0:64, 256:512], 15.0,
                                              ALU.is_le),
             incs=[cV])
        mdone0 = (cV, cV.n)
        P.do("vector", lambda e: e.tensor_copy(B[2][0:64, 0:256],
                                               F[3][0:64, 256:512]),
             waits=[mdone0], incs=[cV])
        mb = (cV, cV.n)
        ident = self.CN["c_ident"]
        for hc in range(C):
            P.do("tensor",
                 lambda e, hc=hc: e.transpose(
                     self.PS[2].bitcast(BF16)[0:32, 0:64] if False else
                     self._psbf(2)[0:32, 0:64],
                     B[2][0:64, 256 + 0:256][:, 0:0] if False else
                     B[2][0:64, hc * 32:(hc + 1) * 32],
                     ident[0:64, 0:64]),
                 waits=[mb, self.ps_free(2)], incs=[cPE])
            pe = (cPE, cPE.n)
            P.do("scalar",
                 lambda e, hc=hc: e.activation(
                     F[3][0:32, 768 + hc * 64:768 + (hc + 1) * 64],
                     self._psbf(2)[0:32, 0:64], AF.Copy),
                 waits=[pe], incs=[cS])
            self.ps_read(2, cS)
        mtd = (cS, cS.n)
        P.dma("gpsimd", self.mT_hb[0:32, :], F[3][0:32, 768:1280],
              waits=[mtd], ctr=cSP)
        mhbm = (cSP, cSP.n)
        nst = lambda buf: (lambda c: buf[:, c * W:(c + 1) * W])
        self.emit_qkv64(self.Wt[0], self.Wt[1], self.Wt[2], wq, wk, wv_,
                        None, None, ln, war0=kzfill, kz=True)
        ov = self.o_hb.rearrange("(h d) t -> h d t", d=64)
        self._owar = None
        self._mrowwar = None
        P.do("gpsimd", lambda e: e.memset(B[1][:, 0:2080], 1.0),
             waits=[(cPE, cPE.n), (cV, cV.n), (cS, cS.n)], incs=[self.cG])
        P.do("gpsimd", lambda e: e.memset(B[5][:, 0:2080], 1.0),
             incs=[self.cG])
        vones = (self.cG, self.cG.n)
        self._kzwar = None
        for pr in range(8):
            pvm = (cPE, cPE.n)
            P.dma("sync", B[0][:], self.q_hb[pr * 128:(pr + 1) * 128, :],
                  waits=[(self._qwar if pr == 0 else None), (cPE, pvm[1])],
                  ctr=cL)
            P.dma("sync", B[1][:, 0:2080].rearrange("p (t d) -> p t d", d=65)
                  [:, :, 0:64],
                  self.v_hb[2 * pr],
                  waits=[self._vwar if pr == 0 else None,
                         vones if pr == 0 else None], ctr=cL)
            P.dma("sync", B[5][:, 0:2080].rearrange("p (t d) -> p t d", d=65)
                  [:, :, 0:64],
                  self.v_hb[2 * pr + 1],
                  ctr=cL)
            for hh in range(2):
                h = 2 * pr + hh
                P.dma("sync", B[3][:],
                      self.kz_hb[hh][pr * 128:(pr + 1) * 128, :],
                      waits=[self._kwar if (pr, hh) == (0, 0) else None] +
                      (self._kzwar or []), ctr=cL)
                ldp = (cL, cL.n)
                self.emit_pair_bound(ldp, 0.125)
                negM = VD[:, 54:55]
                mdone = (cS, cS.n)
                qrows = B[0]
                krows = B[3]
                vtile = [B[1], B[5]][hh]
                P.dma("sync", ST[0:1, 0:1024].rearrange("o (k q) -> o k q", q=32),
                      self.mT_hb[:, pr * 64 + hh * 32:pr * 64 + hh * 32 + 32]
                      .rearrange("k q -> k q")[None] if False else
                      self.mT_hb[:, pr * 64 + hh * 32:pr * 64 + hh * 32 + 32],
                      waits=[mhbm if pr == 0 and hh == 0 else None,
                             self._mrowwar], ctr=cL)
                mrl = (cL, cL.n)
                for g2 in range(2):
                    P.do("tensor",
                         lambda e, g2=g2: e.matmul(
                             PS[5][:], onesf[:],
                             ST[0:1, g2 * 512:(g2 + 1) * 512],
                             start=True, stop=True),
                         waits=[mrl, self.ps_free(5)], incs=[cPE])
                    pe = (cPE, cPE.n)
                    P.do("scalar",
                         lambda e, g2=g2: e.activation(
                             B[2][:, 2048 + g2 * 512:2048 + (g2 + 1) * 512],
                             PS[5][:], AF.Copy),
                         waits=[pe], incs=[cS])
                    self.ps_read(5, cS)
                self._mrowwar = (cS, cS.n)
                mex = (cS, cS.n)
                mexT = B[2][:, 2048:3072]
                SCB = (0, 1, 3, 4)
                for qg in range(NT):
                    pend = []

                    def emit_D(mslot, kt, mm, first):
                        P.do("tensor",
                             lambda e, m=mslot, kt=kt, vt=vtile, f=first:
                             e.matmul(PS[2][0:65, :],
                                      vt[:, kt * 65:(kt + 1) * 65], m,
                                      start=f, stop=False),
                             waits=[mm] + ([self.ps_free(2)] if first else []),
                             incs=[cPE])
                        self._pmwar[kt % 4] = (cPE, cPE.n)

                    for kt in range(32):
                        ps = SCB[kt % 4]
                        P.do("tensor",
                             lambda e, kt=kt, qg=qg, p=ps, kr=krows, qr=qrows:
                             e.matmul(PS[p][:],
                                      kr[:, kt * 128:(kt + 1) * 128],
                                      qr[:, qg * W:(qg + 1) * W],
                                      start=True, stop=True),
                             waits=[mdone if (hh, qg, kt) == (0, 0, 0) else None,
                                    mex if (qg, kt) == (0, 0) else None,
                                    self.ps_free(ps)],
                             incs=[cPE])
                        pe = (cPE, cPE.n)
                        if len(pend) == 4:
                            a = pend.pop(0)
                            emit_D(*a, first=(a[1] == 0))
                        slot = B[2][:, (kt % 4) * W:(kt % 4 + 1) * W]
                        mslot = B[4][:, (kt % 4) * W:(kt % 4 + 1) * W]
                        P.do("scalar",
                             lambda e, p=ps, s=slot: e.activation(
                                 s, PS[p][:], AF.Exp, scale=VE[:, 60:61],
                                 bias=negM),
                             waits=[pe, self._pjwar.get(kt % 4)], incs=[cS])
                        self.ps_read(ps, cS)
                        P.do("vector",
                             lambda e, s=slot, m=mslot, kt=kt, qg=qg, mx=mexT:
                             e.tensor_tensor(
                                 m.rearrange("p (b r) -> p b r", r=128),
                                 s.rearrange("p (b r) -> p b r", r=128),
                                 mx[:, kt * 32 + qg * 4:kt * 32 + qg * 4 + 4]
                                 .rearrange("p (b one) -> p b one", one=1)
                                 .broadcast_to([128, 4, 128]),
                                 ALU.mult),
                             waits=[(cS, cS.n), self._pmwar.get(kt % 4)],
                             incs=[cV])
                        self._pjwar[kt % 4] = (cV, cV.n)
                        pend.append((mslot, kt, (cV, cV.n)))
                    for args in pend:
                        emit_D(*args, first=(args[1] == 0))
                    self.emit_att_finalize(ov, h, qg, den64=True)
                self._kzwar = [(cPE, cPE.n), (cS, cS.n)]
        attm = (cSP, cSP.n)
        xsv = x_src.rearrange("(c p) t -> p c t", p=128)
        xdv = x_dst.rearrange("(c p) t -> p c t", p=128)
        ohv = self.o_hb.rearrange("(c p) t -> p c t", p=128)
        rN = Rot([B[0], B[1]])
        rX = Rot([F[2], F[3]])
        rX.cons = [(cV, cV.n), (cV, cV.n)]
        self._f1war = 0
        for tc in range(NT):
            nb, war = rN.get(tc)
            P.dma("sync", nb[:].rearrange("p (c w) -> p c w", w=W),
                  ohv[:, :, tc * W:(tc + 1) * W],
                  waits=[war if war else None, attm if tc == 0 else None],
                  ctr=cL)
            xt, warx = rX.get(tc)
            P.dma("sync", xt[:].rearrange("p (c w) -> p c w", w=W),
                  xsv[:, :, tc * W:(tc + 1) * W], waits=[warx], ctr=cL)
            ld = (cL, cL.n)
            for oc in range(C):
                ps = oc % 2
                pe = self.mm_proj(ps, self.Wt[3], oc, nst(nb),
                                  extra_waits=[ld, wo])
                P.do("vector",
                     lambda e, p=ps, o=oc, x=xt: e.scalar_tensor_tensor(
                         F[1][:, o * W:(o + 1) * W], PS[p][:],
                         VE[:, 50 + o:51 + o], x[:, o * W:(o + 1) * W],
                         ALU.add, ALU.add),
                     waits=[pe, (self.cXW, self._f1war) if tc >= 1 else None],
                     incs=[cV])
                self.ps_read(ps, cV)
            rN.consumed(tc, cV)
            rX.consumed(tc, cV)
            P.dma("sync", xdv[:, :, tc * W:(tc + 1) * W],
                  F[1][:].rearrange("p (c w) -> p c w", w=W),
                  waits=[(cV, cV.n)], ctr=self.cXW)
            self._f1war = self.cXW.n
        prj = (self.cXW, self.cXW.n)
        sgT = self.wd["L2sgnT"].rearrange("(c p) o -> p c o", p=128)
        w3v = self.wv8(self.Wt[3])
        self._sgw = (cPE, cPE.n)
        for c in range(C):
            P.dma("sync", F[3][:, 0:1024].rearrange("p (c o) -> p c o", o=1024)
                  [:, 0, :] if False else F[3][:, 0:1024],
                  sgT[:, c, :], waits=[self._sgw, prj if c == 0 else None],
                  ctr=cL)
            ld = (cL, cL.n)
            P.do("scalar",
                 lambda e, c=c: e.activation(w3v[:, c, :], F[3][:, 0:1024],
                                             AF.Sign),
                 waits=[ld], incs=[cS])
            self._sgw = (cS, cS.n)
        sgd = (cS, cS.n)
        rN = Rot([B[0], B[1]])
        self._ffwar = 0
        for tc in range(NT):
            nb, war = rN.get(tc)
            P.dma("sync", nb[:].rearrange("p (c w) -> p c w", w=W),
                  self.xn_h[tc].rearrange("c p w -> p c w"),
                  waits=[war], ctr=cL)
            ld = (cL, cL.n)
            for oc in range(C):
                ps = oc % 2
                pe = self.mm_proj(ps, self.Wt[3], oc, nst(nb),
                                  extra_waits=[ld, sgd])
                P.do("vector",
                     lambda e, p=ps, o=oc: e.tensor_copy(
                         F[1][:, o * W:(o + 1) * W], PS[p][:]),
                     waits=[pe, (cSP, self._ffwar) if tc >= 1 else None],
                     incs=[cV])
                self.ps_read(ps, cV)
            rN.consumed(tc, cV)
            P.dma("gpsimd", xdv[:, :, tc * W:(tc + 1) * W],
                  F[1][:].rearrange("p (c w) -> p c w", w=W),
                  waits=[(cV, cV.n)], ctr=cSP, accum_op=ALU.add)
            self._ffwar = cSP.n

    def emit_l3(self, x_src, x_dst):
        P = self.P
        cL, cPE, cV, cS, cSP = self.cL, self.cPE, self.cV, self.cS, self.cSP
        F, B, PS, PR = self.F, self.B, self.PS, self.PR
        VE, VD = self.VE, self.VD
        onesb = self.CN["c_ones_col_bf"]
        onesf = self.CN["c_ones_row_f"]
        cswp = self.CN["c_swp"]
        CCS = self.CCS
        self._ccwar = None
        ctri = self.CN["c_ctri"]
        self._pjwar = {}
        self._pmwar = {}
        waq = self.wload(self.wv8(self.Wt[0]), "L3waq")
        wak = self.wload(self.wv8(self.Wt[1]), "L3wak")
        wav = self.wload(self.wv8(self.Wt[2]), "L3wav")
        wao = self.wload(self.wv8(self.Wt[3]), "L3wao")
        P.dma("gpsimd", self.SM[:, 0:128].rearrange("p (c o) -> p c o", o=16),
              self.wd["L3sel"].rearrange("(c p) o -> p c o", p=128), ctr=self.cW)
        wsel = (self.cW, self.cW.n)
        self.emit_ln(x_src)
        ln = self.lnmark
        P.do("tensor", lambda e: e.matmul(PS[5][:, 0:1], onesf[:],
                                          VE[0:1, 58:59], start=True, stop=True),
             waits=[self.consts_ld, self.ps_free(5)], incs=[cPE])
        P.do("scalar", lambda e: e.activation(VD[:, 56:57], PS[5][:, 0:1],
                                              AF.Copy),
             waits=[(cPE, cPE.n)], incs=[cS])
        self.ps_read(5, cS)
        qv3 = self.q_hb.rearrange("(h d) t -> h d t", d=128)
        kv3 = self.k_hb.rearrange("(h d) t -> h d t", d=128)
        nst = lambda buf: (lambda c: buf[:, c * W:(c + 1) * W])
        rN = Rot([B[0], B[1]])
        self._qsp = None
        self._vsp = None
        for tc in range(NT):
            nb, war = rN.get(tc)
            P.dma("sync", nb[:].rearrange("p (c w) -> p c w", w=W),
                  self.xn_h[tc].rearrange("c p w -> p c w"),
                  waits=[war, ln], ctr=cL)
            P.dma("sync", CCS[:, 0:W], self.cd["c_cc"][:, tc * W:(tc + 1) * W],
                  waits=[self._ccwar], ctr=cL)
            P.dma("sync", CCS[:, W:2 * W],
                  self.cd["c_ss"][:, tc * W:(tc + 1) * W], ctr=cL)
            ccld = (cL, cL.n)
            ld = (cL, cL.n)
            for h in range(C):
                for (wt, wld, dst, dorope) in ((self.Wt[0], waq, qv3, True),
                                               (self.Wt[1], wak, kv3, True),
                                               (self.Wt[2], wav, None, False)):
                    if dorope:
                        ps = h % 2
                        pe = self.mm_proj(ps, wt, h, nst(nb),
                                          extra_waits=[ld, wld])
                        P.do("scalar",
                             lambda e, p=ps: e.activation(B[2][:, 0:W],
                                                          PS[p][:], AF.Copy),
                             waits=[pe, self._qsp], incs=[cS])
                        self.ps_read(ps, cS)
                        ev1 = (cS, cS.n)
                        P.do("tensor",
                             lambda e: e.matmul(PS[4][:], cswp[:], B[2][:, 0:W],
                                                start=True, stop=True),
                             waits=[ev1, self.ps_free(4)], incs=[cPE])
                        pe2 = (cPE, cPE.n)
                        P.do("scalar",
                             lambda e: e.activation(B[2][:, W:2 * W], PS[4][:],
                                                    AF.Copy),
                             waits=[pe2], incs=[cS])
                        self.ps_read(4, cS)
                        ev2 = (cS, cS.n)
                        P.do("vector",
                             lambda e: e.tensor_mul(
                                 F[3][:, 0:W], B[2][:, 0:W], CCS[:, 0:W]),
                             waits=[ev1, ccld], incs=[cV])
                        P.do("vector",
                             lambda e: e.tensor_mul(
                                 F[3][:, W:2 * W], B[2][:, W:2 * W],
                                 CCS[:, W:2 * W]),
                             waits=[ev2], incs=[cV])
                        P.do("gpsimd",
                             lambda e: e.tensor_add(B[2][:, 2 * W:3 * W],
                                                    F[3][:, 0:W],
                                                    F[3][:, W:2 * W]),
                             waits=[(cV, cV.n), (self.cG, self.cG.n)],
                             incs=[self.cG])
                        self._qsp = (self.cG, self.cG.n)
                        P.dma("sync", dst[h][:, tc * W:(tc + 1) * W],
                              B[2][:, 2 * W:3 * W],
                              waits=[(self.cG, self.cG.n)], ctr=cSP)
                    else:
                        w2v = self.wv8(wt)
                        for tt in range(4):
                            ps = 2 + tt % 2
                            for c in range(C):
                                P.do("tensor",
                                     lambda e, c=c, h=h, t=tt, p=ps, nb=nb,
                                     f=(c == 0), l=(c == C - 1):
                                     e.matmul(PS[p][:, 0:128],
                                              nb[:, c * W + t * 128:
                                                 c * W + (t + 1) * 128],
                                              w2v[:, c, h * 128:(h + 1) * 128],
                                              start=f, stop=l),
                                     waits=[wld, self.ps_free(ps)]
                                     if c == 0 else [],
                                     incs=[cPE] if c == C - 1 else [])
                            pe = (cPE, cPE.n)
                            P.do("scalar",
                                 lambda e, p=ps, t=tt: e.activation(
                                     B[3][:, t * 128:(t + 1) * 128],
                                     PS[p][:, 0:128], AF.Copy),
                                 waits=[pe, self._vsp if tt == 0 else None],
                                 incs=[cS])
                            self.ps_read(ps, cS)
                        P.dma("gpsimd",
                              self.v_hb3[h][:, tc * 4:(tc + 1) * 4, :],
                              B[3][:, 0:512].rearrange("p (t d) -> p t d",
                                                       d=128),
                              waits=[(cS, cS.n)], ctr=cSP)
                        self._vsp = (cSP, cSP.n)
            self._ccwar = (cV, cV.n)
            rN.consumed(tc, cPE)
        qkvm = (cSP, cSP.n)
        ov3 = self.o_hb.rearrange("(h d) t -> h d t", d=128)
        self._owar = None
        for h in range(C):
            pvm = (cPE, cPE.n)
            P.dma("sync", B[0][:], self.q_hb[h * 128:(h + 1) * 128, :],
                  waits=[qkvm if h == 0 else None, (cPE, pvm[1])], ctr=cL)
            P.dma("sync", B[3][:], self.k_hb[h * 128:(h + 1) * 128, :],
                  ctr=cL)
            P.dma("sync", B[1][:],
                  self.v_hb3[h].rearrange("p t d -> p (t d)"),
                  ctr=cL)
            ldp = (cL, cL.n)
            self.emit_pair_bound(ldp, 1.0 / 128.0)
            negM = VD[:, 54:55]
            mdone = (cS, cS.n)
            SCB = (0, 1, 3, 4)
            for qg in range(NT):
                nj = 4 * qg + 4
                pend = []

                def emit_D(mop, jt, mm, first):
                    P.do("tensor",
                         lambda e, m=mop, f=first: e.matmul(
                             PR[0][0:1, 0:W], onesb[:], m, start=f, stop=False),
                         waits=[mm] + ([self.ps_free(10)] if first else []),
                         incs=[cPE])
                    P.do("tensor",
                         lambda e, m=mop, jt=jt, f=first: e.matmul(
                             PS[2][:], B[1][:, jt * 128:(jt + 1) * 128], m,
                             start=f, stop=False),
                         waits=[self.ps_free(2)] if first else [],
                         incs=[cPE])
                    self._pmwar[jt % 4] = (cPE, cPE.n)
                    self._pjwar[jt % 4] = (cPE, cPE.n)

                for jt in range(nj):
                    ps = SCB[jt % 4]
                    diag_i = jt - 4 * qg
                    P.do("tensor",
                         lambda e, jt=jt, qg=qg, p=ps: e.matmul(
                             PS[p][:], B[3][:, jt * 128:(jt + 1) * 128],
                             B[0][:, qg * W:(qg + 1) * W],
                             start=True, stop=True),
                         waits=[mdone if (qg, jt) == (0, 0) else None,
                                self.ps_free(ps)],
                         incs=[cPE])
                    pe = (cPE, cPE.n)
                    if len(pend) == 4:
                        a = pend.pop(0)
                        emit_D(*a, first=(a[1] == 0))
                    slot = B[2][:, (jt % 4) * W:(jt % 4 + 1) * W]
                    P.do("scalar",
                         lambda e, p=ps, s=slot: e.activation(
                             s, PS[p][:], AF.Exp, scale=VE[:, 63:64],
                             bias=negM),
                         waits=[pe, self._pjwar.get(jt % 4)], incs=[cS])
                    self.ps_read(ps, cS)
                    if diag_i >= 0:
                        mslot = B[4][:, (jt % 4) * W:(jt % 4 + 1) * W]
                        P.do("vector",
                             lambda e, s=slot, m=mslot, i=diag_i: e.tensor_mul(
                                 m, s, ctri[:, i * W:(i + 1) * W]),
                             waits=[(cS, cS.n), self._pmwar.get(jt % 4)],
                             incs=[cV])
                        self._pjwar[jt % 4] = (cV, cV.n)
                        pend.append((mslot, jt, (cV, cV.n)))
                    else:
                        self._pjwar[jt % 4] = (cS, cS.n)
                        pend.append((slot, jt, (cS, cS.n)))
                for a in pend:
                    emit_D(*a, first=(a[1] == 0))
                self.emit_att_finalize(ov3, h, qg, drows=128)
        attm = (cSP, cSP.n)
        xsv = x_src.rearrange("(c p) t -> p c t", p=128)
        xdv = x_dst.rearrange("(c p) t -> p c t", p=128)
        ohv = self.o_hb.rearrange("(c p) t -> p c t", p=128)
        rN = Rot([B[0], B[1]])
        rX = Rot([F[2], F[3]])
        rX.cons = [(cV, cV.n), (cV, cV.n)]
        self._f1war = 0
        for tc in range(NT):
            nb, war = rN.get(tc)
            P.dma("sync", nb[:].rearrange("p (c w) -> p c w", w=W),
                  ohv[:, :, tc * W:(tc + 1) * W],
                  waits=[war if war else None, attm if tc == 0 else None],
                  ctr=cL)
            xt, warx = rX.get(tc)
            P.dma("sync", xt[:].rearrange("p (c w) -> p c w", w=W),
                  xsv[:, :, tc * W:(tc + 1) * W], waits=[warx], ctr=cL)
            ld = (cL, cL.n)
            for oc in range(C):
                ps = oc % 2
                pe = self.mm_proj(ps, self.Wt[3], oc, nst(nb),
                                  extra_waits=[ld, wao])
                P.do("vector",
                     lambda e, p=ps, o=oc, x=xt: e.tensor_add(
                         F[1][:, o * W:(o + 1) * W], PS[p][:],
                         x[:, o * W:(o + 1) * W]),
                     waits=[pe, (self.cXW, self._f1war) if tc >= 1 else None],
                     incs=[cV])
                self.ps_read(ps, cV)
            rN.consumed(tc, cV)
            rX.consumed(tc, cV)
            P.dma("sync", xdv[:, :, tc * W:(tc + 1) * W],
                  F[1][:].rearrange("p (c w) -> p c w", w=W),
                  waits=[(cV, cV.n)], ctr=self.cXW)
            self._f1war = self.cXW.n
        prj = (self.cXW, self.cXW.n)
        rN = Rot([B[0], B[1]])
        for tc in range(NT):
            nb, war = rN.get(tc)
            P.dma("sync", nb[:].rearrange("p (c w) -> p c w", w=W),
                  self.xn_h[tc].rearrange("c p w -> p c w"),
                  waits=[war], ctr=cL)
            ld = (cL, cL.n)
            for tt in range(4):
                ps = tt % 2
                for c in range(C):
                    P.do("tensor",
                         lambda e, c=c, t=tt, p=ps, nb=nb, f=(c == 0),
                         l=(c == C - 1): e.matmul(
                             PS[p][:, 0:16],
                             nb[:, c * W + t * 128:c * W + (t + 1) * 128],
                             self.SM[:, 0:128].rearrange(
                                 "p (c o) -> p c o", o=16)[:, c, :],
                             start=f, stop=l),
                         waits=[ld, wsel, self.ps_free(ps)] if c == 0 else [],
                         incs=[cPE] if c == C - 1 else [])
                pe = (cPE, cPE.n)
                P.do("scalar",
                     lambda e, p=ps, t=tt, tc=tc: e.activation(
                         F[2][:, (tc * 4 + t) * 16:(tc * 4 + t + 1) * 16],
                         PS[p][:, 0:16], AF.Sigmoid),
                     waits=[pe], incs=[cS])
                self.ps_read(ps, cS)
            rN.consumed(tc, cPE)
        seld = (cS, cS.n)
        sel3 = F[2][:, 0:512].rearrange("p (t e) -> p t e", e=16)
        cnt3 = F[2][:, 512:1024].rearrange("p (t e) -> p t e", e=16)
        for i in range(16):
            P.do("vector",
                 lambda e, i=i: e.tensor_tensor(
                     F[3][:, 0:512].rearrange("p (t e) -> p t e", e=16),
                     sel3, sel3[:, :, i:i + 1].broadcast_to([128, 32, 16]),
                     ALU.is_gt),
                 waits=[seld if i == 0 else None, (cV, cV.n)], incs=[cV])
            P.do("vector",
                 lambda e, i=i: e.tensor_reduce(
                     cnt3[:, :, i],
                     F[3][:, 0:512].rearrange("p (t e) -> p t e", e=16),
                     AX.X, ALU.add),
                 waits=[(cV, cV.n)], incs=[cV])
        P.do("vector", lambda e: e.tensor_single_scalar(
            F[2][:, 512:1024], F[2][:, 512:1024], 3.0, ALU.is_le),
             waits=[(cV, cV.n)], incs=[cV])
        P.do("vector", lambda e: e.tensor_mul(F[3][:, 0:512], F[2][:, 0:512],
                                              F[2][:, 512:1024]),
             waits=[(cV, cV.n)], incs=[cV])
        P.do("vector", lambda e: e.tensor_reduce(
            F[3][:, 512:544], F[3][:, 0:512].rearrange("p (t e) -> p t e",
                                                       e=16),
            AX.X, ALU.add), waits=[(cV, cV.n)], incs=[cV])
        P.do("vector", lambda e: e.tensor_single_scalar(
            F[3][:, 512:544], F[3][:, 512:544], 1e-9, ALU.max),
             waits=[(cV, cV.n)], incs=[cV])
        P.do("vector", lambda e: e.reciprocal(F[3][:, 544:576],
                                              F[3][:, 512:544]),
             waits=[(cV, cV.n)], incs=[cV])
        P.do("vector", lambda e: e.tensor_tensor(
            F[3][:, 0:512].rearrange("p (t e) -> p t e", e=16),
            F[3][:, 0:512].rearrange("p (t e) -> p t e", e=16),
            F[3][:, 544:576].rearrange("p (t o) -> p t o", o=1)
            .broadcast_to([128, 32, 16]), ALU.mult),
             waits=[(cV, cV.n)], incs=[cV])
        wnd = (cV, cV.n)
        WN = F[3]
        P.do("vector", lambda e: e.tensor_copy(B[5][:, 0:512], F[3][:, 0:512]),
             waits=[(cV, cV.n)], incs=[cV])
        WNB = B[5]
        wnd = (cV, cV.n)
        if getattr(self, "dbg_l3", None) is not None:
            P.dma("sync", self.dbg_l3["sel"], F[2][:, 0:512],
                  waits=[wnd], ctr=cSP)
            P.dma("sync", self.dbg_l3["wn"], F[3][:, 0:576],
                  waits=[wnd], ctr=cSP)
            wnd = (cSP, cSP.n)
        kv_of = lambda wt, slot: wt[:, slot * 4096:slot * 4096 + 2048].rearrange(
            "p (c o) -> p c o", o=256)
        vv_of = lambda wt, slot: wt[:, slot * 4096 + 2048:(slot + 1) * 4096
                                    ].rearrange("p (c o) -> p c o", o=1024)
        self._pairwar = {}
        self._ffwar = 0
        self._hswar = {}
        self._wbwar = {}
        self._wrowwar = None
        rNf = Rot([B[0], B[1]])
        for tc in range(NT):
            nb, war = rNf.get(tc)
            P.dma("sync", nb[:].rearrange("p (c w) -> p c w", w=W),
                  self.xn_h[tc].rearrange("c p w -> p c w"),
                  waits=[war, wnd if tc == 0 else None], ctr=cL)
            ld = (cL, cL.n)
            for pss in range(4):
                pair = (self.Wt[0], self.Wt[1]) if pss % 2 == 0 else                     (self.Wt[2], self.Wt[3])
                wwait = [self._pairwar.get(pss % 2),
                         prj if (tc, pss) == (0, 0) else None]
                for ei in range(4):
                    ex = pss * 4 + ei
                    wt, slot = pair[ei // 2], ei % 2
                    P.dma("gpsimd", kv_of(wt, slot),
                          self.wd["L3keys"][ex].rearrange("(c p) o -> p c o",
                                                          p=128),
                          waits=wwait if ei == 0 else [], ctr=self.cW)
                    P.dma("gpsimd", vv_of(wt, slot),
                          self.wd["L3vals"][ex].rearrange("(c p) o -> p c o",
                                                          p=128),
                          ctr=self.cW)
                wmark = (self.cW, self.cW.n)
                identf = self.CN["c_identf"]
                for ei in range(4):
                    ex = pss * 4 + ei
                    wt, slot = pair[ei // 2], ei % 2
                    for tt in range(4):
                        P.do("tensor",
                             lambda e, tt=tt, ex=ex, tc=tc: e.matmul(
                                 PR[1][0:1, tt * 128:(tt + 1) * 128],
                                 WNB[:, (tc * 4 + tt) * 16 + ex:
                                     (tc * 4 + tt) * 16 + ex + 1],
                                 self.CN["c_ident"][:], start=True, stop=True),
                             waits=[wnd if (tc, pss, ei, tt) == (0, 0, 0, 0)
                                    else None,
                                    self.ps_free(11) if tt == 0 else None],
                             incs=[cPE] if tt == 3 else [])
                    pewb = (cPE, cPE.n)
                    P.do("scalar",
                         lambda e: e.activation(F[2][0:1, 512:1024],
                                                PR[1][0:1, 0:512], AF.Copy),
                         waits=[pewb, self._wrowwar], incs=[cS])
                    self.ps_read(11, cS)
                    wre = (cS, cS.n)
                    P.do("tensor",
                         lambda e: e.matmul(PS[5][:], onesf[:],
                                            F[2][0:1, 512:1024],
                                            start=True, stop=True),
                         waits=[wre, self.ps_free(5)], incs=[cPE])
                    self._wrowwar = (cPE, cPE.n)
                    pewb2 = (cPE, cPE.n)
                    P.do("scalar",
                         lambda e, ei=ei: e.activation(
                             B[4][:, ei * 512:(ei + 1) * 512], PS[5][:],
                             AF.Copy),
                         waits=[pewb2, self._wbwar.get(ei)], incs=[cS])
                    self.ps_read(5, cS)
                    for half in range(2):
                        ps = half
                        for c in range(C):
                            P.do("tensor",
                                 lambda e, c=c, h=half, nb=nb, wt=wt,
                                 slot=slot, p=ps, f=(c == 0), l=(c == C - 1):
                                 e.matmul(
                                     PS[p][:],
                                     kv_of(wt, slot)[:, c,
                                                     h * 128:(h + 1) * 128],
                                     nb[:, c * W:(c + 1) * W],
                                     start=f, stop=l),
                                 waits=[ld, wmark, self.ps_free(ps)]
                                 if c == 0 else [],
                                 incs=[cPE] if c == C - 1 else [])
                        pek = (cPE, cPE.n)
                        P.do("scalar",
                             lambda e, ei=ei, h=half, p=ps: e.activation(
                                 B[2][:, (ei * 2 + h) * 512:
                                      (ei * 2 + h + 1) * 512],
                                 PS[p][:], AF.Relu),
                             waits=[pek, self._hswar.get((ei, h))], incs=[cS])
                        self.ps_read(ps, cS)
                        P.do("vector",
                             lambda e, ei=ei, h=half: e.tensor_mul(
                                 B[2][:, (ei * 2 + h) * 512:
                                      (ei * 2 + h + 1) * 512],
                                 B[2][:, (ei * 2 + h) * 512:
                                      (ei * 2 + h + 1) * 512],
                                 B[4][:, ei * 512:(ei + 1) * 512]),
                             waits=[(cS, cS.n)], incs=[cV])
                    self._wbwar[ei] = (cV, cV.n)
                hall = (cV, cV.n)
                for oc in range(C):
                    ps = 2 + oc % 2
                    kk = 0
                    for ei in range(4):
                        wt, slot = pair[ei // 2], ei % 2
                        for hc in range(2):
                            P.do("tensor",
                                 lambda e, ei=ei, hc=hc, oc=oc, wt=wt,
                                 slot=slot, p=ps, f=(kk == 0), l=(kk == 7):
                                 e.matmul(
                                     PS[p][:],
                                     vv_of(wt, slot)[:, hc,
                                                     oc * 128:(oc + 1) * 128],
                                     B[2][:, (ei * 2 + hc) * 512:
                                          (ei * 2 + hc + 1) * 512],
                                     start=f, stop=l),
                                 waits=[hall, self.ps_free(ps)]
                                 if kk == 0 else [],
                                 incs=[cPE] if kk == 7 else [])
                            kk += 1
                    pev = (cPE, cPE.n)
                    col = oc * W
                    if pss == 0:
                        P.do("vector",
                             lambda e, col=col, p=ps: e.tensor_copy(
                                 F[1][:, col:col + W], PS[p][:]),
                             waits=[pev] + ([(cSP, self._ffwar)
                                             if self._ffwar else None]
                                            if oc == 0 else []),
                             incs=[cV])
                    else:
                        src_acc = F[1] if pss in (1, 3) else F[0]
                        dst_acc = F[0] if pss in (1, 3) else F[1]
                        P.do("vector",
                             lambda e, col=col, p=ps, s=src_acc, d=dst_acc:
                             e.tensor_add(d[:, col:col + W], PS[p][:],
                                          s[:, col:col + W]),
                             waits=[pev], incs=[cV])
                    self.ps_read(ps, cV)
                vend = (cPE, cPE.n)
                for ei in range(4):
                    self._hswar[(ei, 0)] = vend
                    self._hswar[(ei, 1)] = vend
                self._pairwar[pss % 2] = vend
            rNf.consumed(tc, cPE)
            P.do("vector",
                 lambda e: e.tensor_scalar_mul(F[1][:], F[0][:], VD[:, 56:57]),
                 waits=[(cV, cV.n)], incs=[cV])
            if getattr(self, "dbg_l3", None) is not None:
                mdv = self.dbg_l3["moe"].rearrange("(c p) t -> p c t", p=128)
                P.dma("gpsimd", mdv[:, :, tc * W:(tc + 1) * W],
                      F[1][:].rearrange("p (c w) -> p c w", w=W),
                      waits=[(cV, cV.n)], ctr=cSP)
            else:
                P.dma("gpsimd", xdv[:, :, tc * W:(tc + 1) * W],
                      F[1][:].rearrange("p (c w) -> p c w", w=W),
                      waits=[(cV, cV.n)], ctr=cSP, accum_op=ALU.add)
            self._ffwar = cSP.n

    def _psbf(self, i):
        return self.PS[i][:].bitcast(BF16)[:, 0:1024]

    def emit_pair_bound(self, ldp, scale):
        P = self.P
        cPE, cV, cS = self.cPE, self.cV, self.cS
        B, PR, PS, VD = self.B, self.PR, self.PS, self.VD
        onesb = self.CN["c_ones_col_bf"]
        onesf = self.CN["c_ones_row_f"]
        for (bsrc, col) in ((B[0], 52), (B[3], 53)):
            P.do("scalar", lambda e, s=bsrc: e.activation(B[2][:], s[:],
                                                          AF.Square),
                 waits=[ldp, (cV, cV.n)], incs=[cS])
            sq = (cS, cS.n)
            for i in range(NT):
                pri = PR[i % 2]
                P.do("tensor",
                     lambda e, i=i, pri=pri: e.matmul(
                         pri[0:1, 0:W], onesb[:],
                         B[2][:, i * W:(i + 1) * W], start=True, stop=True),
                     waits=[sq, self.ps_free(10 + i % 2)], incs=[cPE])
                pe = (cPE, cPE.n)
                P.do("vector",
                     lambda e, i=i, pri=pri: e.reduce_max(
                         VD[0:1, 44 + i:45 + i], pri[0:1, 0:W], AX.X),
                     waits=[pe], incs=[cV])
                self.ps_read(10 + i % 2, cV)
            P.do("vector",
                 lambda e, col=col: e.reduce_max(VD[0:1, col:col + 1],
                                                 VD[0:1, 44:52], AX.X),
                 waits=[(cV, cV.n)], incs=[cV])
        P.do("vector", lambda e: e.tensor_mul(VD[0:1, 52:53], VD[0:1, 52:53],
                                              VD[0:1, 53:54]),
             waits=[(cV, cV.n)], incs=[cV])
        P.do("scalar", lambda e: e.activation(VD[0:1, 53:54], VD[0:1, 52:53],
                                              AF.Sqrt),
             waits=[(cV, cV.n)], incs=[cS])
        P.do("vector", lambda e, scale=scale: e.tensor_scalar_mul(
            VD[0:1, 52:53], VD[0:1, 53:54], -scale),
             waits=[(cS, cS.n)], incs=[cV])
        P.do("tensor", lambda e: e.matmul(PS[5][:, 0:1], onesf[:],
                                          VD[0:1, 52:53], start=True,
                                          stop=True),
             waits=[(cV, cV.n), self.ps_free(5)], incs=[cPE])
        P.do("scalar", lambda e: e.activation(VD[:, 54:55], PS[5][:, 0:1],
                                              AF.Copy),
             waits=[(cPE, cPE.n)], incs=[cS])
        self.ps_read(5, cS)

    def emit_att_finalize(self, ov, h, qg, corr=None, drows=64, den64=False):
        P = self.P
        cPE, cV, cS, cSP = self.cPE, self.cV, self.cS, self.cSP
        F, B, PS, PR = self.F, self.B, self.PS, self.PR
        onesf = self.CN["c_ones_row_f"]
        denpe = (cPE, cPE.n)
        if den64:
            P.do("vector",
                 lambda e: e.reciprocal(F[2][64:65, 1024:1024 + W],
                                        PS[2][64:65, 0:W]),
                 waits=[denpe, (cV, cV.n)], incs=[cV])
            P.do("tensor",
                 lambda e: e.matmul(PS[5][:],
                                    self.CN["c_onesf_all"][64:65, 0:128],
                                    F[2][64:65, 1024:1024 + W],
                                    start=True, stop=True),
                 waits=[(cV, cV.n), self.ps_free(5)], incs=[cPE])
            P.do("scalar",
                 lambda e: e.activation(F[3][:, 0:W], PS[5][:], AF.Copy),
                 waits=[(cPE, cPE.n)], incs=[cS])
            self.ps_read(5, cS)
            dn = (cS, cS.n)
            P.do("vector",
                 lambda e, dr=drows: e.tensor_mul(B[4][0:dr, 3584:3584 + W],
                                                  PS[2][0:dr, :],
                                                  F[3][0:dr, 0:W]),
                 waits=[dn, self._owar], incs=[cV])
            self.ps_read(2, cV)
            P.dma("gpsimd", ov[h][:, qg * W:(qg + 1) * W],
                  B[4][0:drows, 3584:3584 + W],
                  waits=[(cV, cV.n)], ctr=cSP)
            self._owar = (cSP, cSP.n)
            return
        P.do("scalar",
             lambda e: e.activation(F[2][0:1, 0:W], PR[0][0:1, 0:W], AF.Copy),
             waits=[denpe], incs=[cS])
        self.ps_read(10, cS)
        if corr is not None:
            corr()
        P.do("vector",
             lambda e: e.reciprocal(F[2][0:1, 1024:1024 + W], F[2][0:1, 0:W]),
             waits=[(cS, cS.n), (cV, cV.n)], incs=[cV])
        P.do("tensor",
             lambda e: e.matmul(PS[5][:], onesf[:], F[2][0:1, 1024:1024 + W],
                                start=True, stop=True),
             waits=[(cV, cV.n), self.ps_free(5)], incs=[cPE])
        P.do("scalar",
             lambda e: e.activation(F[3][:, 0:W], PS[5][:], AF.Copy),
             waits=[(cPE, cPE.n)], incs=[cS])
        self.ps_read(5, cS)
        dn = (cS, cS.n)
        P.do("vector",
             lambda e, dr=drows: e.tensor_mul(B[4][0:dr, 3584:3584 + W],
                                              PS[2][0:dr, :], F[3][0:dr, 0:W]),
             waits=[dn, self._owar], incs=[cV])
        self.ps_read(2, cV)
        self._fin_extra(dn)
        P.dma("gpsimd", ov[h][:, qg * W:(qg + 1) * W],
              B[4][0:drows, 3584:3584 + W],
              waits=[(cV, cV.n)], ctr=cSP)
        self._owar = (cSP, cSP.n)

    def _fin_extra(self, dn):
        pass

    def _softmax16(self, raw, tmp, bdst, vd_col):
        P, cV, cS = self.P, self.cV, self.cS
        VD = self.VD
        P.do("vector", lambda e: e.reduce_max(VD[0:16, vd_col:vd_col + 1],
                                              raw[0:16, :], AX.X, negate=True),
             waits=[(cS, cS.n)], incs=[cV])
        P.do("scalar", lambda e: e.activation(tmp[0:16, :], raw[0:16, :], AF.Exp,
                                              bias=VD[0:16, vd_col:vd_col + 1],
                                              accum_out=VD[0:16, vd_col + 1:vd_col + 2]),
             waits=[(cV, cV.n)], incs=[cS])
        P.do("vector", lambda e: e.reciprocal(VD[0:16, vd_col + 1:vd_col + 2],
                                              VD[0:16, vd_col + 1:vd_col + 2]),
             waits=[(cS, cS.n)], incs=[cV])
        P.do("vector", lambda e: e.tensor_scalar_mul(
            bdst[0:16, :], tmp[0:16, :], VD[0:16, vd_col + 1:vd_col + 2]),
             waits=[(cV, cV.n)], incs=[cV])
        self._smx = (cV, cV.n)

    def _pool16(self, wbf, sp, rN, PQ, col):
        P, cL, cPE, cV, cS = self.P, self.cL, self.cPE, self.cV, self.cS
        selv = self.CN["c_sel"]
        PS = self.PS
        parts = self.F[2]
        for tc in range(NT):
            nb, war = rN.get(tc)
            P.dma("sync", nb[:].rearrange("p (c w) -> p c w", w=W),
                  sp[tc].rearrange("c p w -> p c w"),
                  waits=[war], ctr=cL)
            ld = (cL, cL.n)
            for c in range(C):
                ps = 2 + (tc * C + c) % 2
                P.do("tensor",
                     lambda e, c=c, t=tc, p=ps: e.matmul(
                         PS[p][:, 0:W], selv[:, c * 128:(c + 1) * 128],
                         wbf[0:16, t * W:(t + 1) * W], start=True, stop=True),
                     waits=[self._smx, self.ps_free(ps)], incs=[cPE])
                pe = (cPE, cPE.n)
                P.do("vector",
                     lambda e, p=ps, c=c, t=tc, n=nb: e.scalar_tensor_tensor(
                         self.B[2][:, 0:W], PS[p][:, 0:W], 1.0,
                         n[:, c * W:(c + 1) * W], ALU.mult, ALU.mult,
                         accum_out=parts[:, (c * NT + t):(c * NT + t + 1)]),
                     waits=[pe, ld,
                            self._b2war if (tc, c) == (0, 0) else None],
                     incs=[cV])
                self.ps_read(ps, cV)
            rN.consumed(tc, cV)
        vv = (cV, cV.n)
        for c in range(C):
            P.do("vector",
                 lambda e, c=c: e.tensor_reduce(
                     PQ[:, col + c:col + c + 1],
                     parts[:, c * NT:(c + 1) * NT], AX.X, ALU.add),
                 waits=[vv] if c == 0 else [], incs=[cV])
        return (self.cV, self.cV.n)

    def finalize(self, dbg=()):
        P = self.P
        allw = [(c, c.n) for c in (self.cW, self.cL, self.cSP, self.cPE, self.cV,
                                   self.cS, self.cG, self.cXW) if c.n > 0]
        cD = P.ctr("cD")
        for name, src_ap, shape, dt in dbg:
            out = self.nc.dram_tensor("dbg_" + name, list(shape), dt,
                                      kind="ExternalOutput").ap()
            P.dma("sync", out, src_ap, waits=allw, ctr=cD)
        if cD.n:
            allw = allw + [(cD, cD.n)]
        P.do("sync", lambda e: e.nop(), waits=allw)
        P.emit()
        return self.nc


WSHAPES = {
    0: {"L0Wq": (D, D), "L0Wk": (D, D), "L0Wt": (D, D), "L0Wqa": (D, 16),
        "L0Wka": (D, 16), "L0WrT": (D, D), "L0WkT": (D, D), "L0WvT": (D, D)},
    1: {"L1WqT": (D, D), "L1WkT": (D, D), "L1WvT": (D, D), "L1WoT": (D, D),
        "L1U1": (D, 256), "L1V1": (256, 4096), "L1U2": (4096, 256),
        "L1V2": (256, D)},
    2: {"L2qkvT": (D, 3 * D), "L2projT": (D, D), "L2sgnT": (D, D)},
    3: {"L3waq": (D, D), "L3wak": (D, D), "L3wav": (D, D), "L3wao": (D, D),
        "L3sel": (D, 16), "L3keys": (16, D, 256), "L3vals": (16, 256, D)},
}


def emit_barrier(em):
    P = em.P
    allw = [(c, c.n) for c in (em.cW, em.cL, em.cSP, em.cPE, em.cV,
                               em.cS, em.cG, em.cXW) if c.n > 0]
    for eng in ("tensor", "vector", "scalar", "gpsimd", "sync"):
        P.do(eng, lambda e: e.nop(), waits=allw)


def build_program(layers, dbg=False):
    em = Emitter(layers)
    src = em.x_in
    for i, l in enumerate(layers):
        dst = em.x_out if i == len(layers) - 1 else em.xbuf[i % 2]
        if i > 0:
            emit_barrier(em)
        em.load_layer_consts(l)
        if l == 0:
            em.emit_l0(src, dst)
        elif l == 1:
            em.emit_l1(src, dst)
        elif l == 2:
            em.emit_l2(src, dst)
        elif l == 3:
            em.emit_l3(src, dst)
        src = dst
    taps = []
    if dbg and 0 in layers:
        taps = [("xn", em.xn_h, [NT, C, 128, W], BF16),
                ("sp1", em.sp1, [NT, C, 128, W], BF16),
                ("sp2", em.sp2, [NT, C, 128, W], BF16),
                ("sp3", em.sp3, [NT, C, 128, W], BF16),
                ("sp4", em.sp4, [NT, C, 128, W], BF16),
                ("pq", em.PQ[:], [128, 128], F32)]
    elif dbg:
        taps = [("xn", em.xn_h, [NT, C, 128, W], BF16),
                ("q", em.q_hb, [D, T], BF16),
                ("k", em.k_hb, [D, T], BF16),
                ("v", em.v_hb, [16, 128, 32, 64], BF16),
                ("o", em.o_hb, [D, T], BF16)]
    return em.finalize(taps)


def _np_fm(a2d):
    return np.ascontiguousarray(np.asarray(a2d, np.float32).T)


def _pack_vecs(g):
    v = np.zeros((128, 128), np.float32)

    def put(col, arr):
        a = np.asarray(arr, np.float32).reshape(-1, 128)
        v[:, col:col + a.shape[0]] = a.T

    put(0, g["L0_attn_bq"])
    put(8, g["L0_attn_bk"])
    put(16, g["L0_attn_bt"])
    put(24, g["L0_ffn_mu_r"])
    put(32, g["L0_ffn_mu_k"])
    v[0:16, 40] = np.asarray(g["L0_attn_bqa"], np.float32)
    v[0:16, 41] = np.asarray(g["L0_attn_bka"], np.float32)
    v[:, 60] = 0.125
    v[:, 61] = -0.5
    v[:, 62] = 1e-5
    v[:, 63] = 1.0 / 128.0
    put(64, g["L1_attn_bq"]); put(72, g["L1_attn_bk"])
    put(80, g["L1_attn_bv"]); put(88, g["L1_attn_bo"])
    put(96, g["L1_ffn_b1"]); put(42, g["L1_ffn_b2"])
    put(50, g["L2_attn_proj_b"])
    v[0, 58] = np.asarray(g["L3_ffn_route_scale"], np.float32).reshape(-1)[0]
    return v


def host_prep(inputs):
    g = {k: np.asarray(v) for k, v in inputs.items()}
    base = {"vecs": _pack_vecs(g)}
    t = lambda a: np.ascontiguousarray(np.asarray(a, np.float32))
    tt = lambda a: np.ascontiguousarray(np.asarray(a, np.float32).T)
    base.update({
        "L0Wq": t(g["L0_attn_Wq"]), "L0Wk": t(g["L0_attn_Wk"]),
        "L0Wt": t(g["L0_attn_Wt"]), "L0Wqa": t(g["L0_attn_Wqa"]),
        "L0Wka": t(g["L0_attn_Wka"]), "L0WrT": tt(g["L0_ffn_W_r"]),
        "L0WkT": tt(g["L0_ffn_W_k"]), "L0WvT": tt(g["L0_ffn_W_v"]),
        "L1WqT": tt(g["L1_attn_Wq"]), "L1WkT": tt(g["L1_attn_Wk"]),
        "L1WvT": tt(g["L1_attn_Wv"]), "L1WoT": tt(g["L1_attn_Wo"]),
        "L1U1": t(g["L1_ffn_U1"]), "L1V1": t(g["L1_ffn_V1"]),
        "L1U2": t(g["L1_ffn_U2"]), "L1V2": t(g["L1_ffn_V2"]),
        "L2qkvT": tt(g["L2_attn_qkv_w"]), "L2projT": tt(g["L2_attn_proj_w"]),
        "L2sgnT": tt(g["L2_ffn_weight"]),
        "L3waq": t(g["L3_attn_w_aq"].reshape(D, D)),
        "L3wak": t(g["L3_attn_w_ak"].reshape(D, D)),
        "L3wav": t(g["L3_attn_w_av"].reshape(D, D)),
        "L3wao": t(g["L3_attn_w_ao"].reshape(D, D)),
        "L3sel": tt(g["L3_ffn_expert_sel"]),
        "L3keys": t(g["L3_ffn_keys"]), "L3vals": t(g["L3_ffn_values"]),
    })
    for n, a in _CONSTS.items():
        base[n] = np.ascontiguousarray(a)
    x = np.asarray(g["L0_attn_hidden_states"], np.float32)
    return [dict(base, x_in=_np_fm(x[b])) for b in range(8)]


def _filter_inputs(ncp, m):
    names = set()
    for mls_name in list(m.keys()):
        names.add(mls_name)
    return m


def kernel(**inputs):
    in_maps = host_prep(inputs)
    x_cur = None
    for layers in GROUPS:
        ncp = build_program(layers)
        need = {"x_in", "vecs"} | set(WSHAPES_KEYS(layers)) | set(consts_for(layers))
        maps = []
        for b in range(8):
            m = {k: v for k, v in in_maps[b].items() if k in need}
            if x_cur is not None:
                m["x_in"] = x_cur[b]
            maps.append(m)
        res = run_bass_kernel_spmd(ncp, maps, core_ids=list(range(8)))
        x_cur = [np.ascontiguousarray(np.asarray(res.results[b]["x_out"],
                                                 np.float32)) for b in range(8)]
    out = np.stack([x.T for x in x_cur], axis=0)
    return np.ascontiguousarray(out, np.float32)


def WSHAPES_KEYS(layers):
    ks = []
    for l in layers:
        ks += list(WSHAPES[l].keys())
    return ks
